# version 19
# baseline (speedup 1.0000x reference)
import contextlib

import numpy as np
import ml_dtypes

import concourse.bass as bass
import concourse.mybir as mybir
from concourse.bass_utils import run_bass_kernel_spmd

F32 = mybir.dt.float32
BF16 = mybir.dt.bfloat16
I32 = mybir.dt.int32
AF = mybir.ActivationFunctionType
ALU = mybir.AluOpType
AX = mybir.AxisListType

DEBUG = False
TRACE_SITES = False

ENGINES = ("tensor", "vector", "scalar", "gpsimd", "sync")


class Buf:
    __slots__ = ("name", "writer", "readers")

    def __init__(self, name=""):
        self.name = name
        self.writer = None
        self.readers = []


class Fw:

    def __init__(self, nc):
        self.nc = nc
        self._stack = contextlib.ExitStack()
        self.sem = {}
        self.count = {}
        self.known = {e: {} for e in ENGINES}
        self.eng_sem = {}
        self.prog = {e: [] for e in ENGINES}
        self.sites = {}
        for e in ENGINES:
            sname = "S_" + e
            self.sem[sname] = self._stack.enter_context(nc.semaphore(sname))
            self.count[sname] = 0
            self.eng_sem[e] = sname

    def new_sem(self, name):
        if name not in self.sem:
            self.sem[name] = self._stack.enter_context(self.nc.semaphore(name))
            self.count[name] = 0
        return name

    def _sched_wait(self, eng, token):
        if token is None:
            return
        sname, val = token
        if self.known[eng].get(sname, 0) >= val:
            return
        if not sname.startswith("S_") and val != self.count[sname]:
            raise RuntimeError(
                f"wait on dma sem {sname} at {val} != cumulative "
                f"{self.count[sname]}; use a dedicated sem")
        self.known[eng][sname] = val
        sem = self.sem[sname]
        self.prog[eng].append(lambda h, s=sem, v=val: h.wait_ge(s, v))

    def _deps_for(self, ins, outs):
        deps = []
        for b in ins:
            if b.writer is not None:
                deps.append(b.writer)
        for b in outs:
            if b.writer is not None:
                deps.append(b.writer)
            deps.extend(b.readers)
        return deps

    def _finish(self, ins, outs, token):
        for b in ins:
            b.readers.append(token)
        for b in outs:
            b.writer = token
            b.readers = []

    def op(self, eng, fn, ins=(), outs=(), deps=()):
        for d in self._deps_for(ins, outs):
            self._sched_wait(eng, d)
        for d in deps:
            self._sched_wait(eng, d)
        sname = self.eng_sem[eng]
        self.count[sname] += 1
        sem = self.sem[sname]

        def run(h, fn=fn, sem=sem):
            fn(h).then_inc(sem, 1)

        self.prog[eng].append(run)
        if TRACE_SITES:
            import traceback
            self.sites.setdefault(eng, {})[len(self.prog[eng]) - 1] = \
                traceback.extract_stack()[-4:-2]
        token = (sname, self.count[sname])
        self._finish(ins, outs, token)
        return token

    def dma(self, eng, out_ap, in_ap, ins=(), outs=(), sem="dma", deps=(),
            **kw):
        self.new_sem(sem)
        for d in self._deps_for(ins, outs):
            self._sched_wait(eng, d)
        for d in deps:
            self._sched_wait(eng, d)
        self.count[sem] += 16
        semh = self.sem[sem]

        def run(h, out_ap=out_ap, in_ap=in_ap, semh=semh, kw=kw):
            h.dma_start(out=out_ap, in_=in_ap, **kw).then_inc(semh, 16)

        self.prog[eng].append(run)
        token = (sem, self.count[sem])
        self._finish(ins, outs, token)
        return token

    def wait_all(self, eng, tokens):
        best = {}
        for sname, val in tokens:
            best[sname] = max(best.get(sname, 0), val)
        for t in best.items():
            self._sched_wait(eng, t)

    def barrier_tokens(self):
        toks = [(self.eng_sem[e], self.count[self.eng_sem[e]])
                for e in ENGINES]
        toks += [(s, self.count[s]) for s in self.sem
                 if not s.startswith("S_")]
        return toks

    def emit(self):
        with self.nc.Block() as block:
            for e in ENGINES:
                if not self.prog[e]:
                    continue
                getattr(block, e)(lambda h, e=e: [f(h) for f in self.prog[e]])
        self._stack.close()


D = 1024
H = 4096
DC = 8
HC = 32
TT = 256
NT = 16
NTOK = NT * TT
RK = 64
DS = 16
EPS = 1e-5
INV_D = float(1.0 / 1024.0)
MAGIC = float(3 << 22)
N_Q1 = float(16 * 2048 * 1024)
N_Q2 = float(16 * 2048 * 4096)
RG = [[0, 1, 2, 3, 4, 5, 6, 7]]
NL = 16
NLE = 18
GEOLO = 0.04
LIN_W = 0.10
Q1_WIN = 0.06
Q2_WIN = 0.045
Q1_SUB_PER_TILE = 2 * 128 * 128
Q2_SUB_PER_TILE = 4 * 128 * 128
Q1_SLOTS = 4 * 8 * NT
Q2_SLOTS = 64 * 16
V_W0, V_W1, V_W2, V_CB, V_A, V_DSK, V_BDT, V_GAM, V_BET = range(9)
ABSM = 0x7FFFFFFF


class K:
    def __init__(self, nc, debug=False):
        self.nc = nc
        self.fw = Fw(nc)
        self.debug = debug
        self._bufs = {}
        self.dbg_idx = 0

    def buf(self, key):
        if key not in self._bufs:
            b = Buf(key)
            b.readers = list(getattr(self, "_barrier_toks", ()))
            self._bufs[key] = b
        return self._bufs[key]

    def carve(self, off_bytes, shape, dtype=F32):
        n = 1
        for s in shape[1:]:
            n *= s
        esz = 2 if dtype == BF16 else 4
        assert off_bytes % 4 == 0
        f0 = off_bytes // 4
        nf = (n * esz + 3) // 4
        ap = self.arena[:, f0:f0 + nf]
        if dtype == BF16:
            ap = ap.bitcast(BF16)[:, :n]
        if len(shape) == 3:
            ap = ap.rearrange("p (a b) -> p a b", a=shape[1])
        return ap

    def ts(self, eng, out, in_, s1, s2, op0, op1=None, ins=(), outs=(),
           deps=(), accum=None):
        if op1 is None and s2 is None and accum is None:
            return self.fw.op(
                eng, lambda h: h.tensor_scalar(out, in_, s1, None, op0),
                ins=ins, outs=outs, deps=deps)
        if op1 is None:
            op1 = ALU.add if accum is not None and s2 is None else op1
        assert op1 is not None, "need op1"
        return self.fw.op(
            eng, lambda h: h.tensor_scalar(out, in_, s1, s2, op0, op1,
                                           accum_out=accum),
            ins=ins, outs=outs, deps=deps)

    def tt(self, eng, out, a, b, op, ins=(), outs=(), deps=()):
        return self.fw.op(eng, lambda h: h.tensor_tensor(out, a, b, op),
                          ins=ins, outs=outs, deps=deps)

    def stt(self, eng, out, in0, scalar, in1, op0, op1, ins=(), outs=(),
            deps=()):
        return self.fw.op(
            eng, lambda h: h.scalar_tensor_tensor(out, in0, scalar, in1,
                                                  op0, op1),
            ins=ins, outs=outs, deps=deps)

    def cp(self, eng, out, in_, ins=(), outs=(), deps=()):
        return self.fw.op(eng, lambda h: h.tensor_copy(out, in_),
                          ins=ins, outs=outs, deps=deps)

    def act(self, out, in_, func, bias=0.0, scale=1.0, ins=(), outs=(),
            deps=(), accum=None):
        return self.fw.op(
            "scalar", lambda h: h.activation(out, in_, func, bias=bias,
                                             scale=scale, accum_out=accum),
            ins=ins, outs=outs, deps=deps)

    def mm(self, out, lhsT, rhs, start, stop, ins=(), outs=(), deps=()):
        return self.fw.op(
            "tensor", lambda h: h.matmul(out, lhsT, rhs, start=start,
                                         stop=stop),
            ins=ins, outs=outs, deps=deps)

    def tr(self, out, in_, ins=(), outs=(), deps=()):
        return self.fw.op(
            "tensor", lambda h: h.transpose(out, in_, self.identb[:]),
            ins=list(ins) + [self.buf("identb")], outs=outs, deps=deps)

    def red(self, eng, out, in_, op, absval=False, ins=(), outs=(), deps=()):
        return self.fw.op(
            eng, lambda h: h.tensor_reduce(out, in_, AX.X, op,
                                           apply_absolute_value=absval),
            ins=ins, outs=outs, deps=deps)

    def mset(self, eng, ap, val, outs=(), deps=()):
        return self.fw.op(eng, lambda h: h.memset(ap, val), outs=outs,
                          deps=deps)

    def rcp(self, out, in_, ins=(), outs=(), deps=()):
        return self.fw.op("vector", lambda h: h.reciprocal(out, in_),
                          ins=ins, outs=outs, deps=deps)

    def max8(self, out, in_, ins=(), outs=(), deps=()):
        return self.fw.op("vector", lambda h: h.max(out, in_),
                          ins=ins, outs=outs, deps=deps)

    def mrep(self, out, repl, in_, ins=(), outs=(), deps=()):
        return self.fw.op(
            "vector", lambda h: h.match_replace(out, repl, in_, 0.0),
            ins=ins, outs=outs, deps=deps)

    def coll(self, kind, op, src_ap, dst_ap, ins=(), outs=()):
        return self.fw.op(
            "gpsimd", lambda g: g.collective_compute(kind, op, RG,
                                                     [src_ap], [dst_ap]),
            ins=ins, outs=outs)

    def declare(self):
        nc = self.nc
        DR = nc.dram_tensor
        self.x_in = DR("x", [NTOK, D], F32, kind="ExternalInput")
        self.y_out = DR("y", [NTOK, D], F32, kind="ExternalOutput")
        specs = [("W_in", [D, 2 * D]), ("W_out", [D, D]), ("W_xp", [D, RK]),
                 ("W_dt", [RK, D]), ("W_Bg", [D, DS]), ("W_Cg", [D, DS]),
                 ("W_Bp", [DS, D]), ("W_Cp", [DS, D]), ("w1", [H, D]),
                 ("w2", [D, H])]
        (self.win_d, self.wout_d, self.wxp_d, self.wdt_d, self.wbg_d,
         self.wcg_d, self.wbp_d, self.wcp_d, self.w1_d, self.w2_d) = [
            DR(n, s, F32, kind="ExternalInput") for n, s in specs]
        self.vecs_d = DR("vecs", [128, DC * 12], F32, kind="ExternalInput")
        self.lngb_d = DR("lngb", [128, 2 * D], F32, kind="ExternalInput")
        self.ident_d = DR("ident", [128, 128], BF16, kind="ExternalInput")
        self.rows_d = DR("rows", [1, 64], F32, kind="ExternalInput")
        dk = "ExternalOutput" if self.debug else "Internal"
        self.x2_sp = DR("x2_sp", [NTOK, D], F32, kind=dk)
        self.hln_sp = DR("hln_sp", [NTOK, D], F32, kind=dk)
        self.g_sp = DR("g_sp", [HC, 128, NTOK], F32,
                       kind=("ExternalOutput" if self.debug else "Internal"))
        self.w1q_sp = DR("w1q_sp", [DC, 128, H], BF16, kind="Internal")
        self.w2q_sp = DR("w2q_sp", [HC, 128, D], BF16, kind="Internal")
        self.cc1_src = DR("cc1_src", [1, 64], F32, kind="Internal")
        self.cc1_dst = DR("cc1_dst", [8, 64], F32, kind="Internal")
        self.cc2_src = DR("cc2_src", [1, 32], F32, kind="Internal")
        self.cc2_dst = DR("cc2_dst", [8, 32], F32, kind="Internal")
        self.cc3_src = DR("cc3_src", [1, 2048], F32, kind="Internal")
        self.cc3_dst = DR("cc3_dst", [8, 2048], F32, kind="Internal")
        self.row_sp = DR("row_sp", [1, 1024], F32, kind="Internal")
        if self.debug:
            self.dbg_d = DR("dbg", [1, 320], F32, kind="ExternalOutput")

        sb = nc.alloc_sbuf_tensor
        self.identb = sb("identb", [128, 128], BF16)
        self.vec = sb("vec", [128, DC, 12], F32)
        self.lng_bc = sb("lng_bc", [128, D], F32)
        self.lnb_bc = sb("lnb_bc", [128, D], F32)
        self.onescol = sb("onescol", [128, 1], F32)
        self.zcol = sb("zcol", [128, 1], F32)
        self.onesrow = sb("onesrow", [1, 128], F32)
        self.eps_t = sb("eps_t", [128, 1], F32)
        self.one_t = sb("one_t", [128, 1], F32)
        self.rows = sb("rows_sb", [1, 64], F32)
        self.iota16 = sb("iota16", [1, 16], F32)
        self.stats = sb("stats", [128, 16], F32)
        self.statB = sb("statB", [128, 16], F32)
        self.statR = sb("statR", [128, 16], F32)
        self.lads = sb("lads", [1, 64], F32)
        self.lrow = sb("lrow", [1, 512], F32)
        self.thr = sb("thr", [128, 2 * NLE], F32)
        self.cnt = sb("cnt", [128, NL], F32)
        self.accs = sb("accs", [128, 32], F32)
        self.qwin = sb("qwin", [128, 4], F32)
        self.hstate = sb("hstate", [128, DC], F32)
        self.xblast = sb("xblast", [128, DC], F32)
        self.s_pp = sb("s_pp", [128, 4], F32)
        self.scp = sb("scp", [128, 4], F32)
        self.candx = sb("candx", [128, 16], F32)
        self.gath = sb("gath", [128, 128], F32)
        self.qsub = sb("qsub", [128, 512], F32)
        self.row1k = sb("row1k", [1, 1024], F32)
        self.i32s = sb("i32s", [1, 2], I32)
        self.dbgrow = sb("dbgrow", [1, 320], F32)
        self.srow = sb("srow", [1, 256], F32)
        self.cnthi = sb("cnthi", [128, NT * 8], F32)

        AW = 45056
        self.arena = sb("arena", [128, AW], F32)
        C = self.carve
        self.winb = C(0, [128, DC, 2 * D], BF16)
        self.woutb = C(32768, [128, DC, D], BF16)
        self.wgateb = C(49152, [128, DC, 128], BF16)
        self.wdtb = C(51200, [128, D], BF16)
        self.wbpb = self.wdtb
        self.wcpb = self.wdtb
        self.xt = [C(56832, [128, 2, D]), C(65024, [128, 2, D])]
        self.wstage = [C(56832, [128, 2 * D]), C(65024, [128, 2 * D])]
        self.xres = C(73216, [128, 2, D])
        self.gsta = C(81408, [128, 2, D])
        self.xn2b = C(89600, [128, 2, D], BF16)
        self.xn2T = C(93696, [128, DC, TT], BF16)
        self.sq_a = C(93696, [128, D])
        self.xb = [C(97792, [128, DC, TT]), C(105984, [128, DC, TT])]
        self.zt = [C(114176, [128, DC, TT]), C(122368, [128, DC, TT])]
        self.xc = C(130560, [128, DC, TT])
        self.xcf = C(130560, [128, DC * TT])
        self.xcr = C(130560, [128, 2, D])
        self.dAf = C(151040, [128, DC * TT])
        self.dBxf = C(159232, [128, DC * TT])
        self.xactb = C(138752, [128, DC, TT], BF16)
        self.espt = C(142848, [128, DC, TT])
        self.espf = C(142848, [128, 2048])
        self.esp2 = C(142848, [128, 2, D])
        self.sq_a3 = C(89600, [128, 2, D])
        self.dA = C(151040, [128, DC, TT])
        self.dBx = C(159232, [128, DC, TT])
        self.gated = C(167424, [128, DC, TT], BF16)
        self.mixsb = C(171520, [128, 2, D])
        self.tsmall = C(171520, [128, 1024], BF16)
        self.w1qT = C(0, [128, DC, H], BF16)
        self.w2qT = C(0, [128, HC, D], BF16)
        self.hlnF = [C(65536, [128, 2, D]), C(73728, [128, 2, D])]
        self.k1tok = [C(81920, [128, 2, D], BF16),
                      C(86016, [128, 2, D], BF16)]
        self.k1Tw = [C(90112, [128, DC, 2 * TT], BF16),
                     C(98304, [128, DC, 2 * TT], BF16)]
        self.gsb = [C(106496, [128, 4, 2 * TT]),
                    C(114688, [128, 4, 2 * TT])]
        self.w2s = [C(122880, [128, D]), C(126976, [128, D])]
        self.w2tb = [C(131072, [128, D], BF16), C(133120, [128, D], BF16)]
        self.w2x = [C(135168, [128, DC, 128], BF16),
                    C(137216, [128, DC, 128], BF16)]
        self.w1s = [C(139264, [128, D]), C(143360, [128, D])]
        self.q2rd = [C(139264, [128, 4, 2 * TT]),
                     C(147456, [128, 4, 2 * TT])]
        self.w1tb = [C(147456, [128, D], BF16), C(149504, [128, D], BF16)]
        self.w1x = [C(151552, [128, DC, 128], BF16),
                    C(153600, [128, DC, 128], BF16)]
        self.selscr = C(155648, [128, 2048])
        self.q1scr = C(106496, [128, 2048])
        self.slots1 = C(53248, [128, Q1_SLOTS])
        self.slots2 = C(167936, [128, Q2_SLOTS])
        self.x2F = [C(65536, [128, 2, D]), C(73728, [128, 2, D])]
        self.outsb = [C(81920, [128, 2, D]), C(90112, [128, 2, D])]
        self.gF = [C(98304, [128, 4, TT]), C(102400, [128, 4, TT])]
        self.k2 = [C(106496, [128, 4, TT], BF16),
                   C(108544, [128, 4, TT], BF16)]
        assert AW * 4 >= 180224, AW * 4
        self.pb = [nc.alloc_psum_tensor(f"pb{i}", [128, 512], F32)
                   for i in range(6)]
        self.ptr = [nc.alloc_psum_tensor(f"ptr{i}", [128, 1024], BF16)
                    for i in range(2)]
        self.sbuf_left = nc.sbuf_bytes_remaining

    def lad(self, j):
        return self.lads[0:1, j:j + 1]

    def dbg_mark(self, k):
        if not self.debug or self.dbg_idx >= 319:
            return
        i = self.dbg_idx
        self.dbg_idx += 1
        self.mset("vector", self.dbgrow[0:1, i:i + 1], float(77000 + k),
                  outs=[self.buf("dbgrow")])

    def dbg(self, src11, n=1):
        if not self.debug:
            return
        i = self.dbg_idx
        if i + n > 320:
            return
        self.dbg_idx += n
        self.cp("vector", self.dbgrow[0:1, i:i + n], src11,
                ins=[self.buf("lads"), self.buf("srow"), self.buf("cnt")],
                outs=[self.buf("dbgrow")])

    def phase_barrier(self, arena_keys):
        toks = self.fw.barrier_tokens()
        for k in arena_keys:
            self.buf(k).readers = list(self.buf(k).readers) + toks

    def bc_pp(self, src11, dst_ap, dst_key):
        p = self.pb[5]
        self.mm(p[:, 0:1], self.onesrow[:], src11, True, True,
                ins=[self.buf("onesrow"), self.buf("lads")],
                outs=[self.buf("pb5")])
        self.cp("vector", dst_ap, p[:, 0:1], ins=[self.buf("pb5")],
                outs=[self.buf(dst_key)])

    def bc_row(self, src_row, k, dst_ap, dst_key, src_key):
        p = self.pb[5]
        self.mm(p[:, 0:k], self.onesrow[:], src_row, True, True,
                ins=[self.buf("onesrow"), self.buf(src_key)],
                outs=[self.buf("pb5")])
        self.cp("vector", dst_ap, p[:, 0:k], ins=[self.buf("pb5")],
                outs=[self.buf(dst_key)])

    def col_total(self, cols_ap, k, out_row, src_key, out_key="lads"):
        p = self.pb[5]
        self.mm(p[0:1, 0:k], self.onescol[:], cols_ap, True, True,
                ins=[self.buf("onescol"), self.buf(src_key)],
                outs=[self.buf("pb5")])
        self.cp("vector", out_row, p[0:1, 0:k], ins=[self.buf("pb5")],
                outs=[self.buf(out_key)])

    def part_max_to_scalar(self, col_ap, out11, src_key):
        fw, buf = self.fw, self.buf
        self._bounce_n = getattr(self, "_bounce_n", 0) + 1
        fw.dma("gpsimd",
               self.row_sp.ap()[0:1, 0:128].rearrange("a (p s) -> (a p) s",
                                                      p=128),
               col_ap, ins=[buf(src_key), buf("row_sp")],
               outs=[buf("row_sp")], sem=f"bounce{self._bounce_n}")
        fw.dma("sync", self.srow[0:1, 0:128], self.row_sp.ap()[0:1, 0:128],
               ins=[buf("row_sp")], outs=[buf("srow")],
               sem=f"bounce{self._bounce_n}b")
        self.red("vector", out11, self.srow[0:1, 0:128], ALU.max,
                 ins=[buf("srow")], outs=[buf("lads")])

    def init_phase(self):
        fw, buf = self.fw, self.buf
        dma = fw.dma
        dma("sync", self.identb[:], self.ident_d.ap(), outs=[buf("identb")],
            sem="l_i")
        dma("sync", self.vec[:],
            self.vecs_d.ap().rearrange("p (c n) -> p c n", c=DC),
            outs=[buf("vec")], sem="l_v")
        dma("sync", self.lng_bc[:], self.lngb_d.ap()[:, 0:D],
            outs=[buf("lng_bc")], sem="l_g")
        dma("sync", self.lnb_bc[:], self.lngb_d.ap()[:, D:2 * D],
            outs=[buf("lnb_bc")], sem="l_b")
        dma("sync", self.rows[:], self.rows_d.ap(), outs=[buf("rows_sb")],
            sem="l_r")
        for t, v in [("onescol", 1.0), ("onesrow", 1.0), ("eps_t", EPS),
                     ("one_t", 1.0)]:
            self.mset("vector", getattr(self, t)[:], v, outs=[buf(t)])
        for t in ("cnt", "cnthi", "stats", "lads", "accs", "hstate", "zcol",
                  "xblast", "qwin", "s_pp", "scp", "candx", "gath", "qsub",
                  "srow", "lrow", "thr", "dbgrow"):
            self.mset("vector", getattr(self, t)[:], 0.0, outs=[buf(t)])
        self.fw.op("gpsimd", lambda g: g.iota(
            self.iota16[:], [[1, 16]], base=0, channel_multiplier=0,
            allow_small_or_imprecise_dtypes=True), outs=[buf("iota16")])

        self._ldn = 0

        def ldcast(dst_ap, dst_key, src_ap, rows, cols, p0=0):
            w = self._ldn % 2
            self._ldn += 1
            st = self.wstage[w]
            dma("sync", st[p0:p0 + rows, 0:cols], src_ap,
                ins=[buf(f"wstage{w}")], outs=[buf(f"wstage{w}")],
                sem=f"l_w{w}")
            self.cp("vector", dst_ap, st[p0:p0 + rows, 0:cols],
                    ins=[buf(f"wstage{w}")], outs=[buf(dst_key)])

        self.mset("vector", self.wgateb[:, :, :], 0.0,
                  outs=[buf("wgateb")])
        for c in range(DC):
            r = slice(c * 128, c * 128 + 128)
            ldcast(self.winb[:, c, :], "winb", self.win_d.ap()[r, :], 128,
                   2 * D)
            ldcast(self.woutb[:, c, :], "woutb", self.wout_d.ap()[r, :],
                   128, D)
            ldcast(self.wgateb[:, c, 64:128], "wgateb",
                   self.wxp_d.ap()[r, :], 128, RK)
            ldcast(self.wgateb[:, c, 0:DS], "wgateb",
                   self.wbg_d.ap()[r, :], 128, DS)
            ldcast(self.wgateb[:, c, 32:32 + DS], "wgateb",
                   self.wcg_d.ap()[r, :], 128, DS)
        ldcast(self.wdtb[64:128, :], "wdtb", self.wdt_d.ap(), 64, D, p0=64)
        ldcast(self.wbpb[0:16, :], "wbpb", self.wbp_d.ap(), 16, D)
        ldcast(self.wcpb[32:48, :], "wcpb", self.wcp_d.ap(), 16, D, p0=32)

        self.mset("vector", self.accs[:, 8:10], 0.0, outs=[buf("accs")])
        for c in range(HC):
            w = c % 2
            st = self.wstage[w]
            dma("sync", st[:, 0:D], self.w1_d.ap()[c * 128:(c + 1) * 128, :],
                ins=[buf(f"wstage{w}")], outs=[buf(f"wstage{w}")],
                sem=f"l_w1{w}")
            self.red("vector", self.accs[:, w:w + 1], st[:, 0:D], ALU.add,
                     absval=True, ins=[buf(f"wstage{w}")],
                     outs=[buf("accs")])
            self.tt("vector", self.accs[:, 8:9], self.accs[:, 8:9],
                    self.accs[:, w:w + 1], ALU.add, ins=[buf("accs")],
                    outs=[buf("accs")])
        for c in range(DC):
            for q in range(2):
                w = q % 2
                st = self.wstage[w]
                dma("sync", st[:, 0:2 * D],
                    self.w2_d.ap()[c * 128:(c + 1) * 128,
                                   q * 2 * D:(q + 1) * 2 * D],
                    ins=[buf(f"wstage{w}")], outs=[buf(f"wstage{w}")],
                    sem=f"l_w2{w}")
                self.red("vector", self.accs[:, w:w + 1], st[:, 0:2 * D],
                         ALU.add, absval=True, ins=[buf(f"wstage{w}")],
                         outs=[buf("accs")])
                self.tt("vector", self.accs[:, 9:10], self.accs[:, 9:10],
                        self.accs[:, w:w + 1], ALU.add, ins=[buf("accs")],
                        outs=[buf("accs")])
        self.col_total(self.accs[:, 8:10], 2, self.lads[0:1, 4:6], "accs")
        self.ts("vector", self.lad(0), self.lads[0:1, 4:5],
                float(1.0 / (H * D)), EPS, ALU.mult, ALU.add,
                ins=[buf("lads")], outs=[buf("lads")])
        self.ts("vector", self.lad(2), self.lads[0:1, 5:6],
                float(1.0 / (H * D)), EPS, ALU.mult, ALU.add,
                ins=[buf("lads")], outs=[buf("lads")])
        self.rcp(self.lad(1), self.lad(0), ins=[buf("lads")],
                 outs=[buf("lads")])
        self.rcp(self.lad(3), self.lad(2), ins=[buf("lads")],
                 outs=[buf("lads")])
        self.bc_pp(self.lad(1), self.scp[:, 0:1], "scp")
        self.bc_pp(self.lad(3), self.scp[:, 1:2], "scp")
        self.phase_barrier(["xt0", "xt1"])

    def ln_stats2(self, v3, bv, scr, bscrs, st, bst):
        for tc in range(2):
            self.red("vector", st[:, tc:tc + 1], v3[:, tc, :], ALU.add,
                     ins=[bv], outs=[bst])
            self.act(scr[:, tc, :], v3[:, tc, :], AF.Square, ins=[bv],
                     outs=list(bscrs) + [bst], accum=st[:, 2 + tc:3 + tc])
        self.ts("vector", st[:, 4:6], st[:, 0:2], INV_D, None, ALU.mult,
                ins=[bst], outs=[bst])
        self.ts("vector", st[:, 6:8], st[:, 2:4], INV_D, None, ALU.mult,
                ins=[bst], outs=[bst])
        self.tt("vector", st[:, 8:10], st[:, 4:6], st[:, 4:6], ALU.mult,
                ins=[bst], outs=[bst])
        self.tt("vector", st[:, 6:8], st[:, 6:8], st[:, 8:10], ALU.subtract,
                ins=[bst], outs=[bst])
        self.act(st[:, 8:10], st[:, 6:8], AF.Ln, bias=self.eps_t[:, 0:1],
                 ins=[bst, self.buf("eps_t")], outs=[bst])
        self.act(st[:, 10:12], st[:, 8:10], AF.Exp, scale=-0.5, ins=[bst],
                 outs=[bst])

    def ln_stats(self, v, bv, tc, scr, bscrs, st=None, bst=None):
        if st is None:
            st, bst = self.stats, self.buf("stats")
        self.red("vector", st[:, tc:tc + 1], v, ALU.add, ins=[bv],
                 outs=[bst])
        self.act(scr, v, AF.Square, ins=[bv], outs=list(bscrs) + [bst],
                 accum=st[:, 2 + tc:3 + tc])
        self.ts("vector", st[:, 4 + tc:5 + tc], st[:, tc:tc + 1], INV_D,
                None, ALU.mult, ins=[bst], outs=[bst])
        self.ts("vector", st[:, 6 + tc:7 + tc], st[:, 2 + tc:3 + tc], INV_D,
                None, ALU.mult, ins=[bst], outs=[bst])
        self.tt("vector", st[:, 8 + tc:9 + tc], st[:, 4 + tc:5 + tc],
                st[:, 4 + tc:5 + tc], ALU.mult, ins=[bst], outs=[bst])
        self.tt("vector", st[:, 6 + tc:7 + tc], st[:, 6 + tc:7 + tc],
                st[:, 8 + tc:9 + tc], ALU.subtract, ins=[bst], outs=[bst])
        self.act(st[:, 8 + tc:9 + tc], st[:, 6 + tc:7 + tc], AF.Ln,
                 bias=self.eps_t[:, 0:1], ins=[bst, self.buf("eps_t")],
                 outs=[bst])
        self.act(st[:, 10 + tc:11 + tc], st[:, 8 + tc:9 + tc], AF.Exp,
                 scale=-0.5, ins=[bst], outs=[bst])

    def stage_a(self, i):
        fw, buf = self.fw, self.buf
        ii = i % 2
        xt, bxt = self.xt[ii], buf(f"xt{ii}")
        r0 = i * TT
        fw.dma("sync", xt[:],
               self.x_in.ap()[r0:r0 + TT, :].rearrange(
                   "(tc p) d -> p tc d", p=128),
               ins=[bxt], outs=[bxt], sem=f"l_x{ii}")
        st, bst = self.stats, buf("stats")
        self.ln_stats2(xt, bxt, self.sq_a3, [buf("xn2b"), buf("xn2a")],
                       st, bst)
        r4 = (i % 4) * 4
        self.cp("vector", self.statR[:, r4:r4 + 2], st[:, 4:6],
                ins=[bst], outs=[buf("statR")])
        self.cp("vector", self.statR[:, r4 + 2:r4 + 4], st[:, 10:12],
                ins=[bst], outs=[buf("statR")])
        for tc in range(2):
            self.ts("vector", self.xn2b[:, tc, :], xt[:, tc, :],
                    st[:, 4 + tc:5 + tc],
                    st[:, 10 + tc:11 + tc], ALU.subtract, ALU.mult,
                    ins=[bxt, bst], outs=[buf("xn2b")])
        for tc in range(2):
            for dc in range(DC):
                pt = self.ptr[dc % 2]
                bpt = buf(f"ptr{dc % 2}")
                self.tr(pt[:, 0:128],
                        self.xn2b[:, tc, dc * 128:(dc + 1) * 128],
                        ins=[buf("xn2b")], outs=[bpt])
                self.ts("vector", self.xn2T[:, dc, tc * 128:(tc + 1) * 128],
                        pt[:, 0:128], self.vec[:, dc, V_GAM:V_GAM + 1],
                        self.vec[:, dc, V_BET:V_BET + 1], ALU.mult, ALU.add,
                        ins=[bpt, buf("vec")], outs=[buf("xn2a")])
    def stage_a_mm(self, i):
        buf = self.buf
        ii = i % 2
        xb, z = self.xb[ii], self.zt[ii]
        bxb, bz = buf(f"xb{ii}"), buf(f"zt{ii}")
        for q in range(16):
            pbk, bp = self.pb[q % 2], buf(f"pb{q % 2}")
            for k in range(DC):
                self.mm(pbk[:, 0:TT], self.winb[:, k, q * 128:(q + 1) * 128],
                        self.xn2T[:, k, :], k == 0, k == DC - 1,
                        ins=[buf("winb"), buf("xn2a")], outs=[bp])
            dst = xb[:, q, :] if q < 8 else z[:, q - 8, :]
            dk = bxb if q < 8 else bz
            self.act(dst, pbk[:, 0:TT], AF.Copy, ins=[bp], outs=[dk])

    def stage_b_front_a(self, i):
        fw, buf = self.fw, self.buf
        ii = i % 2
        seqpos = i % 8
        xb, z = self.xb[ii], self.zt[ii]
        bxb, bz = buf(f"xb{ii}"), buf(f"zt{ii}")
        vec, bvec = self.vec, buf("vec")
        xc, bxc = self.xc, buf("xc")
        xbf = xb.rearrange("p a b -> p (a b)")
        NW = DC * TT

        def bc(col):
            return vec[:, :, col:col + 1].broadcast_to([128, DC, TT])

        bdA, bdBx = buf("dA"), buf("dBx")
        self.tt("vector", xc[:, :, :], xb[:, :, :], bc(V_W1), ALU.mult,
                ins=[bxb, bvec], outs=[bxc])
        self.tt("vector", xc[:, :, :], xc[:, :, :], bc(V_CB), ALU.add,
                ins=[bvec, bxc], outs=[bxc])
        self.tt("gpsimd", self.dA[:, :, :], xb[:, :, :], bc(V_W0), ALU.mult,
                ins=[bxb, bvec], outs=[bdA])
        self.tt("vector", self.xcf[:, 1:NW], self.xcf[:, 1:NW],
                self.dAf[:, 0:NW - 1], ALU.add, ins=[bdA, bxc], outs=[bxc])
        self.tt("gpsimd", self.dBx[:, :, :], xb[:, :, :], bc(V_W2), ALU.mult,
                ins=[bxb, bvec], outs=[bdBx])
        self.tt("vector", self.xcf[:, 0:NW - 1], self.xcf[:, 0:NW - 1],
                self.dBxf[:, 1:NW], ALU.add, ins=[bdBx, bxc], outs=[bxc])
        bac = buf("accs")
        self.tt("vector", self.accs[:, 16:23],
                xb[:, 0:7, TT - 1:TT].rearrange("p c o -> p (c o)"),
                vec[:, 1:8, V_W0], ALU.mult, ins=[bxb, bvec], outs=[bac])
        self.tt("vector", xc[:, 1:8, 0:1], xc[:, 1:8, 0:1],
                self.accs[:, 16:23].rearrange("p (c o) -> p c o", o=1),
                ALU.subtract, ins=[bac, bxc], outs=[bxc])
        self.tt("vector", self.accs[:, 16:23],
                xb[:, 1:8, 0:1].rearrange("p c o -> p (c o)"),
                vec[:, 0:7, V_W2], ALU.mult, ins=[bxb, bvec], outs=[bac])
        self.tt("vector", xc[:, 0:7, TT - 1:TT], xc[:, 0:7, TT - 1:TT],
                self.accs[:, 16:23].rearrange("p (c o) -> p c o", o=1),
                ALU.subtract, ins=[bac, bxc], outs=[bxc])
        if seqpos > 0:
            self.tt("vector", self.accs[:, 16:24], self.xblast[:],
                    vec[:, :, V_W0], ALU.mult, ins=[buf("xblast"), bvec],
                    outs=[bac])
            self.tt("vector", xc[:, :, 0:1], xc[:, :, 0:1],
                    self.accs[:, 16:24].rearrange("p (c o) -> p c o", o=1),
                    ALU.add, ins=[bac, bxc], outs=[bxc])
        self.cp("vector", self.xblast[:],
                xb[:, :, TT - 1:TT].rearrange("p c o -> p (c o)"),
                ins=[bxb], outs=[buf("xblast")])

    def stage_b_front_a2(self, i):
        fw, buf = self.fw, self.buf
        ii = i % 2
        seqpos = i % 8
        xb, z = self.xb[ii], self.zt[ii]
        bxb, bz = buf(f"xb{ii}"), buf(f"zt{ii}")
        vec, bvec = self.vec, buf("vec")
        xc, bxc = self.xc, buf("xc")
        bac = buf("accs")
        if seqpos < 7:
            nxt = self.xb[(i + 1) % 2]
            bnxt = buf(f"xb{(i + 1) % 2}")
            self.tt("vector", self.accs[:, 16:24],
                    nxt[:, :, 0:1].rearrange("p c o -> p (c o)"),
                    vec[:, :, V_W2], ALU.mult, ins=[bnxt, bvec], outs=[bac])
            self.tt("vector", xc[:, :, TT - 1:TT], xc[:, :, TT - 1:TT],
                    self.accs[:, 16:24].rearrange("p (c o) -> p c o", o=1),
                    ALU.add, ins=[bac, bxc], outs=[bxc])
        self.act(xc[:, :, :], xc[:, :, :], AF.Silu, ins=[bxc], outs=[bxc])
        self.act(z[:, :, :], z[:, :, :], AF.Silu, ins=[bz], outs=[bz])
        self.cp("vector", self.xactb[:, :, :], xc[:, :, :], ins=[bxc],
                outs=[buf("xactb")])

    def stage_b_front_b(self, i):
        fw, buf = self.fw, self.buf
        ii = i % 2
        seqpos = i % 8
        xb, z = self.xb[ii], self.zt[ii]
        bxb, bz = buf(f"xb{ii}"), buf(f"zt{ii}")
        vec, bvec = self.vec, buf("vec")
        xc, bxc = self.xc, buf("xc")
        bxab = buf("xactb")
        p2, bp2 = self.pb[2], buf("pb2a")
        btm = buf("mixsb")
        t1b = self.tsmall[64:128, 0:TT]
        tBb = self.tsmall[0:16, 0:TT]
        tCb = self.tsmall[32:48, 0:TT]
        for k in range(DC):
            self.mm(p2[0:128, 0:TT], self.wgateb[:, k, :],
                    self.xactb[:, k, :], k == 0, k == DC - 1,
                    ins=[buf("wgateb"), bxab], outs=[bp2])
        self.cp("vector", self.tsmall[0:128, 0:TT], p2[0:128, 0:TT],
                ins=[bp2], outs=[btm])
        esp, besp = self.espt, buf("espt")
        for dc in range(DC):
            p3, bp3 = self.pb[3], buf("pb3a")
            self.mm(p3[:, 0:TT], self.wdtb[64:128, dc * 128:(dc + 1) * 128],
                    t1b, True, True, ins=[buf("wdtb"), btm], outs=[bp3])
            self.act(esp[:, dc, :], p3[:, 0:TT], AF.Exp,
                     bias=vec[:, dc, V_BDT:V_BDT + 1], ins=[bp3, bvec],
                     outs=[besp])
        self.act(esp[:, :, :], esp[:, :, :], AF.Ln,
                 bias=self.one_t[:, 0:1], ins=[besp, buf("one_t")],
                 outs=[besp])
        dA, bdA = self.dA, buf("dA")
        self.tt("vector", dA[:, :, :], esp[:, :, :],
                vec[:, :, V_A:V_A + 1].broadcast_to([128, DC, TT]),
                ALU.mult, ins=[besp, bvec], outs=[bdA])
        self.act(dA[:, :, :], dA[:, :, :], AF.Exp, ins=[bdA], outs=[bdA])
        dBx, bdBx = self.dBx, buf("dBx")
        if seqpos == 0:
            self.mset("vector", self.hstate[:], 0.0, outs=[buf("hstate")])
        for dc in range(DC):
            p4, bp4 = self.pb[3], buf("pb3b")
            self.mm(p4[:, TT:2 * TT],
                    self.wbpb[0:16, dc * 128:(dc + 1) * 128],
                    tBb, True, True, ins=[buf("wbpb"), btm], outs=[bp4])
            self.tt("vector", dBx[:, dc, :], p4[:, TT:2 * TT],
                    esp[:, dc, :], ALU.mult, ins=[bp4, besp], outs=[bdBx])
        self.tt("gpsimd", dBx[:, :, :], dBx[:, :, :], xc[:, :, :],
                ALU.mult, ins=[bdBx, bxc], outs=[bdBx])
        bac = buf("accs")
        dA0 = dA[:, :, 0:1].rearrange("p c o -> p (c o)")
        dBx0 = dBx[:, :, 0:1].rearrange("p c o -> p (c o)")
        self.tt("vector", self.accs[:, 24:32], dA0, self.hstate[:],
                ALU.mult, ins=[bdA, buf("hstate")], outs=[bac])
        self.tt("vector", dBx0, dBx0, self.accs[:, 24:32], ALU.add,
                ins=[bac, bdBx], outs=[bdBx])
        self.ts("vector", dA0, dA0, 0.0, None, ALU.mult, ins=[bdA],
                outs=[bdA])
        self.fw.op("vector", lambda v: v.tensor_tensor_scan(
            self.dBxf[:, :], self.dAf[:, :], self.dBxf[:, :],
            self.zcol[:, 0:1], ALU.mult, ALU.add),
            ins=[bdA, bdBx, buf("zcol")], outs=[bdBx])
        self.cp("vector", self.hstate[:],
                dBx[:, :, TT - 1:TT].rearrange("p c o -> p (c o)"),
                ins=[bdBx], outs=[buf("hstate")])
        for dc in range(DC):
            p5, bp5 = self.pb[2], buf("pb2b")
            self.mm(p5[:, TT:2 * TT],
                    self.wcpb[32:48, dc * 128:(dc + 1) * 128],
                    tCb, True, True, ins=[buf("wcpb"), btm], outs=[bp5])
            self.tt("vector", dA[:, dc, :], dBx[:, dc, :], p5[:, TT:2 * TT],
                    ALU.mult, ins=[bdBx, bp5], outs=[bdA])
        self.tt("gpsimd", esp[:, :, :], xc[:, :, :],
                vec[:, :, V_DSK:V_DSK + 1].broadcast_to([128, DC, TT]),
                ALU.mult, ins=[bxc, bvec], outs=[besp])
        self.tt("vector", dA[:, :, :], dA[:, :, :], esp[:, :, :], ALU.add,
                ins=[besp, bdA], outs=[bdA])
        self.tt("vector", self.gated[:, :, :], dA[:, :, :], z[:, :, :],
                ALU.mult, ins=[bdA, bz], outs=[buf("gated")])

    def stage_b_tail(self, i):
        fw, buf = self.fw, self.buf
        ii = i % 2
        xt, bxt = self.xres, buf("xres")
        fw.dma("sync", xt[:],
               self.x_in.ap()[i * TT:i * TT + TT, :].rearrange(
                   "(tc p) d -> p tc d", p=128),
               ins=[bxt], outs=[bxt], sem=f"l_xr{i % 2}")
        mix, bmix = self.mixsb, buf("mixsb")
        r4 = (i % 4) * 4
        sR, bsR = self.statR, buf("statR")
        for tc in range(2):
            self.ts("vector", mix[:, tc, :], xt[:, tc, :],
                    sR[:, r4 + tc:r4 + tc + 1],
                    sR[:, r4 + 2 + tc:r4 + 3 + tc],
                    ALU.subtract, ALU.mult, ins=[bxt, bsR], outs=[bmix])
        for tc in range(2):
            for nq in range(2):
                pbk, bp = self.pb[4 + nq], buf(f"pb{4 + nq}")
                for k in range(DC):
                    self.mm(pbk[:, 0:512],
                            self.gated[:, k, tc * 128:(tc + 1) * 128],
                            self.woutb[:, k, nq * 512:(nq + 1) * 512],
                            k == 0, k == DC - 1,
                            ins=[buf("gated"), buf("woutb")], outs=[bp])
                self.tt("vector", mix[:, tc, nq * 512:(nq + 1) * 512],
                        mix[:, tc, nq * 512:(nq + 1) * 512],
                        pbk[:, 0:512], ALU.add, ins=[bp], outs=[bmix])
        st, bst = self.statB, buf("statB")
        hl = self.esp2
        bhl = buf("espt")
        self.ln_stats2(mix, bmix, self.esp2, [buf("espt")], st, bst)
        for tc in range(2):
            self.ts("vector", mix[:, tc, :], mix[:, tc, :],
                    st[:, 4 + tc:5 + tc], st[:, 10 + tc:11 + tc],
                    ALU.subtract, ALU.mult, ins=[bmix, bst], outs=[bmix])
            self.tt("vector", mix[:, tc, :], mix[:, tc, :], xt[:, tc, :],
                    ALU.add, ins=[bmix, bxt], outs=[bmix])
        r0 = i * TT
        fw.dma("gpsimd",
               self.x2_sp.ap()[r0:r0 + TT, :].rearrange(
                   "(tc p) d -> p tc d", p=128),
               mix[:, :, :], ins=[bmix, buf("x2_sp")], outs=[buf("x2_sp")],
               sem=f"s_x2{i % 2}")
        self.ln_stats2(mix, bmix, self.esp2, [buf("espt")], st, bst)
        for tc in range(2):
            self.ts("vector", hl[:, tc, :], mix[:, tc, :],
                    st[:, 4 + tc:5 + tc], st[:, 10 + tc:11 + tc],
                    ALU.subtract, ALU.mult, ins=[bmix, bst], outs=[bhl])
            self.tt("vector", hl[:, tc, :], hl[:, tc, :], self.lng_bc[:],
                    ALU.mult, ins=[bhl, buf("lng_bc")], outs=[bhl])
            self.tt("vector", hl[:, tc, :], hl[:, tc, :], self.lnb_bc[:],
                    ALU.add, ins=[bhl, buf("lnb_bc")], outs=[bhl])
        fw.dma("gpsimd",
               self.hln_sp.ap()[r0:r0 + TT, :].rearrange(
                   "(tc p) d -> p tc d", p=128),
               hl[:, :, :], ins=[bhl, buf("hln_sp")], outs=[buf("hln_sp")],
               sem=f"s_hln{i % 2}")
        if i <= 7:
            self.ladder_tile(hl[:, :, :].rearrange("p a b -> p (a b)"),
                             bhl, i, need_abs=True, subs=32, nq=1,
                             shift=(i % 8))
        else:
            self.capture_q1_sb(
                hl[:, :, :].rearrange("p a b -> p (a b)"), bhl, i)
            rb, rk = self.xres, buf("xres")
            fw.dma("sync", rb[:],
                   self.hln_sp.ap()[(i - 8) * TT:(i - 7) * TT, :].rearrange(
                       "(tc p) d -> p tc d", p=128),
                   ins=[buf("hln_sp"), rk], outs=[rk], sem=f"l_hr{i % 2}")
            self.capture_q1_sb(
                rb.rearrange("p a b -> p (a b)"), rk, i - 8)

    def capture_q1_sb(self, flat, bflat, idx):
        buf = self.buf
        width = 2048
        bdA, bdBx = buf("dA"), buf("dBx")
        self.ts("vector", self.dBxf.bitcast(I32)[:, 0:width],
                flat.bitcast(I32), ABSM, None, ALU.bitwise_and,
                ins=[bflat], outs=[bdBx])
        data = self.dBxf[:, 0:width]
        self.ts("vector", self.dAf[:, 0:width], data, self.qwin[:, 1:2],
                None, ALU.is_gt, ins=[bdBx, buf("qwin")],
                outs=[bdA, buf("accs")], accum=self.accs[:, 30:31])
        self.tt("vector", self.cnthi[:, idx:idx + 1],
                self.cnthi[:, idx:idx + 1], self.accs[:, 30:31], ALU.add,
                ins=[buf("accs")], outs=[buf("cnthi")])
        self.ts("vector", self.dAf[:, 0:width], data, self.qwin[:, 1:2],
                None, ALU.is_le, ins=[bdBx, buf("qwin")], outs=[bdA])
        self.tt("gpsimd", self.dAf[:, 0:width], self.dAf[:, 0:width], data,
                ALU.mult, ins=[bdBx, bdA], outs=[bdA])
        for s in range(4):
            self.max8(
                self.slots1[:, idx * 32 + s * 8:idx * 32 + (s + 1) * 8],
                self.dAf[:, s * 512:(s + 1) * 512],
                ins=[bdA], outs=[buf("slots1c")])

    def ladder_tile(self, flat, bflat, i, need_abs, subs, nq, width=2048,
                    shift=0):
        buf = self.buf
        eng = "vector"
        if i == 0:
            self.red("vector", self.accs[:, 25:26], flat, ALU.max,
                     absval=True, ins=[bflat], outs=[buf("accs")])
            self.tt("vector", self.accs[:, 24:25], self.accs[:, 24:25],
                    self.accs[:, 25:26], ALU.max, ins=[buf("accs")],
                    outs=[buf("accs")])
            return
        nsub = width // subs
        sub = flat[:, shift:shift + width - subs + 1:subs]
        if need_abs:
            self.ts(eng, self.qsub[:, 0:nsub].bitcast(I32),
                    sub.bitcast(I32), ABSM, None, ALU.bitwise_and,
                    ins=[bflat], outs=[buf("qsub")])
            data, bdata = self.qsub[:, 0:nsub], buf("qsub")

        else:
            data, bdata = sub, bflat
        for j in range(NL):
            self.ts(eng, self.qsub[:, 256:256 + nsub], data,
                    self.thr[:, j:j + 1], None, ALU.is_gt,
                    ins=[bdata, buf("thr")],
                    outs=[buf("qsubhi"), buf("accs")],
                    accum=self.accs[:, j:j + 1])
        self.tt(eng, self.cnt[:, 0:NL], self.cnt[:, 0:NL],
                self.accs[:, 0:NL], ALU.add, ins=[buf("accs")],
                outs=[buf("cnt")])

    def ladder_setup_gen1(self, nq):
        buf = self.buf
        self.part_max_to_scalar(self.accs[:, 24:25], self.lad(8), "accs")
        geo = self.rows[0:1, 0:NL]
        self.ts("vector", self.srow[0:1, 0:NL], geo, self.lad(8), None,
                ALU.mult, ins=[buf("rows_sb"), buf("lads")],
                outs=[buf("srow")])
        self.bc_row(self.srow[0:1, 0:NL], NL, self.thr[:, 0:NL], "thr",
                    "srow")
        self.mset("vector", self.cnt[:], 0.0, outs=[buf("cnt")])
        self.mset("vector", self.accs[:, 0:NL], 0.0, outs=[buf("accs")])

    def interp_est(self, rstar, out11):
        buf = self.buf
        L = self.lrow
        bL = buf("lrow")
        self.col_total(self.cnt[:, 0:NL], NL, self.lads[0:1, 32:32 + NL],
                       "cnt")
        ct = self.lads[0:1, 32:32 + NL]
        self.dbg(self.srow[0:1, 0:NL], NL)
        self.dbg(ct, NL)
        bl = buf("lads")
        self.ts("vector", L[0:1, 0:NL], ct, float(rstar), None, ALU.is_ge,
                ins=[bl], outs=[bL])
        self.cp("vector", L[0:1, 32:33], L[0:1, 0:1], ins=[bL], outs=[bL])
        self.ts("vector", L[0:1, 64:64 + NL - 1], L[0:1, 0:NL - 1], -1.0,
                1.0, ALU.mult, ALU.add, ins=[bL], outs=[bL])
        self.tt("vector", L[0:1, 33:33 + NL - 1], L[0:1, 1:NL],
                L[0:1, 64:64 + NL - 1], ALU.mult, ins=[bL], outs=[bL])
        sel = L[0:1, 32:32 + NL]
        self.tt("vector", L[0:1, 64:64 + NL], sel, self.srow[0:1, 0:NL],
                ALU.mult, ins=[bL, buf("srow")], outs=[bL])
        self.red("vector", L[0:1, 96:97], L[0:1, 64:64 + NL], ALU.add,
                 ins=[bL], outs=[bL])
        self.tt("vector", L[0:1, 64:64 + NL], sel, ct, ALU.mult,
                ins=[bL, bl], outs=[bL])
        self.red("vector", L[0:1, 97:98], L[0:1, 64:64 + NL], ALU.add,
                 ins=[bL], outs=[bL])
        self.tt("vector", L[0:1, 64:64 + NL - 1], L[0:1, 33:33 + NL - 1],
                self.srow[0:1, 0:NL - 1], ALU.mult, ins=[bL, buf("srow")],
                outs=[bL])
        self.red("vector", L[0:1, 98:99], L[0:1, 64:64 + NL - 1], ALU.add,
                 ins=[bL], outs=[bL])
        self.tt("vector", L[0:1, 64:64 + NL - 1], L[0:1, 33:33 + NL - 1],
                self.lads[0:1, 32:32 + NL - 1], ALU.mult, ins=[bL, bl],
                outs=[bL])
        self.red("vector", L[0:1, 99:100], L[0:1, 64:64 + NL - 1], ALU.add,
                 ins=[bL], outs=[bL])
        self.ts("vector", L[0:1, 100:101], L[0:1, 97:98], float(rstar),
                None, ALU.subtract, ins=[bL], outs=[bL])
        self.tt("vector", L[0:1, 101:102], L[0:1, 97:98], L[0:1, 99:100],
                ALU.subtract, ins=[bL], outs=[bL])
        self.ts("vector", L[0:1, 101:102], L[0:1, 101:102], 1e-6, None,
                ALU.add, ins=[bL], outs=[bL])
        self.rcp(L[0:1, 101:102], L[0:1, 101:102], ins=[bL], outs=[bL])
        self.tt("vector", L[0:1, 100:101], L[0:1, 100:101], L[0:1, 101:102],
                ALU.mult, ins=[bL], outs=[bL])
        self.tt("vector", L[0:1, 98:99], L[0:1, 98:99], L[0:1, 96:97],
                ALU.subtract, ins=[bL], outs=[bL])
        self.tt("vector", L[0:1, 98:99], L[0:1, 98:99], L[0:1, 100:101],
                ALU.mult, ins=[bL], outs=[bL])
        self.tt("vector", L[0:1, 96:97], L[0:1, 96:97], L[0:1, 98:99],
                ALU.add, ins=[bL], outs=[bL])
        self.red("vector", L[0:1, 102:103], L[0:1, 0:NL], ALU.max,
                 ins=[bL], outs=[bL])
        self.tt("vector", L[0:1, 96:97], L[0:1, 96:97], L[0:1, 102:103],
                ALU.mult, ins=[bL], outs=[bL])
        self.ts("vector", L[0:1, 103:104], L[0:1, 102:103], -1.0, 1.0,
                ALU.mult, ALU.add, ins=[bL], outs=[bL])
        self.tt("vector", L[0:1, 103:104], L[0:1, 103:104],
                self.srow[0:1, NL - 1:NL], ALU.mult, ins=[bL, buf("srow")],
                outs=[bL])
        self.tt("vector", out11, L[0:1, 96:97], L[0:1, 103:104], ALU.add,
                ins=[bL], outs=[buf("lads")])

    def ladder_setup_gen2(self, est11, nq):
        buf = self.buf
        lin = self.rows[0:1, NL:2 * NL]
        self.ts("vector", self.srow[0:1, 0:NL], lin, est11, est11,
                ALU.mult, ALU.add, ins=[buf("rows_sb"), buf("lads")],
                outs=[buf("srow")])
        self.bc_row(self.srow[0:1, 0:NL], NL, self.thr[:, 0:NL], "thr",
                    "srow")
        self.mset("vector", self.cnt[:], 0.0, outs=[buf("cnt")])
        self.mset("vector", self.accs[:, 0:NL], 0.0, outs=[buf("accs")])

    def capture_tile(self, flat, bflat, width, spans, cnthi_col, slots,
                     bslots, slot_off, scrt, bscr):
        buf = self.buf
        data, bdata = flat, bflat
        self.ts("vector", scrt[:, 0:width], data, self.qwin[:, 1:2],
                None, ALU.is_gt, ins=[bdata, buf("qwin")],
                outs=[bscr, buf("accs")],
                accum=self.accs[:, 30:31])
        self.tt("vector", self.cnthi[:, cnthi_col:cnthi_col + 1],
                self.cnthi[:, cnthi_col:cnthi_col + 1],
                self.accs[:, 30:31], ALU.add, ins=[buf("accs")],
                outs=[buf("cnthi")])
        self.ts("vector", scrt[:, 0:width], data, self.qwin[:, 1:2],
                None, ALU.is_le, ins=[bdata, buf("qwin")], outs=[bscr])
        self.tt("gpsimd", scrt[:, 0:width], scrt[:, 0:width],
                data, ALU.mult, ins=[bdata, bscr], outs=[bscr])
        for s in range(width // spans):
            self.max8(slots[:, slot_off + s * 8:slot_off + (s + 1) * 8],
                      scrt[:, s * spans:(s + 1) * spans],
                      ins=[bscr], outs=[bslots])

    def rank_consts(self, n_const, jl_out, jh_out, hw_out):
        buf = self.buf
        bl = buf("lads")
        L = self.lrow
        bL = buf("lrow")
        self.mset("vector", L[0:1, 110:111], float(n_const), outs=[bL])
        self.mset("vector", L[0:1, 111:112], 0.995, outs=[bL])
        self.ts("vector", L[0:1, 112:113], L[0:1, 110:111], 1.0, None,
                ALU.subtract, ins=[bL], outs=[bL])
        self.tt("vector", L[0:1, 113:114], L[0:1, 111:112], L[0:1, 112:113],
                ALU.mult, ins=[bL], outs=[bL])
        self.cp("vector", self.i32s[0:1, 0:1], L[0:1, 113:114], ins=[bL],
                outs=[self.buf("i32s")])
        self.cp("vector", L[0:1, 115:116], self.i32s[0:1, 0:1],
                ins=[self.buf("i32s")], outs=[bL])
        self.ts("vector", L[0:1, 114:115], L[0:1, 113:114], 0.0, None,
                ALU.max, ins=[bL], outs=[bL])
        self.tt("vector", L[0:1, 114:115], L[0:1, 113:114], L[0:1, 115:116],
                ALU.subtract, ins=[bL], outs=[bL])
        self.ts("vector", L[0:1, 114:115], L[0:1, 114:115], 0.0, 0.9999,
                ALU.max, ALU.min, ins=[bL], outs=[bL])
        self.tt("vector", L[0:1, 115:116], L[0:1, 113:114], L[0:1, 114:115],
                ALU.subtract, ins=[bL], outs=[bL])
        self.tt("vector", L[0:1, 116:117], L[0:1, 110:111], L[0:1, 115:116],
                ALU.subtract, ins=[bL], outs=[bL])
        self.ts("vector", jl_out, L[0:1, 116:117], 1.0, None, ALU.subtract,
                ins=[bL], outs=[bl])
        self.ts("vector", L[0:1, 117:118], L[0:1, 114:115], 0.0, None,
                ALU.is_gt, ins=[bL], outs=[bL])
        self.tt("vector", jh_out, jl_out, L[0:1, 117:118], ALU.subtract,
                ins=[bL, bl], outs=[bl])
        self.cp("vector", hw_out, L[0:1, 114:115], ins=[bL], outs=[bl])

    def gap_pick(self, glob_row, thr_row, tgt11, lo_out, hi_out, chi_out,
                 npt):
        buf = self.buf
        L, bL = self.lrow, buf("lrow")
        bl = buf("lads")
        self.ts("vector", L[0:1, 220:221], tgt11, 1.0, None, ALU.add,
                ins=[bl], outs=[bL])
        self.fw.op("vector", lambda v: v.tensor_scalar(
            L[0:1, 128:128 + npt], glob_row, L[0:1, 220:221], None,
            ALU.is_ge), ins=[bL, buf("srow")], outs=[bL])
        self._gp_n2 = getattr(self, "_gp_n2", -1) + 1
        if False and self.debug and self._gp_n2 == 9:
            self.dbg_mark(1)
            self.dbg(L[0:1, 128:128 + npt], npt)
            self.dbg(glob_row, npt)
            self.dbg(L[0:1, 220:221])
        self.ts("vector", L[0:1, 150:150 + npt - 1],
                L[0:1, 129:128 + npt], -1.0, 1.0, ALU.mult, ALU.add,
                ins=[bL], outs=[bL])
        self.tt("vector", L[0:1, 172:172 + npt - 1],
                L[0:1, 128:128 + npt - 1], L[0:1, 150:150 + npt - 1],
                ALU.mult, ins=[bL], outs=[bL])
        self.cp("vector", L[0:1, 172 + npt - 1:172 + npt],
                L[0:1, 128 + npt - 1:128 + npt], ins=[bL], outs=[bL])
        sel = L[0:1, 172:172 + npt]
        self.tt("vector", L[0:1, 195:195 + npt], sel, thr_row, ALU.mult,
                ins=[bL, buf("srow")], outs=[bL])
        self.red("vector", lo_out, L[0:1, 195:195 + npt], ALU.add,
                 ins=[bL], outs=[bl])
        self.tt("vector", L[0:1, 195:195 + npt - 1],
                L[0:1, 172:172 + npt - 1], thr_row[0:1, 1:npt], ALU.mult,
                ins=[bL, buf("srow")], outs=[bL])
        self.red("vector", hi_out, L[0:1, 195:195 + npt - 1], ALU.add,
                 ins=[bL], outs=[bl])
        self.tt("vector", L[0:1, 195:195 + npt - 1],
                L[0:1, 172:172 + npt - 1], glob_row[0:1, 1:npt], ALU.mult,
                ins=[bL, buf("srow")], outs=[bL])
        self.red("vector", chi_out, L[0:1, 195:195 + npt - 1], ALU.add,
                 ins=[bL], outs=[bl])
        self._gp_n = getattr(self, "_gp_n", -1) + 1
        if False and self.debug and self._gp_n in (8, 9) and self.dbg_idx < 230:
            self.dbg_mark(2 + self._gp_n)
            self.dbg(L[0:1, 220:221])
            self.dbg(L[0:1, 128:128 + npt], npt)
            self.dbg(glob_row, npt)
            self.dbg(thr_row, npt)
            self.dbg(sel, npt)

    def phase_barrier_all(self):
        toks = self.fw.barrier_tokens()
        self._barrier_toks = toks
        for b in self._bufs.values():
            b.readers = list(b.readers) + toks

    def window_from_est(self, est11, win):
        buf = self.buf
        self.ts("vector", self.lad(10), est11, float(1.0 - win), None,
                ALU.mult, ins=[buf("lads")], outs=[buf("lads")])
        self.ts("vector", self.lad(11), est11, float(1.0 + win), None,
                ALU.mult, ins=[buf("lads")], outs=[buf("lads")])
        self.bc_pp(self.lad(10), self.qwin[:, 0:1], "qwin")
        self.bc_pp(self.lad(11), self.qwin[:, 1:2], "qwin")

    def w1_prep_chunk(self, c):
        fw, buf = self.fw, self.buf
        par = c % 2
        st, bst = self.w1s[par], buf(f"w1s{par}")
        tb, btb = self.w1tb[par], buf(f"w1t{par}")
        xb_, bxb_ = self.w1x[par], buf(f"w1x{par}")
        fw.dma("sync", st[:, :], self.w1_d.ap()[c * 128:(c + 1) * 128, :],
               ins=[bst], outs=[bst], sem=f"l_w1p{par}")
        self.ts("vector", st[:, :], st[:, :], self.scp[:, 0:1], MAGIC,
                ALU.mult, ALU.add, ins=[bst, buf("scp")], outs=[bst])
        self.ts("vector", st[:, :], st[:, :], MAGIC, -1.0, ALU.subtract,
                ALU.max, ins=[bst], outs=[bst])
        self.ts("vector", tb[:, :], st[:, :], 1.0, None, ALU.min,
                ins=[bst], outs=[btb])
        for dc in range(DC):
            pt = self.ptr[dc % 2]
            bpt = buf(f"ptr{dc % 2}")
            self.tr(pt[:, 0:128], tb[:, dc * 128:(dc + 1) * 128],
                    ins=[btb], outs=[bpt])
            self.cp("vector", xb_[:, dc, :], pt[:, 0:128], ins=[bpt],
                    outs=[bxb_])
        fw.dma("gpsimd",
               self.w1q_sp.ap()[:, :, c * 128:(c + 1) * 128].rearrange(
                   "dc p h -> p dc h"),
               xb_[:], ins=[bxb_, buf("w1q_sp")], outs=[buf("w1q_sp")],
               sem=f"s_w1q{par}")

    def w1_finish(self):
        for dc in range(DC):
            self.fw.dma("sync", self.w1qT[:, dc, :],
                        self.w1q_sp.ap()[dc, :, :],
                        ins=[self.buf("w1q_sp"), self.buf("w1qT")],
                        outs=[self.buf("w1qT"), self.buf("slots1c")],
                        sem="l_w1q")

    def w2_prep_chunk(self, c):
        fw, buf = self.fw, self.buf
        for qt in range(4):
            par = (c * 4 + qt) % 2
            st, bst = self.w2s[par], buf(f"w2s{par}")
            tb, btb = self.w2tb[par], buf(f"w2tb{par}")
            xb_, bxb_ = self.w2x[par], buf(f"w2x{par}")
            fw.dma("sync", st[:, :],
                   self.w2_d.ap()[c * 128:(c + 1) * 128,
                                  qt * 1024:(qt + 1) * 1024],
                   ins=[bst], outs=[bst], sem=f"l_w2p{par}")
            self.ts("vector", st[:, :], st[:, :], self.scp[:, 1:2], MAGIC,
                    ALU.mult, ALU.add, ins=[bst, buf("scp")], outs=[bst])
            self.ts("vector", st[:, :], st[:, :], MAGIC, -1.0,
                    ALU.subtract, ALU.max, ins=[bst], outs=[bst])
            self.ts("vector", tb[:, :], st[:, :], 1.0, None, ALU.min,
                    ins=[bst], outs=[btb])
            for blk in range(8):
                pt = self.ptr[blk % 2]
                bpt = buf(f"ptr{blk % 2}")
                self.tr(pt[:, 0:128], tb[:, blk * 128:(blk + 1) * 128],
                        ins=[btb], outs=[bpt])
                self.cp("vector", xb_[:, blk, :], pt[:, 0:128], ins=[bpt],
                        outs=[bxb_])
            fw.dma("gpsimd",
                   self.w2q_sp.ap()[qt * 8:(qt + 1) * 8, :,
                                    c * 128:(c + 1) * 128].rearrange(
                       "hc p d -> p hc d"),
                   xb_[:], ins=[bxb_, buf("w2q_sp")], outs=[buf("w2q_sp")],
                   sem=f"s_w2q{par}")

    def w2_finish(self):
        for hc in range(HC):
            self.fw.dma("scalar", self.w2qT[:, hc, :],
                        self.w2q_sp.ap()[hc, :, :],
                        ins=[self.buf("w2q_sp")],
                        outs=[self.buf("w1qT"), self.buf("w2qT")],
                        sem="l_w2q")

    def q1_window_phase(self):
        for c in range(HC):
            self.w1_prep_chunk(c)

    def capture_q1_tile(self, flat, bflat, i):
        buf = self.buf
        width = 2048
        self.ts("vector", self.selscr[:, 0:width].bitcast(I32)[:, 0:width],
                flat.bitcast(I32), ABSM, None, ALU.bitwise_and,
                ins=[bflat], outs=[buf("selscr")])
        data, bdata = self.selscr[:, 0:width], buf("selscr")
        scr = self.q1scr
        bscr = buf("q1scr")
        self.ts("vector", scr, data, self.qwin[:, 1:2], None, ALU.is_gt,
                ins=[bdata, buf("qwin")], outs=[bscr, buf("accs")],
                accum=self.accs[:, 30:31])
        self.tt("vector", self.cnthi[:, i:i + 1], self.cnthi[:, i:i + 1],
                self.accs[:, 30:31], ALU.add, ins=[buf("accs")],
                outs=[buf("cnthi")])
        self.ts("vector", scr, data, self.qwin[:, 1:2], None, ALU.is_le,
                ins=[bdata, buf("qwin")], outs=[bscr])
        self.tt("vector", scr, scr, data, ALU.mult, ins=[bdata, bscr],
                outs=[bscr])
        for s in range(8):
            self.max8(self.slots1[:, i * 64 + s * 8:i * 64 + (s + 1) * 8],
                      scr[:, s * 256:(s + 1) * 256],
                      ins=[bscr], outs=[buf("slots1c")])

    def select_quantile(self, qid, slots, bslots, cnthi_lo, cnthi_n, nconst,
                        gamma_lad, s_col, gs_col):
        fw, buf = self.fw, self.buf
        L, bL = self.lrow, buf("lrow")
        bl = buf("lads")
        SR, bsr = self.srow, buf("srow")
        self.red("vector", self.accs[:, 26:27],
                 self.cnthi[:, cnthi_lo:cnthi_lo + cnthi_n], ALU.add,
                 ins=[buf("cnthi")], outs=[buf("accs")])
        self.col_total(self.accs[:, 26:27], 1, self.lad(16), "accs")
        self.rank_consts(nconst, self.lad(13), self.lad(14), self.lad(15))
        self.cp("vector", L[0:1, 0:1], self.lad(16), ins=[bl], outs=[bL])
        self.cp("vector", L[0:1, 1:2], self.lad(10), ins=[bl], outs=[bL])
        self.cp("vector", L[0:1, 2:3], self.lad(11), ins=[bl], outs=[bL])
        fw.dma("gpsimd", self.cc1_src.ap(), L[0:1, 0:64],
               ins=[bL, buf("cc1")], outs=[buf("cc1")], sem=f"c1s{qid}")
        self.coll("AllGather", ALU.bypass, self.cc1_src.ap(),
                  self.cc1_dst.ap(), ins=[buf("cc1")], outs=[buf("cc1d")])
        fw.dma("sync", L[0:1, 0:512],
               self.cc1_dst.ap().rearrange("r k -> (r k)"),
               ins=[buf("cc1d"), bL], outs=[bL], sem=f"c1l{qid}")
        self.red("vector", self.lad(18), L[0:1, 1:512:64], ALU.max,
                 ins=[bL], outs=[bl])
        self.red("vector", self.lad(19), L[0:1, 2:512:64], ALU.min,
                 ins=[bL], outs=[bl])
        self.red("vector", self.lad(12), L[0:1, 0:512:64], ALU.add,
                 ins=[bL], outs=[bl])
        self.tt("vector", self.lad(17), self.lad(19), self.lad(18),
                ALU.subtract, ins=[bl], outs=[bl])
        nslots = slots.shape[-1]
        for rnd in range(2):
            if rnd == 0:
                span, base = self.lad(17), self.lad(18)
            else:
                self.tt("vector", self.lad(36), self.lad(34), self.lad(33),
                        ALU.subtract, ins=[bl], outs=[bl])
                span, base = self.lad(36), self.lad(33)
            self.ts("vector", SR[0:1, 32:32 + NLE],
                    self.rows[0:1, 32:32 + NLE], span, base,
                    ALU.mult, ALU.add, ins=[buf("rows_sb"), bl], outs=[bsr])
            self.bc_row(SR[0:1, 32:32 + NLE], NLE, self.thr[:, NLE:2 * NLE],
                        "thr", "srow")
            for j in range(NLE):
                self.ts("vector", self.selscr[:, 0:nslots], slots,
                        self.thr[:, NLE + j:NLE + j + 1], None, ALU.is_gt,
                        ins=[bslots, buf("thr")], outs=[buf("selscr"),
                                                        buf("accs")],
                        accum=self.accs[:, j:j + 1])
            self.col_total(self.accs[:, 0:NLE], NLE, L[0:1, 300:300 + NLE],
                           "accs", out_key="lrow")
            self.ts("vector", L[0:1, 0:NLE], L[0:1, 300:300 + NLE],
                    self.lad(16), None, ALU.add, ins=[bL, bl], outs=[bL])
            fw.dma("gpsimd", self.cc2_src.ap(), L[0:1, 0:32],
                   ins=[bL, buf("cc2")], outs=[buf("cc2")],
                   sem=f"c2s{qid}{rnd}")
            self.coll("AllGather", ALU.bypass, self.cc2_src.ap(),
                      self.cc2_dst.ap(), ins=[buf("cc2")],
                      outs=[buf("cc2d")])
            fw.dma("sync", L[0:1, 0:256],
                   self.cc2_dst.ap().rearrange("r k -> (r k)"),
                   ins=[buf("cc2d"), bL], outs=[bL], sem=f"c2l{qid}{rnd}")
            self.cp("vector", SR[0:1, 96:96 + NLE], L[0:1, 0:NLE], ins=[bL],
                    outs=[bsr])
            for r in range(1, 8):
                self.tt("vector", SR[0:1, 96:96 + NLE],
                        SR[0:1, 96:96 + NLE],
                        L[0:1, r * 32:r * 32 + NLE], ALU.add, ins=[bL, bsr],
                        outs=[bsr])
            self.gap_pick(SR[0:1, 96:96 + NLE], SR[0:1, 32:32 + NLE],
                          self.lad(13), self.lad(33), self.lad(34),
                          self.lad(35), NLE)
        if self.debug and qid == 2:
            self.dbg_mark(15)
            self.dbg(self.lads[0:1, 20:27], 7)
            self.dbg(self.lads[0:1, 33:36], 3)
            self.dbg(self.srow[0:1, 96:96 + NLE], NLE)
            self.dbg(self.srow[0:1, 32:32 + NLE], NLE)
        self.bc_pp(self.lad(33), self.qwin[:, 2:3], "qwin")
        self.bc_pp(self.lad(34), self.qwin[:, 3:4], "qwin")
        self.stt("vector", self.selscr[:, 0:nslots], slots, self.qwin[:, 2:3],
                 slots, ALU.is_gt, ALU.mult, ins=[bslots, buf("qwin")],
                 outs=[buf("selscr")])
        self.stt("vector", self.selscr[:, 0:nslots], self.selscr[:, 0:nslots],
                 self.qwin[:, 3:4], self.selscr[:, 0:nslots], ALU.is_le,
                 ALU.mult, ins=[buf("qwin")], outs=[buf("selscr")])
        self.max8(self.candx[:, 0:8], self.selscr[:, 0:nslots],
                  ins=[buf("selscr")], outs=[buf("candx")])
        self.mrep(self.selscr[:, 0:nslots], self.candx[:, 0:8],
                  self.selscr[:, 0:nslots], ins=[buf("candx")],
                  outs=[buf("selscr")])
        self.max8(self.candx[:, 8:16], self.selscr[:, 0:nslots],
                  ins=[buf("selscr")], outs=[buf("candx")])
        fw.dma("gpsimd",
               self.cc3_src.ap().rearrange("a (p s) -> (a p) s", p=128),
               self.candx[:],
               ins=[buf("candx"), buf("cc3")], outs=[buf("cc3")],
               sem=f"c3s{qid}")
        self.coll("AllGather", ALU.bypass, self.cc3_src.ap(),
                  self.cc3_dst.ap(), ins=[buf("cc3")], outs=[buf("cc3d")])
        fw.dma("sync", self.gath[:].rearrange("p (r s) -> p r s", r=8),
               self.cc3_dst.ap().rearrange("r (p s) -> p r s", p=128),
               ins=[buf("cc3d"), buf("gath")], outs=[buf("gath")],
               sem=f"c3l{qid}")
        for ti, (tgt, vout) in enumerate([(self.lad(13), self.lad(28))]):
            lo, hi, ca = self.lad(33), self.lad(34), self.lad(35)
            for rnd in range(2):
                self.tt("vector", self.lad(36), hi, lo, ALU.subtract,
                        ins=[bl], outs=[bl])
                self.ts("vector", SR[0:1, 32:32 + NLE],
                        self.rows[0:1, 32:32 + NLE], self.lad(36), lo,
                        ALU.mult, ALU.add, ins=[buf("rows_sb"), bl],
                        outs=[bsr])
                self.bc_row(SR[0:1, 32:32 + NLE], NLE,
                            self.thr[:, NLE:2 * NLE], "thr", "srow")
                for j in range(NLE):
                    self.ts("vector", self.qsub[:, 128:256], self.gath[:],
                            self.thr[:, NLE + j:NLE + j + 1], None,
                            ALU.is_gt, ins=[buf("gath"), buf("thr")],
                            outs=[buf("qsub"), buf("accs")],
                            accum=self.accs[:, j:j + 1])
                self.col_total(self.accs[:, 0:NLE], NLE,
                               L[0:1, 300:300 + NLE], "accs",
                               out_key="lrow")
                self.ts("vector", SR[0:1, 96:96 + NLE],
                        L[0:1, 300:300 + NLE], self.lad(35), None, ALU.add,
                        ins=[bL, bl], outs=[bsr])
                lo2 = self.lad(44 + 3 * rnd + 8 * ti)
                hi2 = self.lad(45 + 3 * rnd + 8 * ti)
                ca2 = self.lad(46 + 3 * rnd + 8 * ti)
                self.gap_pick(SR[0:1, 96:96 + NLE], SR[0:1, 32:32 + NLE],
                              tgt, lo2, hi2, ca2, NLE)
                lo, hi, ca = lo2, hi2, ca2
            self.bc_pp(lo, self.qwin[:, 2:3], "qwin")
            self.bc_pp(hi, self.qwin[:, 3:4], "qwin")
            self.stt("vector", self.qsub[:, 0:128], self.gath[:],
                     self.qwin[:, 2:3], self.gath[:], ALU.is_gt, ALU.mult,
                     ins=[buf("gath"), buf("qwin")], outs=[buf("qsub")])
            self.stt("vector", self.qsub[:, 0:128], self.qsub[:, 0:128],
                     self.qwin[:, 3:4], self.qsub[:, 0:128], ALU.is_le,
                     ALU.mult, ins=[buf("qwin")], outs=[buf("qsub")])
            self.max8(self.candx[:, 0:8], self.qsub[:, 0:128],
                      ins=[buf("qsub")], outs=[buf("candx")])
            fw.dma("gpsimd",
                   self.row_sp.ap().rearrange("a (p s) -> (a p) s", p=128),
                   self.candx[:, 0:8],
                   ins=[buf("candx"), buf("row_sp")], outs=[buf("row_sp")],
                   sem=f"rs{qid}{ti}")
            fw.dma("sync", self.row1k[0:1, :], self.row_sp.ap(),
                   ins=[buf("row_sp"), buf("row1k")], outs=[buf("row1k")],
                   sem=f"rl{qid}{ti}")
            self.max8(SR[0:1, 224:232], self.row1k[0:1, :],
                      ins=[buf("row1k")], outs=[bsr])
            self.mrep(self.row1k[0:1, :], SR[0:1, 224:232],
                      self.row1k[0:1, :], ins=[bsr], outs=[buf("row1k")])
            self.max8(SR[0:1, 232:240], self.row1k[0:1, :],
                      ins=[buf("row1k")], outs=[bsr])
            self.tt("vector", self.lad(36), tgt, ca, ALU.subtract,
                    ins=[bl], outs=[bl])
            self.ts("vector", L[0:1, 52:53], self.lad(36), 15.0, None,
                    ALU.min, ins=[bl], outs=[bL])
            self.ts("vector", L[0:1, 16:32], self.iota16[:], L[0:1, 52:53],
                    None, ALU.is_equal, ins=[buf("iota16"), bL], outs=[bL])
            self.tt("vector", L[0:1, 16:32], L[0:1, 16:32],
                    SR[0:1, 224:240], ALU.mult, ins=[bsr, bL], outs=[bL])
            self.red("vector", vout, L[0:1, 16:32], ALU.add, ins=[bL],
                     outs=[bl])
            ok = self.lad(40 + ti)
            self.ts("vector", L[0:1, 40:41], tgt, 1.0, None, ALU.add,
                    ins=[bl], outs=[bL])
            self.fw.op("vector", lambda v: v.tensor_scalar(
                L[0:1, 41:42], SR[0:1, 96:97], L[0:1, 40:41], None,
                ALU.is_ge), ins=[bsr, bL], outs=[bL])
            self.ts("vector", L[0:1, 42:43], self.lad(36), 0.0, None,
                    ALU.is_ge, ins=[bl], outs=[bL])
            self.ts("vector", L[0:1, 43:44], self.lad(36), 15.5, None,
                    ALU.is_le, ins=[bl], outs=[bL])
            self.fw.op("vector", lambda v: v.tensor_tensor(
                L[0:1, 53:54], SR[0:1, 224:225], SR[0:1, 239:240],
                ALU.is_equal), ins=[bsr], outs=[bL])
            self.tt("vector", L[0:1, 43:44], L[0:1, 43:44], L[0:1, 53:54],
                    ALU.max, ins=[bL], outs=[bL])
            self.tt("vector", L[0:1, 41:42], L[0:1, 41:42], L[0:1, 42:43],
                    ALU.mult, ins=[bL], outs=[bL])
            self.tt("vector", ok, L[0:1, 41:42], L[0:1, 43:44], ALU.mult,
                    ins=[bL], outs=[bl])
        self.cp("vector", self.lad(29), self.lad(28), ins=[bl], outs=[bl])
        self.cp("vector", self.lad(41), self.lad(40), ins=[bl], outs=[bl])
        self.ts("vector", L[0:1, 44:45], self.lad(15), -1.0, 1.0, ALU.mult,
                ALU.add, ins=[bl], outs=[bL])
        self.tt("vector", L[0:1, 45:46], self.lad(28), L[0:1, 44:45],
                ALU.mult, ins=[bl, bL], outs=[bL])
        self.tt("vector", L[0:1, 46:47], self.lad(29), self.lad(15),
                ALU.mult, ins=[bl], outs=[bL])
        self.tt("vector", L[0:1, 45:46], L[0:1, 45:46], L[0:1, 46:47],
                ALU.add, ins=[bL], outs=[bL])
        self.tt("vector", self.lad(42), self.lad(40), self.lad(41),
                ALU.mult, ins=[bl], outs=[bl])
        self.tt("vector", L[0:1, 47:48], self.lad(33), self.lad(34),
                ALU.add, ins=[bl], outs=[bL])
        self.ts("vector", L[0:1, 47:48], L[0:1, 47:48], 0.5, None, ALU.mult,
                ins=[bL], outs=[bL])
        self.tt("vector", L[0:1, 45:46], L[0:1, 45:46], self.lad(42),
                ALU.mult, ins=[bl, bL], outs=[bL])
        self.ts("vector", L[0:1, 48:49], self.lad(42), -1.0, 1.0, ALU.mult,
                ALU.add, ins=[bl], outs=[bL])
        self.tt("vector", L[0:1, 47:48], L[0:1, 47:48], L[0:1, 48:49],
                ALU.mult, ins=[bL], outs=[bL])
        self.tt("vector", L[0:1, 45:46], L[0:1, 45:46], L[0:1, 47:48],
                ALU.add, ins=[bL], outs=[bL])
        self.ts("vector", self.lad(43), L[0:1, 45:46], EPS, None, ALU.max,
                ins=[bL], outs=[bl])
        for s11 in (self.lad(9), self.lad(10), self.lad(11), self.lad(18),
                    self.lad(19), self.lad(12), self.lad(13), self.lad(14),
                    self.lad(28), self.lad(29), self.lad(40), self.lad(41),
                    self.lad(42), self.lad(43)):
            self.dbg(s11)
        self.rcp(L[0:1, 51:52], self.lad(43), ins=[bl], outs=[bL])
        self.ts("vector", self.lad(60), L[0:1, 51:52], 127.0, None,
                ALU.mult, ins=[bL], outs=[bl])
        self.tt("vector", L[0:1, 50:51], self.lad(gamma_lad), self.lad(43),
                ALU.mult, ins=[bl], outs=[bL])
        self.ts("vector", self.lad(61), L[0:1, 50:51],
                float(1.0 / 127.0), None, ALU.mult, ins=[bL], outs=[bl])
        self.bc_pp(self.lad(60), self.s_pp[:, s_col:s_col + 1], "s_pp")
        self.bc_pp(self.lad(61), self.s_pp[:, gs_col:gs_col + 1], "s_pp")

    def f1_phase(self):
        fw, buf = self.fw, self.buf
        self.mset("vector", self.accs[:, 24:26], 0.0, outs=[buf("accs")])
        self.q2_calls = 0
        for i in range(NT):
            par = i % 2
            pi = (i // 2) % 2
            hb, hk = self.hlnF[par], buf(f"hlnF{par}")
            kt, bkt = self.k1tok[par], buf(f"k1tok{par}")
            kT, bkT = self.k1Tw[pi], buf(f"k1Tw{pi}")
            r0 = i * TT
            fw.dma("sync", hb[:],
                   self.hln_sp.ap()[r0:r0 + TT, :].rearrange(
                       "(tc p) d -> p tc d", p=128),
                   ins=[buf("hln_sp"), hk], outs=[hk], sem=f"f1h{par}")
            for tc in range(2):
                v = hb[:, tc, :]
                self.ts("vector", v, v, self.s_pp[:, 0:1], MAGIC, ALU.mult,
                        ALU.add, ins=[hk, buf("s_pp")], outs=[hk])
                self.ts("vector", v, v, MAGIC, -128.0, ALU.subtract,
                        ALU.max, ins=[hk], outs=[hk])
                self.ts("vector", kt[:, tc, :], v, 127.0, None,
                        ALU.min, ins=[hk], outs=[bkt])
            for tc in range(2):
                for dc in range(DC):
                    pt = self.ptr[dc % 2]
                    bpt = buf(f"ptr{dc % 2}")
                    self.tr(pt[:, 0:128],
                            kt[:, tc, dc * 128:(dc + 1) * 128],
                            ins=[bkt], outs=[bpt])
                    self.cp("vector",
                            kT[:, dc,
                               par * TT + tc * 128:par * TT + (tc + 1) * 128],
                            pt[:, 0:128], ins=[bpt], outs=[bkT])
            if par == 1:
                p = i // 2
                r0p = (i - 1) * TT
                gb, gk = self.gsb[pi], buf(f"gsb{pi}")
                self.q2_pair_captures(p - 1)
                for gr in range(8):
                    for h4 in range(4):
                        hc = gr * 4 + h4
                        pbk, bp = self.pb[h4], buf(f"pb{h4}")
                        for dc in range(DC):
                            self.mm(pbk[:, 0:512],
                                    self.w1qT[:, dc,
                                              hc * 128:(hc + 1) * 128],
                                    kT[:, dc, :], dc == 0, dc == DC - 1,
                                    ins=[buf("w1qT"), bkT], outs=[bp])
                        self.act(gb[:, h4, :], pbk[:, 0:512], AF.Gelu,
                                 scale=self.s_pp[:, 1:2],
                                 ins=[bp, buf("s_pp")], outs=[gk])
                    fw.dma("gpsimd",
                           self.g_sp.ap()[gr * 4:(gr + 1) * 4, :,
                                          r0p:r0p + 512].rearrange(
                               "hc p t -> p hc t"),
                           gb[:], ins=[gk, buf("g_sp")],
                           outs=[buf("g_sp")], sem=f"sg{pi}")
                    if p == 0:
                        self.ladder_tile(gb.rearrange("p a b -> p (a b)"),
                                         gk, 0, need_abs=False, subs=32,
                                         nq=2, width=2048, shift=0)
                    elif p <= 2 or (p == 3 and gr % 2 == p % 2):
                        self.ladder_tile(gb.rearrange("p a b -> p (a b)"),
                                         gk, 1, need_abs=False, subs=32,
                                         nq=2, width=2048,
                                         shift=(p * 4 + gr) % 32)
                        self.q2_calls += 1
                if p == 0:
                    self.ladder_setup_gen1(2)
                if p == 2:
                    self.interp_est(0.005 * self.q2_calls * 64 * 128,
                                    self.lad(9))
                    self.ladder_setup_gen2(self.lad(9), 2)
                    self.q2_calls = 0
                if p == 3:
                    self.interp_est(0.005 * self.q2_calls * 64 * 128,
                                    self.lad(9))
                    self.window_from_est(self.lad(9), Q2_WIN)
                self.w2_prep_chunk(p)
        self.q2_pair_captures(7)
        self.w2_finish()

    def q2_pair_captures(self, p):
        if p < 4:
            return
        fw, buf = self.fw, self.buf
        for gr in range(8):
            for s, pp in enumerate((p, p - 4)):
                rb, rk = self.q2rd[s], buf(f"q2rd{s}")
                fw.dma("sync", rb[:],
                       self.g_sp.ap()[gr * 4:(gr + 1) * 4, :,
                                      pp * 512:pp * 512 + 512].rearrange(
                           "hc p t -> p hc t"),
                       ins=[buf("g_sp"), rk], outs=[rk], sem=f"qgr{s}")
                self.capture_tile(
                    rb.rearrange("p a b -> p (a b)"), rk,
                    2048, 1024, NT + pp, self.slots2,
                    buf("slots2c"), (pp * 8 + gr) * 16,
                    self.selscr, buf("selscr"))

    def f2_phase(self):
        fw, buf = self.fw, self.buf
        out_toks = []
        st, bst = self.stats, buf("stats")
        for i in range(NT):
            par = i % 2
            r0 = i * TT
            x2b, bx2 = self.x2F[par], buf(f"x2F{par}")
            ob, bob = self.outsb[par], buf(f"outs{par}")
            fw.dma("sync", x2b[:],
                   self.x2_sp.ap()[r0:r0 + TT, :].rearrange(
                       "(tc p) d -> p tc d", p=128),
                   ins=[buf("x2_sp"), bx2], outs=[bx2], sem=f"f2x{par}")
            for gr in range(8):
                gb = self.gF[gr % 2]
                gk = buf(f"gF{gr % 2}")
                kb = self.k2[gr % 2]
                kk = buf(f"k2_{gr % 2}")
                fw.dma("sync", gb[:],
                       self.g_sp.ap()[gr * 4:(gr + 1) * 4, :,
                                      r0:r0 + TT].rearrange(
                           "hc p t -> p hc t"),
                       ins=[buf("g_sp"), gk], outs=[gk], sem=f"f2g{gr % 2}")
                flat = gb.rearrange("p a b -> p (a b)")
                self.ts("vector", flat, flat, self.s_pp[:, 2:3], MAGIC,
                        ALU.mult, ALU.add, ins=[gk, buf("s_pp")], outs=[gk])
                self.ts("vector", flat, flat, MAGIC, -128.0, ALU.subtract,
                        ALU.max, ins=[gk], outs=[gk])
                self.ts("vector", kb.rearrange("p a b -> p (a b)"), flat,
                        127.0, None, ALU.min, ins=[gk], outs=[kk])
                for h4 in range(4):
                    hc = gr * 4 + h4
                    for tc in range(2):
                        for nq in range(2):
                            pbk = self.pb[tc * 2 + nq]
                            bp = buf(f"pb{tc * 2 + nq}")
                            self.mm(pbk[:, 0:512],
                                    kb[:, h4, tc * 128:(tc + 1) * 128],
                                    self.w2qT[:, hc,
                                              nq * 512:(nq + 1) * 512],
                                    hc == 0, hc == HC - 1,
                                    ins=[kk, buf("w2qT")], outs=[bp])
            for tc in range(2):
                for nq in range(2):
                    pbk = self.pb[tc * 2 + nq]
                    bp = buf(f"pb{tc * 2 + nq}")
                    self.ts("vector",
                            ob[:, tc, nq * 512:(nq + 1) * 512],
                            pbk[:, 0:512], self.s_pp[:, 3:4], None,
                            ALU.mult, ins=[bp, buf("s_pp")],
                            outs=[bob])
                v = ob[:, tc, :]
                self.red("vector", st[:, tc:tc + 1], v, ALU.add,
                         ins=[bob], outs=[bst])
                self.act(self.pb[4][:, 0:512], v[:, 0:512], AF.Square,
                         ins=[bob], outs=[buf("pb4"), bst],
                         accum=st[:, 2 + tc:3 + tc])
                self.act(self.pb[4][:, 0:512], v[:, 512:1024], AF.Square,
                         ins=[bob], outs=[buf("pb4"), buf("accs")],
                         accum=self.accs[:, 28:29])
                self.tt("vector", st[:, 2 + tc:3 + tc],
                        st[:, 2 + tc:3 + tc], self.accs[:, 28:29], ALU.add,
                        ins=[buf("accs")], outs=[bst])
                self.ts("vector", st[:, 4 + tc:5 + tc], st[:, tc:tc + 1],
                        INV_D, None, ALU.mult, ins=[bst], outs=[bst])
                self.ts("vector", st[:, 6 + tc:7 + tc],
                        st[:, 2 + tc:3 + tc], INV_D, None, ALU.mult,
                        ins=[bst], outs=[bst])
                self.tt("vector", st[:, 8 + tc:9 + tc],
                        st[:, 4 + tc:5 + tc], st[:, 4 + tc:5 + tc],
                        ALU.mult, ins=[bst], outs=[bst])
                self.tt("vector", st[:, 6 + tc:7 + tc],
                        st[:, 6 + tc:7 + tc], st[:, 8 + tc:9 + tc],
                        ALU.subtract, ins=[bst], outs=[bst])
                self.act(st[:, 8 + tc:9 + tc], st[:, 6 + tc:7 + tc], AF.Ln,
                         bias=self.eps_t[:, 0:1],
                         ins=[bst, buf("eps_t")], outs=[bst])
                self.act(st[:, 10 + tc:11 + tc], st[:, 8 + tc:9 + tc],
                         AF.Exp, scale=-0.5, ins=[bst], outs=[bst])
                self.ts("vector", v, v, st[:, 4 + tc:5 + tc],
                        st[:, 10 + tc:11 + tc], ALU.subtract, ALU.mult,
                        ins=[bst], outs=[bob])
                self.tt("vector", v, v, x2b[:, tc, :], ALU.add,
                        ins=[bx2], outs=[bob])
            t = fw.dma("gpsimd",
                       self.y_out.ap()[r0:r0 + TT, :].rearrange(
                           "(tc p) d -> p tc d", p=128),
                       ob[:], ins=[bob, buf("y")],
                       outs=[buf("y")], sem=f"out{par}")
            out_toks.append(t)
        if self.debug:
            t = fw.dma("gpsimd", self.dbg_d.ap()[0:1, 0:320],
                       self.dbgrow[0:1, :],
                       ins=[buf("dbgrow"), buf("dbg")], outs=[buf("dbg")],
                       sem="outd")
            out_toks.append(t)
        fw.wait_all("sync", out_toks)

    def emit(self):
        self.fw.emit()


def build_program(debug=False):
    nc = bass.Bass("TRN2", target_bir_lowering=False)
    k = K(nc, debug)
    k.declare()
    k.init_phase()

    def post_b(bi):
        if bi == 0:
            k.ladder_setup_gen1(1)
        if bi == 3:
            k.interp_est(0.005 * 3 * 64 * 128, k.lad(9))
            k.ladder_setup_gen2(k.lad(9), 1)
        if bi == 7:
            k.interp_est(0.005 * 4 * 64 * 128, k.lad(9))
            k.window_from_est(k.lad(9), Q1_WIN)

    for i in range(NT):
        k.stage_a(i)
        if i % 8 != 0:
            k.stage_b_front_a(i - 1)
        if i >= 2 and (i - 2) % 8 < 6:
            k.stage_b_tail(i - 2)
            post_b(i - 2)
        k.stage_a_mm(i)
        if i % 8 != 0:
            k.stage_b_front_a2(i - 1)
            k.stage_b_front_b(i - 1)
        if i % 8 == 7:
            k.stage_b_tail(i - 1)
            post_b(i - 1)
            k.stage_b_front_a(i)
            k.stage_b_front_a2(i)
            k.stage_b_front_b(i)
            k.stage_b_tail(i)
            post_b(i)
    k.phase_barrier_all()
    k.q1_window_phase()
    k.select_quantile(1, k.slots1[:, :], k.buf("slots1c"), 0, NT, N_Q1,
                      0, 0, 1)
    k.w1_finish()
    k.phase_barrier_all()
    k.f1_phase()
    k.phase_barrier_all()
    k.select_quantile(2, k.slots2[:, :], k.buf("slots2c"), NT, NT, N_Q2,
                      2, 2, 3)
    k.phase_barrier_all()
    k.f2_phase()
    k.emit()
    return nc


def make_core_inputs(inputs, core):
    x = np.asarray(inputs["L0_attn_x"])
    xs = np.ascontiguousarray(x[2 * core:2 * core + 2].reshape(NTOK, D))
    vl = [np.asarray(inputs["L0_attn_conv_w"])[0, 0],
          np.asarray(inputs["L0_attn_conv_w"])[1, 0],
          np.asarray(inputs["L0_attn_conv_w"])[2, 0],
          np.asarray(inputs["L0_attn_conv_b"]),
          np.asarray(inputs["L0_attn_A"]),
          np.asarray(inputs["L0_attn_D_skip"]),
          np.asarray(inputs["L0_attn_b_dt"]),
          np.asarray(inputs["L0_attn_gamma"]),
          np.asarray(inputs["L0_attn_beta"]),
          np.zeros(D, np.float32), np.zeros(D, np.float32),
          np.zeros(D, np.float32)]
    vecs = np.stack([np.asarray(v, np.float32).reshape(DC, 128).T
                     for v in vl], axis=2).reshape(128, DC * 12)
    vecs = np.ascontiguousarray(vecs)
    lngb = np.ascontiguousarray(np.concatenate([
        np.broadcast_to(np.asarray(inputs["L0_ffn_ln_g"], np.float32),
                        (128, D)),
        np.broadcast_to(np.asarray(inputs["L0_ffn_ln_b"], np.float32),
                        (128, D))], axis=1))
    ident = np.eye(128, dtype=ml_dtypes.bfloat16)
    rows = np.zeros((1, 64), np.float32)
    rows[0, 0:NL] = (GEOLO ** (np.arange(NL) / (NL - 1))).astype(np.float32)
    rows[0, NL:2 * NL] = ((NL - 1) / 2.0 - np.arange(NL)) / ((NL - 1) / 2.0) \
        * LIN_W
    rows[0, 32:32 + NLE] = np.arange(NLE) / float(NLE - 1)
    return {
        "x": xs,
        "W_in": np.asarray(inputs["L0_attn_W_in"], np.float32),
        "W_out": np.asarray(inputs["L0_attn_W_out"], np.float32),
        "W_xp": np.asarray(inputs["L0_attn_W_xp"], np.float32),
        "W_dt": np.asarray(inputs["L0_attn_W_dt"], np.float32),
        "W_Bg": np.asarray(inputs["L0_attn_W_Bg"], np.float32),
        "W_Cg": np.asarray(inputs["L0_attn_W_Cg"], np.float32),
        "W_Bp": np.asarray(inputs["L0_attn_W_Bp"], np.float32),
        "W_Cp": np.asarray(inputs["L0_attn_W_Cp"], np.float32),
        "w1": np.asarray(inputs["L0_ffn_w1"], np.float32),
        "w2": np.asarray(inputs["L0_ffn_w2"], np.float32),
        "vecs": vecs, "lngb": lngb, "ident": ident, "rows": rows,
    }


def kernel(**inputs):
    nc = build_program(DEBUG)
    in_maps = [make_core_inputs(inputs, c) for c in range(8)]
    res = run_bass_kernel_spmd(nc, in_maps, core_ids=list(range(8)))
    outs = [np.asarray(r["y"]) for r in res.results]
    return np.concatenate(outs, 0).reshape(16, 2048, 1024)


# version 20
# speedup vs baseline: 1.0515x; 1.0515x over previous
import contextlib

import numpy as np
import ml_dtypes

import concourse.bass as bass
import concourse.mybir as mybir
from concourse.bass_utils import run_bass_kernel_spmd

F32 = mybir.dt.float32
BF16 = mybir.dt.bfloat16
I32 = mybir.dt.int32
AF = mybir.ActivationFunctionType
ALU = mybir.AluOpType
AX = mybir.AxisListType

DEBUG = False
TRACE_SITES = False

ENGINES = ("tensor", "vector", "scalar", "gpsimd", "sync")


class Buf:
    __slots__ = ("name", "writer", "readers")

    def __init__(self, name=""):
        self.name = name
        self.writer = None
        self.readers = []


class Fw:

    def __init__(self, nc):
        self.nc = nc
        self._stack = contextlib.ExitStack()
        self.sem = {}
        self.count = {}
        self.known = {e: {} for e in ENGINES}
        self.eng_sem = {}
        self.prog = {e: [] for e in ENGINES}
        self.sites = {}
        for e in ENGINES:
            sname = "S_" + e
            self.sem[sname] = self._stack.enter_context(nc.semaphore(sname))
            self.count[sname] = 0
            self.eng_sem[e] = sname

    def new_sem(self, name):
        if name not in self.sem:
            self.sem[name] = self._stack.enter_context(self.nc.semaphore(name))
            self.count[name] = 0
        return name

    def _sched_wait(self, eng, token):
        if token is None:
            return
        sname, val = token
        if self.known[eng].get(sname, 0) >= val:
            return
        if not sname.startswith("S_") and val != self.count[sname]:
            raise RuntimeError(
                f"wait on dma sem {sname} at {val} != cumulative "
                f"{self.count[sname]}; use a dedicated sem")
        self.known[eng][sname] = val
        sem = self.sem[sname]
        self.prog[eng].append(lambda h, s=sem, v=val: h.wait_ge(s, v))

    def _deps_for(self, ins, outs):
        deps = []
        for b in ins:
            if b.writer is not None:
                deps.append(b.writer)
        for b in outs:
            if b.writer is not None:
                deps.append(b.writer)
            deps.extend(b.readers)
        return deps

    def _finish(self, ins, outs, token):
        for b in ins:
            b.readers.append(token)
        for b in outs:
            b.writer = token
            b.readers = []

    def op(self, eng, fn, ins=(), outs=(), deps=()):
        for d in self._deps_for(ins, outs):
            self._sched_wait(eng, d)
        for d in deps:
            self._sched_wait(eng, d)
        sname = self.eng_sem[eng]
        self.count[sname] += 1
        sem = self.sem[sname]

        def run(h, fn=fn, sem=sem):
            fn(h).then_inc(sem, 1)

        self.prog[eng].append(run)
        if TRACE_SITES:
            import traceback
            self.sites.setdefault(eng, {})[len(self.prog[eng]) - 1] = \
                traceback.extract_stack()[-4:-2]
        token = (sname, self.count[sname])
        self._finish(ins, outs, token)
        return token

    def dma(self, eng, out_ap, in_ap, ins=(), outs=(), sem="dma", deps=(),
            **kw):
        self.new_sem(sem)
        for d in self._deps_for(ins, outs):
            self._sched_wait(eng, d)
        for d in deps:
            self._sched_wait(eng, d)
        self.count[sem] += 16
        semh = self.sem[sem]

        def run(h, out_ap=out_ap, in_ap=in_ap, semh=semh, kw=kw):
            h.dma_start(out=out_ap, in_=in_ap, **kw).then_inc(semh, 16)

        self.prog[eng].append(run)
        token = (sem, self.count[sem])
        self._finish(ins, outs, token)
        return token

    def wait_all(self, eng, tokens):
        best = {}
        for sname, val in tokens:
            best[sname] = max(best.get(sname, 0), val)
        for t in best.items():
            self._sched_wait(eng, t)

    def barrier_tokens(self):
        toks = [(self.eng_sem[e], self.count[self.eng_sem[e]])
                for e in ENGINES]
        toks += [(s, self.count[s]) for s in self.sem
                 if not s.startswith("S_")]
        return toks

    def emit(self):
        with self.nc.Block() as block:
            for e in ENGINES:
                if not self.prog[e]:
                    continue
                getattr(block, e)(lambda h, e=e: [f(h) for f in self.prog[e]])
        self._stack.close()


D = 1024
H = 4096
DC = 8
HC = 32
TT = 256
NT = 16
NTOK = NT * TT
RK = 64
DS = 16
EPS = 1e-5
INV_D = float(1.0 / 1024.0)
MAGIC = float(3 << 22)
N_Q1 = float(16 * 2048 * 1024)
N_Q2 = float(16 * 2048 * 4096)
RG = [[0, 1, 2, 3, 4, 5, 6, 7]]
NL = 16
NLE = 18
GEOLO = 0.04
LIN_W = 0.10
Q1_WIN = 0.06
Q2_WIN = 0.045
Q1_SUB_PER_TILE = 2 * 128 * 128
Q2_SUB_PER_TILE = 4 * 128 * 128
Q1_SLOTS = 4 * 8 * NT
Q2_SLOTS = 64 * 16
V_W0, V_W1, V_W2, V_CB, V_A, V_DSK, V_BDT, V_GAM, V_BET = range(9)
ABSM = 0x7FFFFFFF


class K:
    def __init__(self, nc, debug=False):
        self.nc = nc
        self.fw = Fw(nc)
        self.debug = debug
        self._bufs = {}
        self.dbg_idx = 0

    def buf(self, key):
        if key not in self._bufs:
            b = Buf(key)
            b.readers = list(getattr(self, "_barrier_toks", ()))
            self._bufs[key] = b
        return self._bufs[key]

    def carve(self, off_bytes, shape, dtype=F32):
        n = 1
        for s in shape[1:]:
            n *= s
        esz = 2 if dtype == BF16 else 4
        assert off_bytes % 4 == 0
        f0 = off_bytes // 4
        nf = (n * esz + 3) // 4
        ap = self.arena[:, f0:f0 + nf]
        if dtype == BF16:
            ap = ap.bitcast(BF16)[:, :n]
        if len(shape) == 3:
            ap = ap.rearrange("p (a b) -> p a b", a=shape[1])
        return ap

    def ts(self, eng, out, in_, s1, s2, op0, op1=None, ins=(), outs=(),
           deps=(), accum=None):
        if op1 is None and s2 is None and accum is None:
            return self.fw.op(
                eng, lambda h: h.tensor_scalar(out, in_, s1, None, op0),
                ins=ins, outs=outs, deps=deps)
        if op1 is None:
            op1 = ALU.add if accum is not None and s2 is None else op1
        assert op1 is not None, "need op1"
        return self.fw.op(
            eng, lambda h: h.tensor_scalar(out, in_, s1, s2, op0, op1,
                                           accum_out=accum),
            ins=ins, outs=outs, deps=deps)

    def tt(self, eng, out, a, b, op, ins=(), outs=(), deps=()):
        return self.fw.op(eng, lambda h: h.tensor_tensor(out, a, b, op),
                          ins=ins, outs=outs, deps=deps)

    def stt(self, eng, out, in0, scalar, in1, op0, op1, ins=(), outs=(),
            deps=()):
        return self.fw.op(
            eng, lambda h: h.scalar_tensor_tensor(out, in0, scalar, in1,
                                                  op0, op1),
            ins=ins, outs=outs, deps=deps)

    def cp(self, eng, out, in_, ins=(), outs=(), deps=()):
        return self.fw.op(eng, lambda h: h.tensor_copy(out, in_),
                          ins=ins, outs=outs, deps=deps)

    def act(self, out, in_, func, bias=0.0, scale=1.0, ins=(), outs=(),
            deps=(), accum=None):
        return self.fw.op(
            "scalar", lambda h: h.activation(out, in_, func, bias=bias,
                                             scale=scale, accum_out=accum),
            ins=ins, outs=outs, deps=deps)

    def mm(self, out, lhsT, rhs, start, stop, ins=(), outs=(), deps=()):
        return self.fw.op(
            "tensor", lambda h: h.matmul(out, lhsT, rhs, start=start,
                                         stop=stop),
            ins=ins, outs=outs, deps=deps)

    def tr(self, out, in_, ins=(), outs=(), deps=()):
        return self.fw.op(
            "tensor", lambda h: h.transpose(out, in_, self.identb[:]),
            ins=list(ins) + [self.buf("identb")], outs=outs, deps=deps)

    def red(self, eng, out, in_, op, absval=False, ins=(), outs=(), deps=()):
        return self.fw.op(
            eng, lambda h: h.tensor_reduce(out, in_, AX.X, op,
                                           apply_absolute_value=absval),
            ins=ins, outs=outs, deps=deps)

    def mset(self, eng, ap, val, outs=(), deps=()):
        return self.fw.op(eng, lambda h: h.memset(ap, val), outs=outs,
                          deps=deps)

    def rcp(self, out, in_, ins=(), outs=(), deps=()):
        return self.fw.op("vector", lambda h: h.reciprocal(out, in_),
                          ins=ins, outs=outs, deps=deps)

    def max8(self, out, in_, ins=(), outs=(), deps=()):
        return self.fw.op("vector", lambda h: h.max(out, in_),
                          ins=ins, outs=outs, deps=deps)

    def mrep(self, out, repl, in_, ins=(), outs=(), deps=()):
        return self.fw.op(
            "vector", lambda h: h.match_replace(out, repl, in_, 0.0),
            ins=ins, outs=outs, deps=deps)

    def coll(self, kind, op, src_ap, dst_ap, ins=(), outs=()):
        return self.fw.op(
            "gpsimd", lambda g: g.collective_compute(kind, op, RG,
                                                     [src_ap], [dst_ap]),
            ins=ins, outs=outs)

    def declare(self):
        nc = self.nc
        DR = nc.dram_tensor
        self.x_in = DR("x", [NTOK, D], F32, kind="ExternalInput")
        self.y_out = DR("y", [NTOK, D], F32, kind="ExternalOutput")
        specs = [("W_in", [D, 2 * D]), ("W_out", [D, D]), ("W_xp", [D, RK]),
                 ("W_dt", [RK, D]), ("W_Bg", [D, DS]), ("W_Cg", [D, DS]),
                 ("W_Bp", [DS, D]), ("W_Cp", [DS, D]), ("w1", [H, D]),
                 ("w2", [D, H])]
        (self.win_d, self.wout_d, self.wxp_d, self.wdt_d, self.wbg_d,
         self.wcg_d, self.wbp_d, self.wcp_d, self.w1_d, self.w2_d) = [
            DR(n, s, F32, kind="ExternalInput") for n, s in specs]
        self.vecs_d = DR("vecs", [128, DC * 12], F32, kind="ExternalInput")
        self.lngb_d = DR("lngb", [128, 2 * D], F32, kind="ExternalInput")
        self.ident_d = DR("ident", [128, 128], BF16, kind="ExternalInput")
        self.rows_d = DR("rows", [1, 64], F32, kind="ExternalInput")
        dk = "ExternalOutput" if self.debug else "Internal"
        self.x2_sp = DR("x2_sp", [NTOK, D], F32, kind=dk)
        self.hln_sp = DR("hln_sp", [NTOK, D], F32, kind=dk)
        self.g_sp = DR("g_sp", [HC, 128, NTOK], F32,
                       kind=("ExternalOutput" if self.debug else "Internal"))
        self.w1q_sp = DR("w1q_sp", [DC, 128, H], BF16, kind="Internal")
        self.w2q_sp = DR("w2q_sp", [HC, 128, D], BF16, kind="Internal")
        self.cc1_src = DR("cc1_src", [1, 64], F32, kind="Internal")
        self.cc1_dst = DR("cc1_dst", [8, 64], F32, kind="Internal")
        self.cc2_src = DR("cc2_src", [1, 32], F32, kind="Internal")
        self.cc2_dst = DR("cc2_dst", [8, 32], F32, kind="Internal")
        self.cc3_src = DR("cc3_src", [1, 2048], F32, kind="Internal")
        self.cc3_dst = DR("cc3_dst", [8, 2048], F32, kind="Internal")
        self.row_sp = DR("row_sp", [1, 1024], F32, kind="Internal")
        if self.debug:
            self.dbg_d = DR("dbg", [1, 320], F32, kind="ExternalOutput")

        sb = nc.alloc_sbuf_tensor
        self.identb = sb("identb", [128, 128], BF16)
        self.vec = sb("vec", [128, DC, 12], F32)
        self.lng_bc = sb("lng_bc", [128, D], F32)
        self.lnb_bc = sb("lnb_bc", [128, D], F32)
        self.onescol = sb("onescol", [128, 1], F32)
        self.zcol = sb("zcol", [128, 1], F32)
        self.onesrow = sb("onesrow", [1, 128], F32)
        self.eps_t = sb("eps_t", [128, 1], F32)
        self.one_t = sb("one_t", [128, 1], F32)
        self.rows = sb("rows_sb", [1, 64], F32)
        self.iota16 = sb("iota16", [1, 16], F32)
        self.stats = sb("stats", [128, 16], F32)
        self.statB = sb("statB", [128, 16], F32)
        self.statR = sb("statR", [128, 16], F32)
        self.lads = sb("lads", [1, 64], F32)
        self.lrow = sb("lrow", [1, 512], F32)
        self.thr = sb("thr", [128, 2 * NLE], F32)
        self.cnt = sb("cnt", [128, NL], F32)
        self.accs = sb("accs", [128, 32], F32)
        self.qwin = sb("qwin", [128, 4], F32)
        self.hstate = sb("hstate", [128, DC], F32)
        self.xblast = sb("xblast", [128, DC], F32)
        self.s_pp = sb("s_pp", [128, 4], F32)
        self.scp = sb("scp", [128, 4], F32)
        self.candx = sb("candx", [128, 16], F32)
        self.gath = sb("gath", [128, 128], F32)
        self.qsub = sb("qsub", [128, 512], F32)
        self.row1k = sb("row1k", [1, 1024], F32)
        self.i32s = sb("i32s", [1, 2], I32)
        self.dbgrow = sb("dbgrow", [1, 320], F32)
        self.srow = sb("srow", [1, 256], F32)
        self.cnthi = sb("cnthi", [128, NT * 8], F32)

        AW = 45056
        self.arena = sb("arena", [128, AW], F32)
        C = self.carve
        self.winb = C(0, [128, DC, 2 * D], BF16)
        self.woutb = C(32768, [128, DC, D], BF16)
        self.wgateb = C(49152, [128, DC, 128], BF16)
        self.wdtb = C(51200, [128, D], BF16)
        self.wbpb = self.wdtb
        self.wcpb = self.wdtb
        self.xt = [C(56832, [128, 2, D]), C(65024, [128, 2, D])]
        self.wstage = [C(56832, [128, 2 * D]), C(65024, [128, 2 * D])]
        self.xres = C(73216, [128, 2, D])
        self.gsta = C(81408, [128, 2, D])
        self.xn2b = C(89600, [128, 2, D], BF16)
        self.xn2T = C(93696, [128, DC, TT], BF16)
        self.sq_a = C(93696, [128, D])
        self.xb = [C(97792, [128, DC, TT]), C(105984, [128, DC, TT])]
        self.zt = [C(114176, [128, DC, TT]), C(122368, [128, DC, TT])]
        self.xc = C(130560, [128, DC, TT])
        self.xcf = C(130560, [128, DC * TT])
        self.xcr = C(130560, [128, 2, D])
        self.dAf = C(151040, [128, DC * TT])
        self.dBxf = C(159232, [128, DC * TT])
        self.xactb = C(138752, [128, DC, TT], BF16)
        self.espt = C(142848, [128, DC, TT])
        self.espf = C(142848, [128, 2048])
        self.esp2 = C(142848, [128, 2, D])
        self.sq_a3 = C(89600, [128, 2, D])
        self.dA = C(151040, [128, DC, TT])
        self.dBx = C(159232, [128, DC, TT])
        self.gated = C(167424, [128, DC, TT], BF16)
        self.mixsb = C(171520, [128, 2, D])
        self.tsmall = C(171520, [128, 1024], BF16)
        self.w1qT = C(0, [128, DC, H], BF16)
        self.w2qT = C(0, [128, HC, D], BF16)
        self.hlnF = [C(65536, [128, 2, D]), C(73728, [128, 2, D])]
        self.k1tok = [C(81920, [128, 2, D], BF16),
                      C(86016, [128, 2, D], BF16)]
        self.k1Tw = [C(90112, [128, DC, 2 * TT], BF16),
                     C(98304, [128, DC, 2 * TT], BF16)]
        self.gsb = [C(106496, [128, 4, 2 * TT]),
                    C(114688, [128, 4, 2 * TT])]
        self.w2s = [C(122880, [128, D]), C(126976, [128, D])]
        self.w2tb = [C(131072, [128, D], BF16), C(133120, [128, D], BF16)]
        self.w2x = [C(135168, [128, DC, 128], BF16),
                    C(137216, [128, DC, 128], BF16)]
        self.w1s = [C(139264, [128, D]), C(143360, [128, D])]
        self.q2rd = [C(139264, [128, 4, 2 * TT]),
                     C(147456, [128, 4, 2 * TT])]
        self.w1tb = [C(147456, [128, D], BF16), C(149504, [128, D], BF16)]
        self.w1x = [C(151552, [128, DC, 128], BF16),
                    C(153600, [128, DC, 128], BF16)]
        self.selscr = C(155648, [128, 2048])
        self.q1scr = C(106496, [128, 2048])
        self.slots1 = C(53248, [128, Q1_SLOTS])
        self.slots2 = C(167936, [128, Q2_SLOTS])
        self.x2F = [C(65536, [128, 2, D]), C(73728, [128, 2, D])]
        self.outsb = [C(81920, [128, 2, D]), C(90112, [128, 2, D])]
        self.gF = [C(98304, [128, 4, TT]), C(102400, [128, 4, TT])]
        self.k2 = [C(106496, [128, 4, TT], BF16),
                   C(108544, [128, 4, TT], BF16)]
        assert AW * 4 >= 180224, AW * 4
        self.pb = [nc.alloc_psum_tensor(f"pb{i}", [128, 512], F32)
                   for i in range(6)]
        self.ptr = [nc.alloc_psum_tensor(f"ptr{i}", [128, 1024], BF16)
                    for i in range(2)]
        self.sbuf_left = nc.sbuf_bytes_remaining

    def lad(self, j):
        return self.lads[0:1, j:j + 1]

    def dbg_mark(self, k):
        if not self.debug or self.dbg_idx >= 319:
            return
        i = self.dbg_idx
        self.dbg_idx += 1
        self.mset("vector", self.dbgrow[0:1, i:i + 1], float(77000 + k),
                  outs=[self.buf("dbgrow")])

    def dbg(self, src11, n=1):
        if not self.debug:
            return
        i = self.dbg_idx
        if i + n > 320:
            return
        self.dbg_idx += n
        self.cp("vector", self.dbgrow[0:1, i:i + n], src11,
                ins=[self.buf("lads"), self.buf("srow"), self.buf("cnt")],
                outs=[self.buf("dbgrow")])

    def phase_barrier(self, arena_keys):
        toks = self.fw.barrier_tokens()
        for k in arena_keys:
            self.buf(k).readers = list(self.buf(k).readers) + toks

    def bc_pp(self, src11, dst_ap, dst_key):
        p = self.pb[5]
        self.mm(p[:, 0:1], self.onesrow[:], src11, True, True,
                ins=[self.buf("onesrow"), self.buf("lads")],
                outs=[self.buf("pb5")])
        self.cp("vector", dst_ap, p[:, 0:1], ins=[self.buf("pb5")],
                outs=[self.buf(dst_key)])

    def bc_row(self, src_row, k, dst_ap, dst_key, src_key):
        p = self.pb[5]
        self.mm(p[:, 0:k], self.onesrow[:], src_row, True, True,
                ins=[self.buf("onesrow"), self.buf(src_key)],
                outs=[self.buf("pb5")])
        self.cp("vector", dst_ap, p[:, 0:k], ins=[self.buf("pb5")],
                outs=[self.buf(dst_key)])

    def col_total(self, cols_ap, k, out_row, src_key, out_key="lads"):
        p = self.pb[5]
        self.mm(p[0:1, 0:k], self.onescol[:], cols_ap, True, True,
                ins=[self.buf("onescol"), self.buf(src_key)],
                outs=[self.buf("pb5")])
        self.cp("vector", out_row, p[0:1, 0:k], ins=[self.buf("pb5")],
                outs=[self.buf(out_key)])

    def part_max_to_scalar(self, col_ap, out11, src_key):
        fw, buf = self.fw, self.buf
        self._bounce_n = getattr(self, "_bounce_n", 0) + 1
        fw.dma("gpsimd",
               self.row_sp.ap()[0:1, 0:128].rearrange("a (p s) -> (a p) s",
                                                      p=128),
               col_ap, ins=[buf(src_key), buf("row_sp")],
               outs=[buf("row_sp")], sem=f"bounce{self._bounce_n}")
        fw.dma("sync", self.srow[0:1, 0:128], self.row_sp.ap()[0:1, 0:128],
               ins=[buf("row_sp")], outs=[buf("srow")],
               sem=f"bounce{self._bounce_n}b")
        self.red("vector", out11, self.srow[0:1, 0:128], ALU.max,
                 ins=[buf("srow")], outs=[buf("lads")])

    def init_phase(self):
        fw, buf = self.fw, self.buf
        dma = fw.dma
        dma("sync", self.identb[:], self.ident_d.ap(), outs=[buf("identb")],
            sem="l_i")
        dma("sync", self.vec[:],
            self.vecs_d.ap().rearrange("p (c n) -> p c n", c=DC),
            outs=[buf("vec")], sem="l_v")
        dma("sync", self.lng_bc[:], self.lngb_d.ap()[:, 0:D],
            outs=[buf("lng_bc")], sem="l_g")
        dma("sync", self.lnb_bc[:], self.lngb_d.ap()[:, D:2 * D],
            outs=[buf("lnb_bc")], sem="l_b")
        dma("sync", self.rows[:], self.rows_d.ap(), outs=[buf("rows_sb")],
            sem="l_r")
        for t, v in [("onescol", 1.0), ("onesrow", 1.0), ("eps_t", EPS),
                     ("one_t", 1.0)]:
            self.mset("vector", getattr(self, t)[:], v, outs=[buf(t)])
        for t in ("cnt", "cnthi", "stats", "lads", "accs", "hstate", "zcol",
                  "xblast", "qwin", "s_pp", "scp", "candx", "gath", "qsub",
                  "srow", "lrow", "thr", "dbgrow"):
            self.mset("vector", getattr(self, t)[:], 0.0, outs=[buf(t)])
        self.fw.op("gpsimd", lambda g: g.iota(
            self.iota16[:], [[1, 16]], base=0, channel_multiplier=0,
            allow_small_or_imprecise_dtypes=True), outs=[buf("iota16")])

        self._ldn = 0

        def ldcast(dst_ap, dst_key, src_ap, rows, cols, p0=0):
            w = self._ldn % 2
            self._ldn += 1
            st = self.wstage[w]
            dma("sync", st[p0:p0 + rows, 0:cols], src_ap,
                ins=[buf(f"wstage{w}")], outs=[buf(f"wstage{w}")],
                sem=f"l_w{w}")
            self.cp("vector", dst_ap, st[p0:p0 + rows, 0:cols],
                    ins=[buf(f"wstage{w}")], outs=[buf(dst_key)])

        self.mset("vector", self.wgateb[:, :, :], 0.0,
                  outs=[buf("wgateb")])
        for c in range(DC):
            r = slice(c * 128, c * 128 + 128)
            ldcast(self.winb[:, c, :], "winb", self.win_d.ap()[r, :], 128,
                   2 * D)
            ldcast(self.woutb[:, c, :], "woutb", self.wout_d.ap()[r, :],
                   128, D)
            ldcast(self.wgateb[:, c, 64:128], "wgateb",
                   self.wxp_d.ap()[r, :], 128, RK)
            ldcast(self.wgateb[:, c, 0:DS], "wgateb",
                   self.wbg_d.ap()[r, :], 128, DS)
            ldcast(self.wgateb[:, c, 32:32 + DS], "wgateb",
                   self.wcg_d.ap()[r, :], 128, DS)
        ldcast(self.wdtb[64:128, :], "wdtb", self.wdt_d.ap(), 64, D, p0=64)
        ldcast(self.wbpb[0:16, :], "wbpb", self.wbp_d.ap(), 16, D)
        ldcast(self.wcpb[32:48, :], "wcpb", self.wcp_d.ap(), 16, D, p0=32)

        self.mset("vector", self.accs[:, 8:10], 0.0, outs=[buf("accs")])
        for c in range(HC):
            w = c % 2
            st = self.wstage[w]
            dma("sync", st[:, 0:D], self.w1_d.ap()[c * 128:(c + 1) * 128, :],
                ins=[buf(f"wstage{w}")], outs=[buf(f"wstage{w}")],
                sem=f"l_w1{w}")
            self.red("vector", self.accs[:, w:w + 1], st[:, 0:D], ALU.add,
                     absval=True, ins=[buf(f"wstage{w}")],
                     outs=[buf("accs")])
            self.tt("vector", self.accs[:, 8:9], self.accs[:, 8:9],
                    self.accs[:, w:w + 1], ALU.add, ins=[buf("accs")],
                    outs=[buf("accs")])
        for c in range(DC):
            for q in range(2):
                w = q % 2
                st = self.wstage[w]
                dma("sync", st[:, 0:2 * D],
                    self.w2_d.ap()[c * 128:(c + 1) * 128,
                                   q * 2 * D:(q + 1) * 2 * D],
                    ins=[buf(f"wstage{w}")], outs=[buf(f"wstage{w}")],
                    sem=f"l_w2{w}")
                self.red("vector", self.accs[:, w:w + 1], st[:, 0:2 * D],
                         ALU.add, absval=True, ins=[buf(f"wstage{w}")],
                         outs=[buf("accs")])
                self.tt("vector", self.accs[:, 9:10], self.accs[:, 9:10],
                        self.accs[:, w:w + 1], ALU.add, ins=[buf("accs")],
                        outs=[buf("accs")])
        self.col_total(self.accs[:, 8:10], 2, self.lads[0:1, 4:6], "accs")
        self.ts("vector", self.lad(0), self.lads[0:1, 4:5],
                float(1.0 / (H * D)), EPS, ALU.mult, ALU.add,
                ins=[buf("lads")], outs=[buf("lads")])
        self.ts("vector", self.lad(2), self.lads[0:1, 5:6],
                float(1.0 / (H * D)), EPS, ALU.mult, ALU.add,
                ins=[buf("lads")], outs=[buf("lads")])
        self.rcp(self.lad(1), self.lad(0), ins=[buf("lads")],
                 outs=[buf("lads")])
        self.rcp(self.lad(3), self.lad(2), ins=[buf("lads")],
                 outs=[buf("lads")])
        self.bc_pp(self.lad(1), self.scp[:, 0:1], "scp")
        self.bc_pp(self.lad(3), self.scp[:, 1:2], "scp")
        self.phase_barrier(["xt0", "xt1"])

    def ln_stats2(self, v3, bv, scr, bscrs, st, bst):
        for tc in range(2):
            self.red("vector", st[:, tc:tc + 1], v3[:, tc, :], ALU.add,
                     ins=[bv], outs=[bst])
            self.act(scr[:, tc, :], v3[:, tc, :], AF.Square, ins=[bv],
                     outs=list(bscrs) + [bst], accum=st[:, 2 + tc:3 + tc])
        self.ts("vector", st[:, 4:6], st[:, 0:2], INV_D, None, ALU.mult,
                ins=[bst], outs=[bst])
        self.ts("vector", st[:, 6:8], st[:, 2:4], INV_D, None, ALU.mult,
                ins=[bst], outs=[bst])
        self.tt("vector", st[:, 8:10], st[:, 4:6], st[:, 4:6], ALU.mult,
                ins=[bst], outs=[bst])
        self.tt("vector", st[:, 6:8], st[:, 6:8], st[:, 8:10], ALU.subtract,
                ins=[bst], outs=[bst])
        self.act(st[:, 8:10], st[:, 6:8], AF.Ln, bias=self.eps_t[:, 0:1],
                 ins=[bst, self.buf("eps_t")], outs=[bst])
        self.act(st[:, 10:12], st[:, 8:10], AF.Exp, scale=-0.5, ins=[bst],
                 outs=[bst])

    def ln_stats(self, v, bv, tc, scr, bscrs, st=None, bst=None):
        if st is None:
            st, bst = self.stats, self.buf("stats")
        self.red("vector", st[:, tc:tc + 1], v, ALU.add, ins=[bv],
                 outs=[bst])
        self.act(scr, v, AF.Square, ins=[bv], outs=list(bscrs) + [bst],
                 accum=st[:, 2 + tc:3 + tc])
        self.ts("vector", st[:, 4 + tc:5 + tc], st[:, tc:tc + 1], INV_D,
                None, ALU.mult, ins=[bst], outs=[bst])
        self.ts("vector", st[:, 6 + tc:7 + tc], st[:, 2 + tc:3 + tc], INV_D,
                None, ALU.mult, ins=[bst], outs=[bst])
        self.tt("vector", st[:, 8 + tc:9 + tc], st[:, 4 + tc:5 + tc],
                st[:, 4 + tc:5 + tc], ALU.mult, ins=[bst], outs=[bst])
        self.tt("vector", st[:, 6 + tc:7 + tc], st[:, 6 + tc:7 + tc],
                st[:, 8 + tc:9 + tc], ALU.subtract, ins=[bst], outs=[bst])
        self.act(st[:, 8 + tc:9 + tc], st[:, 6 + tc:7 + tc], AF.Ln,
                 bias=self.eps_t[:, 0:1], ins=[bst, self.buf("eps_t")],
                 outs=[bst])
        self.act(st[:, 10 + tc:11 + tc], st[:, 8 + tc:9 + tc], AF.Exp,
                 scale=-0.5, ins=[bst], outs=[bst])

    def stage_a(self, i):
        fw, buf = self.fw, self.buf
        ii = i % 2
        xt, bxt = self.xt[ii], buf(f"xt{ii}")
        r0 = i * TT
        fw.dma("sync", xt[:],
               self.x_in.ap()[r0:r0 + TT, :].rearrange(
                   "(tc p) d -> p tc d", p=128),
               ins=[bxt], outs=[bxt], sem=f"l_x{ii}")
        st, bst = self.stats, buf("stats")
        self.ln_stats2(xt, bxt, self.sq_a3, [buf("xn2b"), buf("xn2a")],
                       st, bst)
        r4 = (i % 4) * 4
        self.cp("vector", self.statR[:, r4:r4 + 2], st[:, 4:6],
                ins=[bst], outs=[buf("statR")])
        self.cp("vector", self.statR[:, r4 + 2:r4 + 4], st[:, 10:12],
                ins=[bst], outs=[buf("statR")])
        for tc in range(2):
            self.ts("vector", self.xn2b[:, tc, :], xt[:, tc, :],
                    st[:, 4 + tc:5 + tc],
                    st[:, 10 + tc:11 + tc], ALU.subtract, ALU.mult,
                    ins=[bxt, bst], outs=[buf("xn2b")])
        for tc in range(2):
            for dc in range(DC):
                pt = self.ptr[dc % 2]
                bpt = buf(f"ptr{dc % 2}")
                self.tr(pt[:, 0:128],
                        self.xn2b[:, tc, dc * 128:(dc + 1) * 128],
                        ins=[buf("xn2b")], outs=[bpt])
                self.ts("vector", self.xn2T[:, dc, tc * 128:(tc + 1) * 128],
                        pt[:, 0:128], self.vec[:, dc, V_GAM:V_GAM + 1],
                        self.vec[:, dc, V_BET:V_BET + 1], ALU.mult, ALU.add,
                        ins=[bpt, buf("vec")], outs=[buf("xn2a")])
    def stage_a_mm(self, i):
        buf = self.buf
        ii = i % 2
        xb, z = self.xb[ii], self.zt[ii]
        bxb, bz = buf(f"xb{ii}"), buf(f"zt{ii}")
        for q in range(16):
            pbk, bp = self.pb[q % 2], buf(f"pb{q % 2}")
            for k in range(DC):
                self.mm(pbk[:, 0:TT], self.winb[:, k, q * 128:(q + 1) * 128],
                        self.xn2T[:, k, :], k == 0, k == DC - 1,
                        ins=[buf("winb"), buf("xn2a")], outs=[bp])
            dst = xb[:, q, :] if q < 8 else z[:, q - 8, :]
            dk = bxb if q < 8 else bz
            self.act(dst, pbk[:, 0:TT], AF.Copy, ins=[bp], outs=[dk])

    def stage_b_front_a(self, i):
        fw, buf = self.fw, self.buf
        ii = i % 2
        seqpos = i % 8
        xb, z = self.xb[ii], self.zt[ii]
        bxb, bz = buf(f"xb{ii}"), buf(f"zt{ii}")
        vec, bvec = self.vec, buf("vec")
        xc, bxc = self.xc, buf("xc")
        xbf = xb.rearrange("p a b -> p (a b)")
        NW = DC * TT

        def bc(col):
            return vec[:, :, col:col + 1].broadcast_to([128, DC, TT])

        bdA, bdBx = buf("dA"), buf("dBx")
        self.tt("vector", xc[:, :, :], xb[:, :, :], bc(V_W1), ALU.mult,
                ins=[bxb, bvec], outs=[bxc])
        self.tt("vector", xc[:, :, :], xc[:, :, :], bc(V_CB), ALU.add,
                ins=[bvec, bxc], outs=[bxc])
        self.tt("gpsimd", self.dA[:, :, :], xb[:, :, :], bc(V_W0), ALU.mult,
                ins=[bxb, bvec], outs=[bdA])
        self.tt("vector", self.xcf[:, 1:NW], self.xcf[:, 1:NW],
                self.dAf[:, 0:NW - 1], ALU.add, ins=[bdA, bxc], outs=[bxc])
        self.tt("gpsimd", self.dBx[:, :, :], xb[:, :, :], bc(V_W2), ALU.mult,
                ins=[bxb, bvec], outs=[bdBx])
        self.tt("vector", self.xcf[:, 0:NW - 1], self.xcf[:, 0:NW - 1],
                self.dBxf[:, 1:NW], ALU.add, ins=[bdBx, bxc], outs=[bxc])
        bac = buf("accs")
        self.tt("vector", self.accs[:, 16:23],
                xb[:, 0:7, TT - 1:TT].rearrange("p c o -> p (c o)"),
                vec[:, 1:8, V_W0], ALU.mult, ins=[bxb, bvec], outs=[bac])
        self.tt("vector", xc[:, 1:8, 0:1], xc[:, 1:8, 0:1],
                self.accs[:, 16:23].rearrange("p (c o) -> p c o", o=1),
                ALU.subtract, ins=[bac, bxc], outs=[bxc])
        self.tt("vector", self.accs[:, 16:23],
                xb[:, 1:8, 0:1].rearrange("p c o -> p (c o)"),
                vec[:, 0:7, V_W2], ALU.mult, ins=[bxb, bvec], outs=[bac])
        self.tt("vector", xc[:, 0:7, TT - 1:TT], xc[:, 0:7, TT - 1:TT],
                self.accs[:, 16:23].rearrange("p (c o) -> p c o", o=1),
                ALU.subtract, ins=[bac, bxc], outs=[bxc])
        if seqpos > 0:
            self.tt("vector", self.accs[:, 16:24], self.xblast[:],
                    vec[:, :, V_W0], ALU.mult, ins=[buf("xblast"), bvec],
                    outs=[bac])
            self.tt("vector", xc[:, :, 0:1], xc[:, :, 0:1],
                    self.accs[:, 16:24].rearrange("p (c o) -> p c o", o=1),
                    ALU.add, ins=[bac, bxc], outs=[bxc])
        self.cp("vector", self.xblast[:],
                xb[:, :, TT - 1:TT].rearrange("p c o -> p (c o)"),
                ins=[bxb], outs=[buf("xblast")])

    def stage_b_front_a2(self, i):
        fw, buf = self.fw, self.buf
        ii = i % 2
        seqpos = i % 8
        xb, z = self.xb[ii], self.zt[ii]
        bxb, bz = buf(f"xb{ii}"), buf(f"zt{ii}")
        vec, bvec = self.vec, buf("vec")
        xc, bxc = self.xc, buf("xc")
        bac = buf("accs")
        if seqpos < 7:
            nxt = self.xb[(i + 1) % 2]
            bnxt = buf(f"xb{(i + 1) % 2}")
            self.tt("vector", self.accs[:, 16:24],
                    nxt[:, :, 0:1].rearrange("p c o -> p (c o)"),
                    vec[:, :, V_W2], ALU.mult, ins=[bnxt, bvec], outs=[bac])
            self.tt("vector", xc[:, :, TT - 1:TT], xc[:, :, TT - 1:TT],
                    self.accs[:, 16:24].rearrange("p (c o) -> p c o", o=1),
                    ALU.add, ins=[bac, bxc], outs=[bxc])
        self.act(xc[:, :, :], xc[:, :, :], AF.Silu, ins=[bxc], outs=[bxc])
        self.act(z[:, :, :], z[:, :, :], AF.Silu, ins=[bz], outs=[bz])
        self.cp("vector", self.xactb[:, :, :], xc[:, :, :], ins=[bxc],
                outs=[buf("xactb")])

    def stage_b_front_b(self, i):
        fw, buf = self.fw, self.buf
        ii = i % 2
        seqpos = i % 8
        xb, z = self.xb[ii], self.zt[ii]
        bxb, bz = buf(f"xb{ii}"), buf(f"zt{ii}")
        vec, bvec = self.vec, buf("vec")
        xc, bxc = self.xc, buf("xc")
        bxab = buf("xactb")
        p2, bp2 = self.pb[2], buf("pb2a")
        btm = buf("mixsb")
        t1b = self.tsmall[64:128, 0:TT]
        tBb = self.tsmall[0:16, 0:TT]
        tCb = self.tsmall[32:48, 0:TT]
        for k in range(DC):
            self.mm(p2[0:128, 0:TT], self.wgateb[:, k, :],
                    self.xactb[:, k, :], k == 0, k == DC - 1,
                    ins=[buf("wgateb"), bxab], outs=[bp2])
        self.cp("vector", self.tsmall[0:128, 0:TT], p2[0:128, 0:TT],
                ins=[bp2], outs=[btm])
        esp, besp = self.espt, buf("espt")
        for dc in range(DC):
            p3, bp3 = self.pb[3], buf("pb3a")
            self.mm(p3[:, 0:TT], self.wdtb[64:128, dc * 128:(dc + 1) * 128],
                    t1b, True, True, ins=[buf("wdtb"), btm], outs=[bp3])
            self.act(esp[:, dc, :], p3[:, 0:TT], AF.Exp,
                     bias=vec[:, dc, V_BDT:V_BDT + 1], ins=[bp3, bvec],
                     outs=[besp])
        self.act(esp[:, :, :], esp[:, :, :], AF.Ln,
                 bias=self.one_t[:, 0:1], ins=[besp, buf("one_t")],
                 outs=[besp])
        dA, bdA = self.dA, buf("dA")
        self.tt("vector", dA[:, :, :], esp[:, :, :],
                vec[:, :, V_A:V_A + 1].broadcast_to([128, DC, TT]),
                ALU.mult, ins=[besp, bvec], outs=[bdA])
        self.act(dA[:, :, :], dA[:, :, :], AF.Exp, ins=[bdA], outs=[bdA])
        dBx, bdBx = self.dBx, buf("dBx")
        if seqpos == 0:
            self.mset("vector", self.hstate[:], 0.0, outs=[buf("hstate")])
        for dc in range(DC):
            p4, bp4 = self.pb[3], buf("pb3b")
            self.mm(p4[:, TT:2 * TT],
                    self.wbpb[0:16, dc * 128:(dc + 1) * 128],
                    tBb, True, True, ins=[buf("wbpb"), btm], outs=[bp4])
            self.tt("vector", dBx[:, dc, :], p4[:, TT:2 * TT],
                    esp[:, dc, :], ALU.mult, ins=[bp4, besp], outs=[bdBx])
        self.tt("gpsimd", dBx[:, :, :], dBx[:, :, :], xc[:, :, :],
                ALU.mult, ins=[bdBx, bxc], outs=[bdBx])
        bac = buf("accs")
        dA0 = dA[:, :, 0:1].rearrange("p c o -> p (c o)")
        dBx0 = dBx[:, :, 0:1].rearrange("p c o -> p (c o)")
        self.tt("vector", self.accs[:, 24:32], dA0, self.hstate[:],
                ALU.mult, ins=[bdA, buf("hstate")], outs=[bac])
        self.tt("vector", dBx0, dBx0, self.accs[:, 24:32], ALU.add,
                ins=[bac, bdBx], outs=[bdBx])
        self.ts("vector", dA0, dA0, 0.0, None, ALU.mult, ins=[bdA],
                outs=[bdA])
        self.fw.op("vector", lambda v: v.tensor_tensor_scan(
            self.dBxf[:, :], self.dAf[:, :], self.dBxf[:, :],
            self.zcol[:, 0:1], ALU.mult, ALU.add),
            ins=[bdA, bdBx, buf("zcol")], outs=[bdBx])
        self.cp("vector", self.hstate[:],
                dBx[:, :, TT - 1:TT].rearrange("p c o -> p (c o)"),
                ins=[bdBx], outs=[buf("hstate")])
        for dc in range(DC):
            p5, bp5 = self.pb[2], buf("pb2b")
            self.mm(p5[:, TT:2 * TT],
                    self.wcpb[32:48, dc * 128:(dc + 1) * 128],
                    tCb, True, True, ins=[buf("wcpb"), btm], outs=[bp5])
            self.tt("vector", dA[:, dc, :], dBx[:, dc, :], p5[:, TT:2 * TT],
                    ALU.mult, ins=[bdBx, bp5], outs=[bdA])
        self.tt("gpsimd", esp[:, :, :], xc[:, :, :],
                vec[:, :, V_DSK:V_DSK + 1].broadcast_to([128, DC, TT]),
                ALU.mult, ins=[bxc, bvec], outs=[besp])
        self.tt("vector", dA[:, :, :], dA[:, :, :], esp[:, :, :], ALU.add,
                ins=[besp, bdA], outs=[bdA])
        self.tt("vector", self.gated[:, :, :], dA[:, :, :], z[:, :, :],
                ALU.mult, ins=[bdA, bz], outs=[buf("gated")])

    def stage_b_tail(self, i):
        fw, buf = self.fw, self.buf
        ii = i % 2
        xt, bxt = self.xres, buf("xres")
        fw.dma("sync", xt[:],
               self.x_in.ap()[i * TT:i * TT + TT, :].rearrange(
                   "(tc p) d -> p tc d", p=128),
               ins=[bxt], outs=[bxt], sem=f"l_xr{i % 2}")
        mix, bmix = self.mixsb, buf("mixsb")
        r4 = (i % 4) * 4
        sR, bsR = self.statR, buf("statR")
        for tc in range(2):
            self.ts("vector", mix[:, tc, :], xt[:, tc, :],
                    sR[:, r4 + tc:r4 + tc + 1],
                    sR[:, r4 + 2 + tc:r4 + 3 + tc],
                    ALU.subtract, ALU.mult, ins=[bxt, bsR], outs=[bmix])
        for tc in range(2):
            for nq in range(2):
                pbk, bp = self.pb[4 + nq], buf(f"pb{4 + nq}")
                for k in range(DC):
                    self.mm(pbk[:, 0:512],
                            self.gated[:, k, tc * 128:(tc + 1) * 128],
                            self.woutb[:, k, nq * 512:(nq + 1) * 512],
                            k == 0, k == DC - 1,
                            ins=[buf("gated"), buf("woutb")], outs=[bp])
                self.tt("vector", mix[:, tc, nq * 512:(nq + 1) * 512],
                        mix[:, tc, nq * 512:(nq + 1) * 512],
                        pbk[:, 0:512], ALU.add, ins=[bp], outs=[bmix])
        st, bst = self.statB, buf("statB")
        hl = self.esp2
        bhl = buf("espt")
        self.ln_stats2(mix, bmix, self.esp2, [buf("espt")], st, bst)
        for tc in range(2):
            self.ts("vector", mix[:, tc, :], mix[:, tc, :],
                    st[:, 4 + tc:5 + tc], st[:, 10 + tc:11 + tc],
                    ALU.subtract, ALU.mult, ins=[bmix, bst], outs=[bmix])
            self.tt("vector", mix[:, tc, :], mix[:, tc, :], xt[:, tc, :],
                    ALU.add, ins=[bmix, bxt], outs=[bmix])
        r0 = i * TT
        fw.dma("gpsimd",
               self.x2_sp.ap()[r0:r0 + TT, :].rearrange(
                   "(tc p) d -> p tc d", p=128),
               mix[:, :, :], ins=[bmix, buf("x2_sp")], outs=[buf("x2_sp")],
               sem=f"s_x2{i % 2}")
        self.ln_stats2(mix, bmix, self.esp2, [buf("espt")], st, bst)
        for tc in range(2):
            self.ts("vector", hl[:, tc, :], mix[:, tc, :],
                    st[:, 4 + tc:5 + tc], st[:, 10 + tc:11 + tc],
                    ALU.subtract, ALU.mult, ins=[bmix, bst], outs=[bhl])
            self.tt("vector", hl[:, tc, :], hl[:, tc, :], self.lng_bc[:],
                    ALU.mult, ins=[bhl, buf("lng_bc")], outs=[bhl])
            self.tt("vector", hl[:, tc, :], hl[:, tc, :], self.lnb_bc[:],
                    ALU.add, ins=[bhl, buf("lnb_bc")], outs=[bhl])
        fw.dma("gpsimd",
               self.hln_sp.ap()[r0:r0 + TT, :].rearrange(
                   "(tc p) d -> p tc d", p=128),
               hl[:, :, :], ins=[bhl, buf("hln_sp")], outs=[buf("hln_sp")],
               sem=f"s_hln{i % 2}")
        if i <= 7:
            self.ladder_tile(hl[:, :, :].rearrange("p a b -> p (a b)"),
                             bhl, i, need_abs=True, subs=32, nq=1,
                             shift=(i % 8))
        else:
            self.capture_q1_sb(
                hl[:, :, :].rearrange("p a b -> p (a b)"), bhl, i)
            rb, rk = self.xres, buf("xres")
            fw.dma("sync", rb[:],
                   self.hln_sp.ap()[(i - 8) * TT:(i - 7) * TT, :].rearrange(
                       "(tc p) d -> p tc d", p=128),
                   ins=[buf("hln_sp"), rk], outs=[rk], sem=f"l_hr{i % 2}")
            self.capture_q1_sb(
                rb.rearrange("p a b -> p (a b)"), rk, i - 8)

    def capture_q1_sb(self, flat, bflat, idx):
        buf = self.buf
        width = 2048
        bdA, bdBx = buf("dA"), buf("dBx")
        self.ts("vector", self.dBxf.bitcast(I32)[:, 0:width],
                flat.bitcast(I32), ABSM, None, ALU.bitwise_and,
                ins=[bflat], outs=[bdBx])
        data = self.dBxf[:, 0:width]
        self.ts("vector", self.dAf[:, 0:width], data, self.qwin[:, 1:2],
                None, ALU.is_gt, ins=[bdBx, buf("qwin")],
                outs=[bdA, buf("accs")], accum=self.accs[:, 30:31])
        self.tt("vector", self.cnthi[:, idx:idx + 1],
                self.cnthi[:, idx:idx + 1], self.accs[:, 30:31], ALU.add,
                ins=[buf("accs")], outs=[buf("cnthi")])
        self.ts("vector", self.dAf[:, 0:width], data, self.qwin[:, 1:2],
                None, ALU.is_le, ins=[bdBx, buf("qwin")], outs=[bdA])
        self.tt("gpsimd", self.dAf[:, 0:width], self.dAf[:, 0:width], data,
                ALU.mult, ins=[bdBx, bdA], outs=[bdA])
        for s in range(4):
            self.max8(
                self.slots1[:, idx * 32 + s * 8:idx * 32 + (s + 1) * 8],
                self.dAf[:, s * 512:(s + 1) * 512],
                ins=[bdA], outs=[buf("slots1c")])

    def ladder_tile(self, flat, bflat, i, need_abs, subs, nq, width=2048,
                    shift=0):
        buf = self.buf
        eng = "vector"
        if i == 0:
            self.red("vector", self.accs[:, 25:26], flat, ALU.max,
                     absval=True, ins=[bflat], outs=[buf("accs")])
            self.tt("vector", self.accs[:, 24:25], self.accs[:, 24:25],
                    self.accs[:, 25:26], ALU.max, ins=[buf("accs")],
                    outs=[buf("accs")])
            return
        nsub = width // subs
        sub = flat[:, shift:shift + width - subs + 1:subs]
        if need_abs:
            self.ts(eng, self.qsub[:, 0:nsub].bitcast(I32),
                    sub.bitcast(I32), ABSM, None, ALU.bitwise_and,
                    ins=[bflat], outs=[buf("qsub")])
            data, bdata = self.qsub[:, 0:nsub], buf("qsub")

        else:
            data, bdata = sub, bflat
        for j in range(NL):
            self.ts(eng, self.qsub[:, 256:256 + nsub], data,
                    self.thr[:, j:j + 1], None, ALU.is_gt,
                    ins=[bdata, buf("thr")],
                    outs=[buf("qsubhi"), buf("accs")],
                    accum=self.accs[:, j:j + 1])
        self.tt(eng, self.cnt[:, 0:NL], self.cnt[:, 0:NL],
                self.accs[:, 0:NL], ALU.add, ins=[buf("accs")],
                outs=[buf("cnt")])

    def ladder_setup_gen1(self, nq):
        buf = self.buf
        self.part_max_to_scalar(self.accs[:, 24:25], self.lad(8), "accs")
        geo = self.rows[0:1, 0:NL]
        self.ts("vector", self.srow[0:1, 0:NL], geo, self.lad(8), None,
                ALU.mult, ins=[buf("rows_sb"), buf("lads")],
                outs=[buf("srow")])
        self.bc_row(self.srow[0:1, 0:NL], NL, self.thr[:, 0:NL], "thr",
                    "srow")
        self.mset("vector", self.cnt[:], 0.0, outs=[buf("cnt")])
        self.mset("vector", self.accs[:, 0:NL], 0.0, outs=[buf("accs")])

    def interp_est(self, rstar, out11):
        buf = self.buf
        L = self.lrow
        bL = buf("lrow")
        self.col_total(self.cnt[:, 0:NL], NL, self.lads[0:1, 32:32 + NL],
                       "cnt")
        ct = self.lads[0:1, 32:32 + NL]
        self.dbg(self.srow[0:1, 0:NL], NL)
        self.dbg(ct, NL)
        bl = buf("lads")
        self.ts("vector", L[0:1, 0:NL], ct, float(rstar), None, ALU.is_ge,
                ins=[bl], outs=[bL])
        self.cp("vector", L[0:1, 32:33], L[0:1, 0:1], ins=[bL], outs=[bL])
        self.ts("vector", L[0:1, 64:64 + NL - 1], L[0:1, 0:NL - 1], -1.0,
                1.0, ALU.mult, ALU.add, ins=[bL], outs=[bL])
        self.tt("vector", L[0:1, 33:33 + NL - 1], L[0:1, 1:NL],
                L[0:1, 64:64 + NL - 1], ALU.mult, ins=[bL], outs=[bL])
        sel = L[0:1, 32:32 + NL]
        self.tt("vector", L[0:1, 64:64 + NL], sel, self.srow[0:1, 0:NL],
                ALU.mult, ins=[bL, buf("srow")], outs=[bL])
        self.red("vector", L[0:1, 96:97], L[0:1, 64:64 + NL], ALU.add,
                 ins=[bL], outs=[bL])
        self.tt("vector", L[0:1, 64:64 + NL], sel, ct, ALU.mult,
                ins=[bL, bl], outs=[bL])
        self.red("vector", L[0:1, 97:98], L[0:1, 64:64 + NL], ALU.add,
                 ins=[bL], outs=[bL])
        self.tt("vector", L[0:1, 64:64 + NL - 1], L[0:1, 33:33 + NL - 1],
                self.srow[0:1, 0:NL - 1], ALU.mult, ins=[bL, buf("srow")],
                outs=[bL])
        self.red("vector", L[0:1, 98:99], L[0:1, 64:64 + NL - 1], ALU.add,
                 ins=[bL], outs=[bL])
        self.tt("vector", L[0:1, 64:64 + NL - 1], L[0:1, 33:33 + NL - 1],
                self.lads[0:1, 32:32 + NL - 1], ALU.mult, ins=[bL, bl],
                outs=[bL])
        self.red("vector", L[0:1, 99:100], L[0:1, 64:64 + NL - 1], ALU.add,
                 ins=[bL], outs=[bL])
        self.ts("vector", L[0:1, 100:101], L[0:1, 97:98], float(rstar),
                None, ALU.subtract, ins=[bL], outs=[bL])
        self.tt("vector", L[0:1, 101:102], L[0:1, 97:98], L[0:1, 99:100],
                ALU.subtract, ins=[bL], outs=[bL])
        self.ts("vector", L[0:1, 101:102], L[0:1, 101:102], 1e-6, None,
                ALU.add, ins=[bL], outs=[bL])
        self.rcp(L[0:1, 101:102], L[0:1, 101:102], ins=[bL], outs=[bL])
        self.tt("vector", L[0:1, 100:101], L[0:1, 100:101], L[0:1, 101:102],
                ALU.mult, ins=[bL], outs=[bL])
        self.tt("vector", L[0:1, 98:99], L[0:1, 98:99], L[0:1, 96:97],
                ALU.subtract, ins=[bL], outs=[bL])
        self.tt("vector", L[0:1, 98:99], L[0:1, 98:99], L[0:1, 100:101],
                ALU.mult, ins=[bL], outs=[bL])
        self.tt("vector", L[0:1, 96:97], L[0:1, 96:97], L[0:1, 98:99],
                ALU.add, ins=[bL], outs=[bL])
        self.red("vector", L[0:1, 102:103], L[0:1, 0:NL], ALU.max,
                 ins=[bL], outs=[bL])
        self.tt("vector", L[0:1, 96:97], L[0:1, 96:97], L[0:1, 102:103],
                ALU.mult, ins=[bL], outs=[bL])
        self.ts("vector", L[0:1, 103:104], L[0:1, 102:103], -1.0, 1.0,
                ALU.mult, ALU.add, ins=[bL], outs=[bL])
        self.tt("vector", L[0:1, 103:104], L[0:1, 103:104],
                self.srow[0:1, NL - 1:NL], ALU.mult, ins=[bL, buf("srow")],
                outs=[bL])
        self.tt("vector", out11, L[0:1, 96:97], L[0:1, 103:104], ALU.add,
                ins=[bL], outs=[buf("lads")])

    def ladder_setup_gen2(self, est11, nq):
        buf = self.buf
        lin = self.rows[0:1, NL:2 * NL]
        self.ts("vector", self.srow[0:1, 0:NL], lin, est11, est11,
                ALU.mult, ALU.add, ins=[buf("rows_sb"), buf("lads")],
                outs=[buf("srow")])
        self.bc_row(self.srow[0:1, 0:NL], NL, self.thr[:, 0:NL], "thr",
                    "srow")
        self.mset("vector", self.cnt[:], 0.0, outs=[buf("cnt")])
        self.mset("vector", self.accs[:, 0:NL], 0.0, outs=[buf("accs")])

    def capture_tile(self, flat, bflat, width, spans, cnthi_col, slots,
                     bslots, slot_off, scrt, bscr):
        buf = self.buf
        data, bdata = flat, bflat
        self.ts("vector", scrt[:, 0:width], data, self.qwin[:, 1:2],
                None, ALU.is_gt, ins=[bdata, buf("qwin")],
                outs=[bscr, buf("accs")],
                accum=self.accs[:, 30:31])
        self.tt("vector", self.cnthi[:, cnthi_col:cnthi_col + 1],
                self.cnthi[:, cnthi_col:cnthi_col + 1],
                self.accs[:, 30:31], ALU.add, ins=[buf("accs")],
                outs=[buf("cnthi")])
        self.ts("vector", scrt[:, 0:width], data, self.qwin[:, 1:2],
                None, ALU.is_le, ins=[bdata, buf("qwin")], outs=[bscr])
        self.tt("gpsimd", scrt[:, 0:width], scrt[:, 0:width],
                data, ALU.mult, ins=[bdata, bscr], outs=[bscr])
        for s in range(width // spans):
            self.max8(slots[:, slot_off + s * 8:slot_off + (s + 1) * 8],
                      scrt[:, s * spans:(s + 1) * spans],
                      ins=[bscr], outs=[bslots])

    def rank_consts(self, n_const, jl_out, jh_out, hw_out):
        buf = self.buf
        bl = buf("lads")
        L = self.lrow
        bL = buf("lrow")
        self.mset("vector", L[0:1, 110:111], float(n_const), outs=[bL])
        self.mset("vector", L[0:1, 111:112], 0.995, outs=[bL])
        self.ts("vector", L[0:1, 112:113], L[0:1, 110:111], 1.0, None,
                ALU.subtract, ins=[bL], outs=[bL])
        self.tt("vector", L[0:1, 113:114], L[0:1, 111:112], L[0:1, 112:113],
                ALU.mult, ins=[bL], outs=[bL])
        self.cp("vector", self.i32s[0:1, 0:1], L[0:1, 113:114], ins=[bL],
                outs=[self.buf("i32s")])
        self.cp("vector", L[0:1, 115:116], self.i32s[0:1, 0:1],
                ins=[self.buf("i32s")], outs=[bL])
        self.ts("vector", L[0:1, 114:115], L[0:1, 113:114], 0.0, None,
                ALU.max, ins=[bL], outs=[bL])
        self.tt("vector", L[0:1, 114:115], L[0:1, 113:114], L[0:1, 115:116],
                ALU.subtract, ins=[bL], outs=[bL])
        self.ts("vector", L[0:1, 114:115], L[0:1, 114:115], 0.0, 0.9999,
                ALU.max, ALU.min, ins=[bL], outs=[bL])
        self.tt("vector", L[0:1, 115:116], L[0:1, 113:114], L[0:1, 114:115],
                ALU.subtract, ins=[bL], outs=[bL])
        self.tt("vector", L[0:1, 116:117], L[0:1, 110:111], L[0:1, 115:116],
                ALU.subtract, ins=[bL], outs=[bL])
        self.ts("vector", jl_out, L[0:1, 116:117], 1.0, None, ALU.subtract,
                ins=[bL], outs=[bl])
        self.ts("vector", L[0:1, 117:118], L[0:1, 114:115], 0.0, None,
                ALU.is_gt, ins=[bL], outs=[bL])
        self.tt("vector", jh_out, jl_out, L[0:1, 117:118], ALU.subtract,
                ins=[bL, bl], outs=[bl])
        self.cp("vector", hw_out, L[0:1, 114:115], ins=[bL], outs=[bl])

    def gap_pick(self, glob_row, thr_row, tgt11, lo_out, hi_out, chi_out,
                 npt):
        buf = self.buf
        L, bL = self.lrow, buf("lrow")
        bl = buf("lads")
        self.ts("vector", L[0:1, 220:221], tgt11, 1.0, None, ALU.add,
                ins=[bl], outs=[bL])
        self.fw.op("vector", lambda v: v.tensor_scalar(
            L[0:1, 128:128 + npt], glob_row, L[0:1, 220:221], None,
            ALU.is_ge), ins=[bL, buf("srow")], outs=[bL])
        self._gp_n2 = getattr(self, "_gp_n2", -1) + 1
        if False and self.debug and self._gp_n2 == 9:
            self.dbg_mark(1)
            self.dbg(L[0:1, 128:128 + npt], npt)
            self.dbg(glob_row, npt)
            self.dbg(L[0:1, 220:221])
        self.ts("vector", L[0:1, 150:150 + npt - 1],
                L[0:1, 129:128 + npt], -1.0, 1.0, ALU.mult, ALU.add,
                ins=[bL], outs=[bL])
        self.tt("vector", L[0:1, 172:172 + npt - 1],
                L[0:1, 128:128 + npt - 1], L[0:1, 150:150 + npt - 1],
                ALU.mult, ins=[bL], outs=[bL])
        self.cp("vector", L[0:1, 172 + npt - 1:172 + npt],
                L[0:1, 128 + npt - 1:128 + npt], ins=[bL], outs=[bL])
        sel = L[0:1, 172:172 + npt]
        self.tt("vector", L[0:1, 195:195 + npt], sel, thr_row, ALU.mult,
                ins=[bL, buf("srow")], outs=[bL])
        self.red("vector", lo_out, L[0:1, 195:195 + npt], ALU.add,
                 ins=[bL], outs=[bl])
        self.tt("vector", L[0:1, 195:195 + npt - 1],
                L[0:1, 172:172 + npt - 1], thr_row[0:1, 1:npt], ALU.mult,
                ins=[bL, buf("srow")], outs=[bL])
        self.red("vector", hi_out, L[0:1, 195:195 + npt - 1], ALU.add,
                 ins=[bL], outs=[bl])
        self.tt("vector", L[0:1, 195:195 + npt - 1],
                L[0:1, 172:172 + npt - 1], glob_row[0:1, 1:npt], ALU.mult,
                ins=[bL, buf("srow")], outs=[bL])
        self.red("vector", chi_out, L[0:1, 195:195 + npt - 1], ALU.add,
                 ins=[bL], outs=[bl])
        self._gp_n = getattr(self, "_gp_n", -1) + 1
        if False and self.debug and self._gp_n in (8, 9) and self.dbg_idx < 230:
            self.dbg_mark(2 + self._gp_n)
            self.dbg(L[0:1, 220:221])
            self.dbg(L[0:1, 128:128 + npt], npt)
            self.dbg(glob_row, npt)
            self.dbg(thr_row, npt)
            self.dbg(sel, npt)

    def phase_barrier_all(self):
        toks = self.fw.barrier_tokens()
        self._barrier_toks = toks
        for b in self._bufs.values():
            b.readers = list(b.readers) + toks

    def window_from_est(self, est11, win):
        buf = self.buf
        self.ts("vector", self.lad(10), est11, float(1.0 - win), None,
                ALU.mult, ins=[buf("lads")], outs=[buf("lads")])
        self.ts("vector", self.lad(11), est11, float(1.0 + win), None,
                ALU.mult, ins=[buf("lads")], outs=[buf("lads")])
        self.bc_pp(self.lad(10), self.qwin[:, 0:1], "qwin")
        self.bc_pp(self.lad(11), self.qwin[:, 1:2], "qwin")

    def w1_prep_chunk(self, c):
        fw, buf = self.fw, self.buf
        par = c % 2
        st, bst = self.w1s[par], buf(f"w1s{par}")
        tb, btb = self.w1tb[par], buf(f"w1t{par}")
        xb_, bxb_ = self.w1x[par], buf(f"w1x{par}")
        fw.dma("sync", st[:, :], self.w1_d.ap()[c * 128:(c + 1) * 128, :],
               ins=[bst], outs=[bst], sem=f"l_w1p{par}")
        self.ts("vector", st[:, :], st[:, :], self.scp[:, 0:1], MAGIC,
                ALU.mult, ALU.add, ins=[bst, buf("scp")], outs=[bst])
        self.ts("vector", st[:, :], st[:, :], MAGIC, -1.0, ALU.subtract,
                ALU.max, ins=[bst], outs=[bst])
        self.ts("vector", tb[:, :], st[:, :], 1.0, None, ALU.min,
                ins=[bst], outs=[btb])
        for dc in range(DC):
            pt = self.ptr[dc % 2]
            bpt = buf(f"ptr{dc % 2}")
            self.tr(pt[:, 0:128], tb[:, dc * 128:(dc + 1) * 128],
                    ins=[btb], outs=[bpt])
            self.cp("vector", xb_[:, dc, :], pt[:, 0:128], ins=[bpt],
                    outs=[bxb_])
        fw.dma("gpsimd",
               self.w1q_sp.ap()[:, :, c * 128:(c + 1) * 128].rearrange(
                   "dc p h -> p dc h"),
               xb_[:], ins=[bxb_, buf("w1q_sp")], outs=[buf("w1q_sp")],
               sem=f"s_w1q{par}")

    def w1_finish(self):
        for dc in range(DC):
            self.fw.dma("sync", self.w1qT[:, dc, :],
                        self.w1q_sp.ap()[dc, :, :],
                        ins=[self.buf("w1q_sp"), self.buf("w1qT")],
                        outs=[self.buf("w1qT"), self.buf("slots1c")],
                        sem="l_w1q")

    def w2_prep_chunk(self, c):
        fw, buf = self.fw, self.buf
        for qt in range(4):
            par = (c * 4 + qt) % 2
            st, bst = self.w2s[par], buf(f"w2s{par}")
            tb, btb = self.w2tb[par], buf(f"w2tb{par}")
            xb_, bxb_ = self.w2x[par], buf(f"w2x{par}")
            fw.dma("sync", st[:, :],
                   self.w2_d.ap()[c * 128:(c + 1) * 128,
                                  qt * 1024:(qt + 1) * 1024],
                   ins=[bst], outs=[bst], sem=f"l_w2p{par}")
            self.ts("vector", st[:, :], st[:, :], self.scp[:, 1:2], MAGIC,
                    ALU.mult, ALU.add, ins=[bst, buf("scp")], outs=[bst])
            self.ts("vector", st[:, :], st[:, :], MAGIC, -1.0,
                    ALU.subtract, ALU.max, ins=[bst], outs=[bst])
            self.ts("vector", tb[:, :], st[:, :], 1.0, None, ALU.min,
                    ins=[bst], outs=[btb])
            for blk in range(8):
                pt = self.ptr[blk % 2]
                bpt = buf(f"ptr{blk % 2}")
                self.tr(pt[:, 0:128], tb[:, blk * 128:(blk + 1) * 128],
                        ins=[btb], outs=[bpt])
                self.cp("vector", xb_[:, blk, :], pt[:, 0:128], ins=[bpt],
                        outs=[bxb_])
            fw.dma("gpsimd",
                   self.w2q_sp.ap()[qt * 8:(qt + 1) * 8, :,
                                    c * 128:(c + 1) * 128].rearrange(
                       "hc p d -> p hc d"),
                   xb_[:], ins=[bxb_, buf("w2q_sp")], outs=[buf("w2q_sp")],
                   sem=f"s_w2q{par}")

    def w2_finish(self):
        for hc in range(HC):
            self.fw.dma("scalar", self.w2qT[:, hc, :],
                        self.w2q_sp.ap()[hc, :, :],
                        ins=[self.buf("w2q_sp")],
                        outs=[self.buf("w1qT"), self.buf("w2qT")],
                        sem="l_w2q")

    def q1_window_phase(self):
        for c in range(HC):
            self.w1_prep_chunk(c)

    def capture_q1_tile(self, flat, bflat, i):
        buf = self.buf
        width = 2048
        self.ts("vector", self.selscr[:, 0:width].bitcast(I32)[:, 0:width],
                flat.bitcast(I32), ABSM, None, ALU.bitwise_and,
                ins=[bflat], outs=[buf("selscr")])
        data, bdata = self.selscr[:, 0:width], buf("selscr")
        scr = self.q1scr
        bscr = buf("q1scr")
        self.ts("vector", scr, data, self.qwin[:, 1:2], None, ALU.is_gt,
                ins=[bdata, buf("qwin")], outs=[bscr, buf("accs")],
                accum=self.accs[:, 30:31])
        self.tt("vector", self.cnthi[:, i:i + 1], self.cnthi[:, i:i + 1],
                self.accs[:, 30:31], ALU.add, ins=[buf("accs")],
                outs=[buf("cnthi")])
        self.ts("vector", scr, data, self.qwin[:, 1:2], None, ALU.is_le,
                ins=[bdata, buf("qwin")], outs=[bscr])
        self.tt("vector", scr, scr, data, ALU.mult, ins=[bdata, bscr],
                outs=[bscr])
        for s in range(8):
            self.max8(self.slots1[:, i * 64 + s * 8:i * 64 + (s + 1) * 8],
                      scr[:, s * 256:(s + 1) * 256],
                      ins=[bscr], outs=[buf("slots1c")])

    def select_quantile(self, qid, slots, bslots, cnthi_lo, cnthi_n, nconst,
                        gamma_lad, s_col, gs_col):
        fw, buf = self.fw, self.buf
        L, bL = self.lrow, buf("lrow")
        bl = buf("lads")
        SR, bsr = self.srow, buf("srow")
        self.red("vector", self.accs[:, 26:27],
                 self.cnthi[:, cnthi_lo:cnthi_lo + cnthi_n], ALU.add,
                 ins=[buf("cnthi")], outs=[buf("accs")])
        self.col_total(self.accs[:, 26:27], 1, self.lad(16), "accs")
        self.rank_consts(nconst, self.lad(13), self.lad(14), self.lad(15))
        self.cp("vector", L[0:1, 0:1], self.lad(16), ins=[bl], outs=[bL])
        self.cp("vector", L[0:1, 1:2], self.lad(10), ins=[bl], outs=[bL])
        self.cp("vector", L[0:1, 2:3], self.lad(11), ins=[bl], outs=[bL])
        fw.dma("gpsimd", self.cc1_src.ap(), L[0:1, 0:64],
               ins=[bL, buf("cc1")], outs=[buf("cc1")], sem=f"c1s{qid}")
        self.coll("AllGather", ALU.bypass, self.cc1_src.ap(),
                  self.cc1_dst.ap(), ins=[buf("cc1")], outs=[buf("cc1d")])
        fw.dma("sync", L[0:1, 0:512],
               self.cc1_dst.ap().rearrange("r k -> (r k)"),
               ins=[buf("cc1d"), bL], outs=[bL], sem=f"c1l{qid}")
        self.red("vector", self.lad(18), L[0:1, 1:512:64], ALU.max,
                 ins=[bL], outs=[bl])
        self.red("vector", self.lad(19), L[0:1, 2:512:64], ALU.min,
                 ins=[bL], outs=[bl])
        self.red("vector", self.lad(12), L[0:1, 0:512:64], ALU.add,
                 ins=[bL], outs=[bl])
        self.tt("vector", self.lad(17), self.lad(19), self.lad(18),
                ALU.subtract, ins=[bl], outs=[bl])
        nslots = slots.shape[-1]
        for rnd in range(2):
            if rnd == 0:
                span, base = self.lad(17), self.lad(18)
            else:
                self.tt("vector", self.lad(36), self.lad(34), self.lad(33),
                        ALU.subtract, ins=[bl], outs=[bl])
                span, base = self.lad(36), self.lad(33)
            self.ts("vector", SR[0:1, 32:32 + NLE],
                    self.rows[0:1, 32:32 + NLE], span, base,
                    ALU.mult, ALU.add, ins=[buf("rows_sb"), bl], outs=[bsr])
            self.bc_row(SR[0:1, 32:32 + NLE], NLE, self.thr[:, NLE:2 * NLE],
                        "thr", "srow")
            for j in range(NLE):
                self.ts("vector", self.selscr[:, 0:nslots], slots,
                        self.thr[:, NLE + j:NLE + j + 1], None, ALU.is_gt,
                        ins=[bslots, buf("thr")], outs=[buf("selscr"),
                                                        buf("accs")],
                        accum=self.accs[:, j:j + 1])
            self.col_total(self.accs[:, 0:NLE], NLE, L[0:1, 300:300 + NLE],
                           "accs", out_key="lrow")
            self.ts("vector", L[0:1, 0:NLE], L[0:1, 300:300 + NLE],
                    self.lad(16), None, ALU.add, ins=[bL, bl], outs=[bL])
            fw.dma("gpsimd", self.cc2_src.ap(), L[0:1, 0:32],
                   ins=[bL, buf("cc2")], outs=[buf("cc2")],
                   sem=f"c2s{qid}{rnd}")
            self.coll("AllGather", ALU.bypass, self.cc2_src.ap(),
                      self.cc2_dst.ap(), ins=[buf("cc2")],
                      outs=[buf("cc2d")])
            fw.dma("sync", L[0:1, 0:256],
                   self.cc2_dst.ap().rearrange("r k -> (r k)"),
                   ins=[buf("cc2d"), bL], outs=[bL], sem=f"c2l{qid}{rnd}")
            self.cp("vector", SR[0:1, 96:96 + NLE], L[0:1, 0:NLE], ins=[bL],
                    outs=[bsr])
            for r in range(1, 8):
                self.tt("vector", SR[0:1, 96:96 + NLE],
                        SR[0:1, 96:96 + NLE],
                        L[0:1, r * 32:r * 32 + NLE], ALU.add, ins=[bL, bsr],
                        outs=[bsr])
            self.gap_pick(SR[0:1, 96:96 + NLE], SR[0:1, 32:32 + NLE],
                          self.lad(13), self.lad(33), self.lad(34),
                          self.lad(35), NLE)
        if self.debug and qid == 2:
            self.dbg_mark(15)
            self.dbg(self.lads[0:1, 20:27], 7)
            self.dbg(self.lads[0:1, 33:36], 3)
            self.dbg(self.srow[0:1, 96:96 + NLE], NLE)
            self.dbg(self.srow[0:1, 32:32 + NLE], NLE)
        self.bc_pp(self.lad(33), self.qwin[:, 2:3], "qwin")
        self.bc_pp(self.lad(34), self.qwin[:, 3:4], "qwin")
        self.stt("vector", self.selscr[:, 0:nslots], slots, self.qwin[:, 2:3],
                 slots, ALU.is_gt, ALU.mult, ins=[bslots, buf("qwin")],
                 outs=[buf("selscr")])
        self.stt("vector", self.selscr[:, 0:nslots], self.selscr[:, 0:nslots],
                 self.qwin[:, 3:4], self.selscr[:, 0:nslots], ALU.is_le,
                 ALU.mult, ins=[buf("qwin")], outs=[buf("selscr")])
        self.max8(self.candx[:, 0:8], self.selscr[:, 0:nslots],
                  ins=[buf("selscr")], outs=[buf("candx")])
        self.mrep(self.selscr[:, 0:nslots], self.candx[:, 0:8],
                  self.selscr[:, 0:nslots], ins=[buf("candx")],
                  outs=[buf("selscr")])
        self.max8(self.candx[:, 8:16], self.selscr[:, 0:nslots],
                  ins=[buf("selscr")], outs=[buf("candx")])
        fw.dma("gpsimd",
               self.cc3_src.ap().rearrange("a (p s) -> (a p) s", p=128),
               self.candx[:],
               ins=[buf("candx"), buf("cc3")], outs=[buf("cc3")],
               sem=f"c3s{qid}")
        self.coll("AllGather", ALU.bypass, self.cc3_src.ap(),
                  self.cc3_dst.ap(), ins=[buf("cc3")], outs=[buf("cc3d")])
        fw.dma("sync", self.gath[:].rearrange("p (r s) -> p r s", r=8),
               self.cc3_dst.ap().rearrange("r (p s) -> p r s", p=128),
               ins=[buf("cc3d"), buf("gath")], outs=[buf("gath")],
               sem=f"c3l{qid}")
        for ti, (tgt, vout) in enumerate([(self.lad(13), self.lad(28))]):
            lo, hi, ca = self.lad(33), self.lad(34), self.lad(35)
            for rnd in range(2):
                self.tt("vector", self.lad(36), hi, lo, ALU.subtract,
                        ins=[bl], outs=[bl])
                self.ts("vector", SR[0:1, 32:32 + NLE],
                        self.rows[0:1, 32:32 + NLE], self.lad(36), lo,
                        ALU.mult, ALU.add, ins=[buf("rows_sb"), bl],
                        outs=[bsr])
                self.bc_row(SR[0:1, 32:32 + NLE], NLE,
                            self.thr[:, NLE:2 * NLE], "thr", "srow")
                for j in range(NLE):
                    self.ts("vector", self.qsub[:, 128:256], self.gath[:],
                            self.thr[:, NLE + j:NLE + j + 1], None,
                            ALU.is_gt, ins=[buf("gath"), buf("thr")],
                            outs=[buf("qsub"), buf("accs")],
                            accum=self.accs[:, j:j + 1])
                self.col_total(self.accs[:, 0:NLE], NLE,
                               L[0:1, 300:300 + NLE], "accs",
                               out_key="lrow")
                self.ts("vector", SR[0:1, 96:96 + NLE],
                        L[0:1, 300:300 + NLE], self.lad(35), None, ALU.add,
                        ins=[bL, bl], outs=[bsr])
                lo2 = self.lad(44 + 3 * rnd + 8 * ti)
                hi2 = self.lad(45 + 3 * rnd + 8 * ti)
                ca2 = self.lad(46 + 3 * rnd + 8 * ti)
                self.gap_pick(SR[0:1, 96:96 + NLE], SR[0:1, 32:32 + NLE],
                              tgt, lo2, hi2, ca2, NLE)
                lo, hi, ca = lo2, hi2, ca2
            self.bc_pp(lo, self.qwin[:, 2:3], "qwin")
            self.bc_pp(hi, self.qwin[:, 3:4], "qwin")
            self.stt("vector", self.qsub[:, 0:128], self.gath[:],
                     self.qwin[:, 2:3], self.gath[:], ALU.is_gt, ALU.mult,
                     ins=[buf("gath"), buf("qwin")], outs=[buf("qsub")])
            self.stt("vector", self.qsub[:, 0:128], self.qsub[:, 0:128],
                     self.qwin[:, 3:4], self.qsub[:, 0:128], ALU.is_le,
                     ALU.mult, ins=[buf("qwin")], outs=[buf("qsub")])
            self.max8(self.candx[:, 0:8], self.qsub[:, 0:128],
                      ins=[buf("qsub")], outs=[buf("candx")])
            fw.dma("gpsimd",
                   self.row_sp.ap().rearrange("a (p s) -> (a p) s", p=128),
                   self.candx[:, 0:8],
                   ins=[buf("candx"), buf("row_sp")], outs=[buf("row_sp")],
                   sem=f"rs{qid}{ti}")
            fw.dma("sync", self.row1k[0:1, :], self.row_sp.ap(),
                   ins=[buf("row_sp"), buf("row1k")], outs=[buf("row1k")],
                   sem=f"rl{qid}{ti}")
            self.max8(SR[0:1, 224:232], self.row1k[0:1, :],
                      ins=[buf("row1k")], outs=[bsr])
            self.mrep(self.row1k[0:1, :], SR[0:1, 224:232],
                      self.row1k[0:1, :], ins=[bsr], outs=[buf("row1k")])
            self.max8(SR[0:1, 232:240], self.row1k[0:1, :],
                      ins=[buf("row1k")], outs=[bsr])
            self.tt("vector", self.lad(36), tgt, ca, ALU.subtract,
                    ins=[bl], outs=[bl])
            self.ts("vector", L[0:1, 52:53], self.lad(36), 15.0, None,
                    ALU.min, ins=[bl], outs=[bL])
            self.ts("vector", L[0:1, 16:32], self.iota16[:], L[0:1, 52:53],
                    None, ALU.is_equal, ins=[buf("iota16"), bL], outs=[bL])
            self.tt("vector", L[0:1, 16:32], L[0:1, 16:32],
                    SR[0:1, 224:240], ALU.mult, ins=[bsr, bL], outs=[bL])
            self.red("vector", vout, L[0:1, 16:32], ALU.add, ins=[bL],
                     outs=[bl])
            ok = self.lad(40 + ti)
            self.ts("vector", L[0:1, 40:41], tgt, 1.0, None, ALU.add,
                    ins=[bl], outs=[bL])
            self.fw.op("vector", lambda v: v.tensor_scalar(
                L[0:1, 41:42], SR[0:1, 96:97], L[0:1, 40:41], None,
                ALU.is_ge), ins=[bsr, bL], outs=[bL])
            self.ts("vector", L[0:1, 42:43], self.lad(36), 0.0, None,
                    ALU.is_ge, ins=[bl], outs=[bL])
            self.ts("vector", L[0:1, 43:44], self.lad(36), 15.5, None,
                    ALU.is_le, ins=[bl], outs=[bL])
            self.fw.op("vector", lambda v: v.tensor_tensor(
                L[0:1, 53:54], SR[0:1, 224:225], SR[0:1, 239:240],
                ALU.is_equal), ins=[bsr], outs=[bL])
            self.tt("vector", L[0:1, 43:44], L[0:1, 43:44], L[0:1, 53:54],
                    ALU.max, ins=[bL], outs=[bL])
            self.tt("vector", L[0:1, 41:42], L[0:1, 41:42], L[0:1, 42:43],
                    ALU.mult, ins=[bL], outs=[bL])
            self.tt("vector", ok, L[0:1, 41:42], L[0:1, 43:44], ALU.mult,
                    ins=[bL], outs=[bl])
        self.cp("vector", self.lad(29), self.lad(28), ins=[bl], outs=[bl])
        self.cp("vector", self.lad(41), self.lad(40), ins=[bl], outs=[bl])
        self.ts("vector", L[0:1, 44:45], self.lad(15), -1.0, 1.0, ALU.mult,
                ALU.add, ins=[bl], outs=[bL])
        self.tt("vector", L[0:1, 45:46], self.lad(28), L[0:1, 44:45],
                ALU.mult, ins=[bl, bL], outs=[bL])
        self.tt("vector", L[0:1, 46:47], self.lad(29), self.lad(15),
                ALU.mult, ins=[bl], outs=[bL])
        self.tt("vector", L[0:1, 45:46], L[0:1, 45:46], L[0:1, 46:47],
                ALU.add, ins=[bL], outs=[bL])
        self.tt("vector", self.lad(42), self.lad(40), self.lad(41),
                ALU.mult, ins=[bl], outs=[bl])
        self.tt("vector", L[0:1, 47:48], self.lad(33), self.lad(34),
                ALU.add, ins=[bl], outs=[bL])
        self.ts("vector", L[0:1, 47:48], L[0:1, 47:48], 0.5, None, ALU.mult,
                ins=[bL], outs=[bL])
        self.tt("vector", L[0:1, 45:46], L[0:1, 45:46], self.lad(42),
                ALU.mult, ins=[bl, bL], outs=[bL])
        self.ts("vector", L[0:1, 48:49], self.lad(42), -1.0, 1.0, ALU.mult,
                ALU.add, ins=[bl], outs=[bL])
        self.tt("vector", L[0:1, 47:48], L[0:1, 47:48], L[0:1, 48:49],
                ALU.mult, ins=[bL], outs=[bL])
        self.tt("vector", L[0:1, 45:46], L[0:1, 45:46], L[0:1, 47:48],
                ALU.add, ins=[bL], outs=[bL])
        self.ts("vector", self.lad(43), L[0:1, 45:46], EPS, None, ALU.max,
                ins=[bL], outs=[bl])
        for s11 in (self.lad(9), self.lad(10), self.lad(11), self.lad(18),
                    self.lad(19), self.lad(12), self.lad(13), self.lad(14),
                    self.lad(28), self.lad(29), self.lad(40), self.lad(41),
                    self.lad(42), self.lad(43)):
            self.dbg(s11)
        self.rcp(L[0:1, 51:52], self.lad(43), ins=[bl], outs=[bL])
        self.ts("vector", self.lad(60), L[0:1, 51:52], 127.0, None,
                ALU.mult, ins=[bL], outs=[bl])
        self.tt("vector", L[0:1, 50:51], self.lad(gamma_lad), self.lad(43),
                ALU.mult, ins=[bl], outs=[bL])
        self.ts("vector", self.lad(61), L[0:1, 50:51],
                float(1.0 / 127.0), None, ALU.mult, ins=[bL], outs=[bl])
        self.bc_pp(self.lad(60), self.s_pp[:, s_col:s_col + 1], "s_pp")
        self.bc_pp(self.lad(61), self.s_pp[:, gs_col:gs_col + 1], "s_pp")

    def f1_phase(self):
        fw, buf = self.fw, self.buf
        self.mset("vector", self.accs[:, 24:26], 0.0, outs=[buf("accs")])
        self.q2_calls = 0
        for i in range(NT):
            par = i % 2
            pi = (i // 2) % 2
            hb, hk = self.hlnF[par], buf(f"hlnF{par}")
            kt, bkt = self.k1tok[par], buf(f"k1tok{par}")
            kT, bkT = self.k1Tw[pi], buf(f"k1Tw{pi}")
            r0 = i * TT
            fw.dma("sync", hb[:],
                   self.hln_sp.ap()[r0:r0 + TT, :].rearrange(
                       "(tc p) d -> p tc d", p=128),
                   ins=[buf("hln_sp"), hk], outs=[hk], sem=f"f1h{par}")
            for tc in range(2):
                v = hb[:, tc, :]
                self.ts("vector", v, v, self.s_pp[:, 0:1], MAGIC, ALU.mult,
                        ALU.add, ins=[hk, buf("s_pp")], outs=[hk])
                self.ts("vector", v, v, MAGIC, -128.0, ALU.subtract,
                        ALU.max, ins=[hk], outs=[hk])
                self.ts("vector", kt[:, tc, :], v, 127.0, None,
                        ALU.min, ins=[hk], outs=[bkt])
            for tc in range(2):
                for dc in range(DC):
                    pt = self.ptr[dc % 2]
                    bpt = buf(f"ptr{dc % 2}")
                    self.tr(pt[:, 0:128],
                            kt[:, tc, dc * 128:(dc + 1) * 128],
                            ins=[bkt], outs=[bpt])
                    self.cp("vector",
                            kT[:, dc,
                               par * TT + tc * 128:par * TT + (tc + 1) * 128],
                            pt[:, 0:128], ins=[bpt], outs=[bkT])
            if par == 1:
                p = i // 2
                r0p = (i - 1) * TT
                gb, gk = self.gsb[pi], buf(f"gsb{pi}")
                for gr in range(8):
                    if p >= 4:
                        rb, rk = self.q2rd[gr % 2], buf(f"q2rd{gr % 2}")
                        fw.dma("scalar", rb[:],
                               self.g_sp.ap()[gr * 4:(gr + 1) * 4, :,
                                              (p - 4) * 512:
                                              (p - 4) * 512 + 512].rearrange(
                                   "hc p t -> p hc t"),
                               ins=[buf("g_sp"), rk], outs=[rk],
                               sem=f"qgr{gr % 2}")
                    for h4 in range(4):
                        hc = gr * 4 + h4
                        pbk, bp = self.pb[h4], buf(f"pb{h4}")
                        for dc in range(DC):
                            self.mm(pbk[:, 0:512],
                                    self.w1qT[:, dc,
                                              hc * 128:(hc + 1) * 128],
                                    kT[:, dc, :], dc == 0, dc == DC - 1,
                                    ins=[buf("w1qT"), bkT], outs=[bp])
                        self.act(gb[:, h4, :], pbk[:, 0:512], AF.Gelu,
                                 scale=self.s_pp[:, 1:2],
                                 ins=[bp, buf("s_pp")], outs=[gk])
                    fw.dma("gpsimd",
                           self.g_sp.ap()[gr * 4:(gr + 1) * 4, :,
                                          r0p:r0p + 512].rearrange(
                               "hc p t -> p hc t"),
                           gb[:], ins=[gk, buf("g_sp")],
                           outs=[buf("g_sp")], sem=f"sg{pi}")
                    if p == 0:
                        self.ladder_tile(gb.rearrange("p a b -> p (a b)"),
                                         gk, 0, need_abs=False, subs=32,
                                         nq=2, width=2048, shift=0)
                    elif p <= 2 or (p == 3 and gr % 2 == p % 2):
                        self.ladder_tile(gb.rearrange("p a b -> p (a b)"),
                                         gk, 1, need_abs=False, subs=32,
                                         nq=2, width=2048,
                                         shift=(p * 4 + gr) % 32)
                        self.q2_calls += 1
                    if p >= 4:
                        j = p - 4
                        rb, rk = self.q2rd[gr % 2], buf(f"q2rd{gr % 2}")
                        self.capture_tile(
                            gb.rearrange("p a b -> p (a b)"), gk,
                            2048, 1024, NT + p, self.slots2,
                            buf("slots2c"), (p * 8 + gr) * 16,
                            self.selscr, buf("selscr"))
                        self.capture_tile(
                            rb.rearrange("p a b -> p (a b)"), rk,
                            2048, 1024, NT + j, self.slots2,
                            buf("slots2c"), (j * 8 + gr) * 16,
                            self.selscr, buf("selscr"))
                if p == 0:
                    self.ladder_setup_gen1(2)
                if p == 2:
                    self.interp_est(0.005 * self.q2_calls * 64 * 128,
                                    self.lad(9))
                    self.ladder_setup_gen2(self.lad(9), 2)
                    self.q2_calls = 0
                if p == 3:
                    self.interp_est(0.005 * self.q2_calls * 64 * 128,
                                    self.lad(9))
                    self.window_from_est(self.lad(9), Q2_WIN)
                self.w2_prep_chunk(p)
        self.w2_finish()

    def f2_phase(self):
        fw, buf = self.fw, self.buf
        out_toks = []
        st, bst = self.stats, buf("stats")
        for i in range(NT):
            par = i % 2
            r0 = i * TT
            x2b, bx2 = self.x2F[par], buf(f"x2F{par}")
            ob, bob = self.outsb[par], buf(f"outs{par}")
            fw.dma("sync", x2b[:],
                   self.x2_sp.ap()[r0:r0 + TT, :].rearrange(
                       "(tc p) d -> p tc d", p=128),
                   ins=[buf("x2_sp"), bx2], outs=[bx2], sem=f"f2x{par}")
            for gr in range(8):
                gb = self.gF[gr % 2]
                gk = buf(f"gF{gr % 2}")
                kb = self.k2[gr % 2]
                kk = buf(f"k2_{gr % 2}")
                fw.dma("sync" if gr % 2 == 0 else "scalar", gb[:],
                       self.g_sp.ap()[gr * 4:(gr + 1) * 4, :,
                                      r0:r0 + TT].rearrange(
                           "hc p t -> p hc t"),
                       ins=[buf("g_sp"), gk], outs=[gk], sem=f"f2g{gr % 2}")
                flat = gb.rearrange("p a b -> p (a b)")
                self.ts("vector", flat, flat, self.s_pp[:, 2:3], MAGIC,
                        ALU.mult, ALU.add, ins=[gk, buf("s_pp")], outs=[gk])
                self.ts("vector", flat, flat, MAGIC, -128.0, ALU.subtract,
                        ALU.max, ins=[gk], outs=[gk])
                self.ts("vector", kb.rearrange("p a b -> p (a b)"), flat,
                        127.0, None, ALU.min, ins=[gk], outs=[kk])
                for h4 in range(4):
                    hc = gr * 4 + h4
                    for tc in range(2):
                        for nq in range(2):
                            pbk = self.pb[tc * 2 + nq]
                            bp = buf(f"pb{tc * 2 + nq}")
                            self.mm(pbk[:, 0:512],
                                    kb[:, h4, tc * 128:(tc + 1) * 128],
                                    self.w2qT[:, hc,
                                              nq * 512:(nq + 1) * 512],
                                    hc == 0, hc == HC - 1,
                                    ins=[kk, buf("w2qT")], outs=[bp])
            for tc in range(2):
                for nq in range(2):
                    pbk = self.pb[tc * 2 + nq]
                    bp = buf(f"pb{tc * 2 + nq}")
                    self.ts("vector",
                            ob[:, tc, nq * 512:(nq + 1) * 512],
                            pbk[:, 0:512], self.s_pp[:, 3:4], None,
                            ALU.mult, ins=[bp, buf("s_pp")],
                            outs=[bob])
                v = ob[:, tc, :]
                self.red("vector", st[:, tc:tc + 1], v, ALU.add,
                         ins=[bob], outs=[bst])
                self.act(self.pb[4][:, 0:512], v[:, 0:512], AF.Square,
                         ins=[bob], outs=[buf("pb4"), bst],
                         accum=st[:, 2 + tc:3 + tc])
                self.act(self.pb[4][:, 0:512], v[:, 512:1024], AF.Square,
                         ins=[bob], outs=[buf("pb4"), buf("accs")],
                         accum=self.accs[:, 28:29])
                self.tt("vector", st[:, 2 + tc:3 + tc],
                        st[:, 2 + tc:3 + tc], self.accs[:, 28:29], ALU.add,
                        ins=[buf("accs")], outs=[bst])
                self.ts("vector", st[:, 4 + tc:5 + tc], st[:, tc:tc + 1],
                        INV_D, None, ALU.mult, ins=[bst], outs=[bst])
                self.ts("vector", st[:, 6 + tc:7 + tc],
                        st[:, 2 + tc:3 + tc], INV_D, None, ALU.mult,
                        ins=[bst], outs=[bst])
                self.tt("vector", st[:, 8 + tc:9 + tc],
                        st[:, 4 + tc:5 + tc], st[:, 4 + tc:5 + tc],
                        ALU.mult, ins=[bst], outs=[bst])
                self.tt("vector", st[:, 6 + tc:7 + tc],
                        st[:, 6 + tc:7 + tc], st[:, 8 + tc:9 + tc],
                        ALU.subtract, ins=[bst], outs=[bst])
                self.act(st[:, 8 + tc:9 + tc], st[:, 6 + tc:7 + tc], AF.Ln,
                         bias=self.eps_t[:, 0:1],
                         ins=[bst, buf("eps_t")], outs=[bst])
                self.act(st[:, 10 + tc:11 + tc], st[:, 8 + tc:9 + tc],
                         AF.Exp, scale=-0.5, ins=[bst], outs=[bst])
                self.ts("vector", v, v, st[:, 4 + tc:5 + tc],
                        st[:, 10 + tc:11 + tc], ALU.subtract, ALU.mult,
                        ins=[bst], outs=[bob])
                self.tt("vector", v, v, x2b[:, tc, :], ALU.add,
                        ins=[bx2], outs=[bob])
            t = fw.dma("gpsimd",
                       self.y_out.ap()[r0:r0 + TT, :].rearrange(
                           "(tc p) d -> p tc d", p=128),
                       ob[:], ins=[bob, buf("y")],
                       outs=[buf("y")], sem=f"out{par}")
            out_toks.append(t)
        if self.debug:
            t = fw.dma("gpsimd", self.dbg_d.ap()[0:1, 0:320],
                       self.dbgrow[0:1, :],
                       ins=[buf("dbgrow"), buf("dbg")], outs=[buf("dbg")],
                       sem="outd")
            out_toks.append(t)
        fw.wait_all("sync", out_toks)

    def emit(self):
        self.fw.emit()


def build_program(debug=False):
    nc = bass.Bass("TRN2", target_bir_lowering=False)
    k = K(nc, debug)
    k.declare()
    k.init_phase()

    def post_b(bi):
        if bi == 0:
            k.ladder_setup_gen1(1)
        if bi == 3:
            k.interp_est(0.005 * 3 * 64 * 128, k.lad(9))
            k.ladder_setup_gen2(k.lad(9), 1)
        if bi == 7:
            k.interp_est(0.005 * 4 * 64 * 128, k.lad(9))
            k.window_from_est(k.lad(9), Q1_WIN)

    for i in range(NT):
        k.stage_a(i)
        if i % 8 != 0:
            k.stage_b_front_a(i - 1)
        if i >= 2 and (i - 2) % 8 < 6:
            k.stage_b_tail(i - 2)
            post_b(i - 2)
        k.stage_a_mm(i)
        if i % 8 != 0:
            k.stage_b_front_a2(i - 1)
            k.stage_b_front_b(i - 1)
        if i % 8 == 7:
            k.stage_b_tail(i - 1)
            post_b(i - 1)
            k.stage_b_front_a(i)
            k.stage_b_front_a2(i)
            k.stage_b_front_b(i)
            k.stage_b_tail(i)
            post_b(i)
    k.phase_barrier_all()
    k.q1_window_phase()
    k.select_quantile(1, k.slots1[:, :], k.buf("slots1c"), 0, NT, N_Q1,
                      0, 0, 1)
    k.w1_finish()
    k.phase_barrier_all()
    k.f1_phase()
    k.phase_barrier_all()
    k.select_quantile(2, k.slots2[:, :], k.buf("slots2c"), NT, NT, N_Q2,
                      2, 2, 3)
    k.phase_barrier_all()
    k.f2_phase()
    k.emit()
    return nc


def make_core_inputs(inputs, core):
    x = np.asarray(inputs["L0_attn_x"])
    xs = np.ascontiguousarray(x[2 * core:2 * core + 2].reshape(NTOK, D))
    vl = [np.asarray(inputs["L0_attn_conv_w"])[0, 0],
          np.asarray(inputs["L0_attn_conv_w"])[1, 0],
          np.asarray(inputs["L0_attn_conv_w"])[2, 0],
          np.asarray(inputs["L0_attn_conv_b"]),
          np.asarray(inputs["L0_attn_A"]),
          np.asarray(inputs["L0_attn_D_skip"]),
          np.asarray(inputs["L0_attn_b_dt"]),
          np.asarray(inputs["L0_attn_gamma"]),
          np.asarray(inputs["L0_attn_beta"]),
          np.zeros(D, np.float32), np.zeros(D, np.float32),
          np.zeros(D, np.float32)]
    vecs = np.stack([np.asarray(v, np.float32).reshape(DC, 128).T
                     for v in vl], axis=2).reshape(128, DC * 12)
    vecs = np.ascontiguousarray(vecs)
    lngb = np.ascontiguousarray(np.concatenate([
        np.broadcast_to(np.asarray(inputs["L0_ffn_ln_g"], np.float32),
                        (128, D)),
        np.broadcast_to(np.asarray(inputs["L0_ffn_ln_b"], np.float32),
                        (128, D))], axis=1))
    ident = np.eye(128, dtype=ml_dtypes.bfloat16)
    rows = np.zeros((1, 64), np.float32)
    rows[0, 0:NL] = (GEOLO ** (np.arange(NL) / (NL - 1))).astype(np.float32)
    rows[0, NL:2 * NL] = ((NL - 1) / 2.0 - np.arange(NL)) / ((NL - 1) / 2.0) \
        * LIN_W
    rows[0, 32:32 + NLE] = np.arange(NLE) / float(NLE - 1)
    return {
        "x": xs,
        "W_in": np.asarray(inputs["L0_attn_W_in"], np.float32),
        "W_out": np.asarray(inputs["L0_attn_W_out"], np.float32),
        "W_xp": np.asarray(inputs["L0_attn_W_xp"], np.float32),
        "W_dt": np.asarray(inputs["L0_attn_W_dt"], np.float32),
        "W_Bg": np.asarray(inputs["L0_attn_W_Bg"], np.float32),
        "W_Cg": np.asarray(inputs["L0_attn_W_Cg"], np.float32),
        "W_Bp": np.asarray(inputs["L0_attn_W_Bp"], np.float32),
        "W_Cp": np.asarray(inputs["L0_attn_W_Cp"], np.float32),
        "w1": np.asarray(inputs["L0_ffn_w1"], np.float32),
        "w2": np.asarray(inputs["L0_ffn_w2"], np.float32),
        "vecs": vecs, "lngb": lngb, "ident": ident, "rows": rows,
    }


def kernel(**inputs):
    nc = build_program(DEBUG)
    in_maps = [make_core_inputs(inputs, c) for c in range(8)]
    res = run_bass_kernel_spmd(nc, in_maps, core_ids=list(range(8)))
    outs = [np.asarray(r["y"]) for r in res.results]
    return np.concatenate(outs, 0).reshape(16, 2048, 1024)


# version 21
# speedup vs baseline: 1.0595x; 1.0075x over previous
import contextlib

import numpy as np
import ml_dtypes

import concourse.bass as bass
import concourse.mybir as mybir
from concourse.bass_utils import run_bass_kernel_spmd

F32 = mybir.dt.float32
BF16 = mybir.dt.bfloat16
I32 = mybir.dt.int32
AF = mybir.ActivationFunctionType
ALU = mybir.AluOpType
AX = mybir.AxisListType

DEBUG = False
TRACE_SITES = False

ENGINES = ("tensor", "vector", "scalar", "gpsimd", "sync")


class Buf:
    __slots__ = ("name", "writer", "readers")

    def __init__(self, name=""):
        self.name = name
        self.writer = None
        self.readers = []


class Fw:

    def __init__(self, nc):
        self.nc = nc
        self._stack = contextlib.ExitStack()
        self.sem = {}
        self.count = {}
        self.known = {e: {} for e in ENGINES}
        self.eng_sem = {}
        self.prog = {e: [] for e in ENGINES}
        self.sites = {}
        for e in ENGINES:
            sname = "S_" + e
            self.sem[sname] = self._stack.enter_context(nc.semaphore(sname))
            self.count[sname] = 0
            self.eng_sem[e] = sname

    def new_sem(self, name):
        if name not in self.sem:
            self.sem[name] = self._stack.enter_context(self.nc.semaphore(name))
            self.count[name] = 0
        return name

    def _sched_wait(self, eng, token):
        if token is None:
            return
        sname, val = token
        if self.known[eng].get(sname, 0) >= val:
            return
        if not sname.startswith("S_") and val != self.count[sname]:
            raise RuntimeError(
                f"wait on dma sem {sname} at {val} != cumulative "
                f"{self.count[sname]}; use a dedicated sem")
        self.known[eng][sname] = val
        sem = self.sem[sname]
        self.prog[eng].append(lambda h, s=sem, v=val: h.wait_ge(s, v))

    def _deps_for(self, ins, outs):
        deps = []
        for b in ins:
            if b.writer is not None:
                deps.append(b.writer)
        for b in outs:
            if b.writer is not None:
                deps.append(b.writer)
            deps.extend(b.readers)
        return deps

    def _finish(self, ins, outs, token):
        for b in ins:
            b.readers.append(token)
        for b in outs:
            b.writer = token
            b.readers = []

    def op(self, eng, fn, ins=(), outs=(), deps=()):
        for d in self._deps_for(ins, outs):
            self._sched_wait(eng, d)
        for d in deps:
            self._sched_wait(eng, d)
        sname = self.eng_sem[eng]
        self.count[sname] += 1
        sem = self.sem[sname]

        def run(h, fn=fn, sem=sem):
            fn(h).then_inc(sem, 1)

        self.prog[eng].append(run)
        if TRACE_SITES:
            import traceback
            self.sites.setdefault(eng, {})[len(self.prog[eng]) - 1] = \
                traceback.extract_stack()[-4:-2]
        token = (sname, self.count[sname])
        self._finish(ins, outs, token)
        return token

    def dma(self, eng, out_ap, in_ap, ins=(), outs=(), sem="dma", deps=(),
            **kw):
        self.new_sem(sem)
        for d in self._deps_for(ins, outs):
            self._sched_wait(eng, d)
        for d in deps:
            self._sched_wait(eng, d)
        self.count[sem] += 16
        semh = self.sem[sem]

        def run(h, out_ap=out_ap, in_ap=in_ap, semh=semh, kw=kw):
            h.dma_start(out=out_ap, in_=in_ap, **kw).then_inc(semh, 16)

        self.prog[eng].append(run)
        token = (sem, self.count[sem])
        self._finish(ins, outs, token)
        return token

    def wait_all(self, eng, tokens):
        best = {}
        for sname, val in tokens:
            best[sname] = max(best.get(sname, 0), val)
        for t in best.items():
            self._sched_wait(eng, t)

    def barrier_tokens(self):
        toks = [(self.eng_sem[e], self.count[self.eng_sem[e]])
                for e in ENGINES]
        toks += [(s, self.count[s]) for s in self.sem
                 if not s.startswith("S_")]
        return toks

    def emit(self):
        with self.nc.Block() as block:
            for e in ENGINES:
                if not self.prog[e]:
                    continue
                getattr(block, e)(lambda h, e=e: [f(h) for f in self.prog[e]])
        self._stack.close()


D = 1024
H = 4096
DC = 8
HC = 32
TT = 256
NT = 16
NTOK = NT * TT
RK = 64
DS = 16
EPS = 1e-5
INV_D = float(1.0 / 1024.0)
MAGIC = float(3 << 22)
N_Q1 = float(16 * 2048 * 1024)
N_Q2 = float(16 * 2048 * 4096)
RG = [[0, 1, 2, 3, 4, 5, 6, 7]]
NL = 16
NLE = 18
GEOLO = 0.04
LIN_W = 0.10
Q1_WIN = 0.06
Q2_WIN = 0.045
Q1_SUB_PER_TILE = 2 * 128 * 128
Q2_SUB_PER_TILE = 4 * 128 * 128
Q1_SLOTS = 4 * 8 * NT
Q2_SLOTS = 64 * 16
V_W0, V_W1, V_W2, V_CB, V_A, V_DSK, V_BDT, V_GAM, V_BET = range(9)
ABSM = 0x7FFFFFFF


class K:
    def __init__(self, nc, debug=False):
        self.nc = nc
        self.fw = Fw(nc)
        self.debug = debug
        self._bufs = {}
        self.dbg_idx = 0

    def buf(self, key):
        if key not in self._bufs:
            b = Buf(key)
            b.readers = list(getattr(self, "_barrier_toks", ()))
            self._bufs[key] = b
        return self._bufs[key]

    def carve(self, off_bytes, shape, dtype=F32):
        n = 1
        for s in shape[1:]:
            n *= s
        esz = 2 if dtype == BF16 else 4
        assert off_bytes % 4 == 0
        f0 = off_bytes // 4
        nf = (n * esz + 3) // 4
        ap = self.arena[:, f0:f0 + nf]
        if dtype == BF16:
            ap = ap.bitcast(BF16)[:, :n]
        if len(shape) == 3:
            ap = ap.rearrange("p (a b) -> p a b", a=shape[1])
        return ap

    def ts(self, eng, out, in_, s1, s2, op0, op1=None, ins=(), outs=(),
           deps=(), accum=None):
        if op1 is None and s2 is None and accum is None:
            return self.fw.op(
                eng, lambda h: h.tensor_scalar(out, in_, s1, None, op0),
                ins=ins, outs=outs, deps=deps)
        if op1 is None:
            op1 = ALU.add if accum is not None and s2 is None else op1
        assert op1 is not None, "need op1"
        return self.fw.op(
            eng, lambda h: h.tensor_scalar(out, in_, s1, s2, op0, op1,
                                           accum_out=accum),
            ins=ins, outs=outs, deps=deps)

    def tt(self, eng, out, a, b, op, ins=(), outs=(), deps=()):
        return self.fw.op(eng, lambda h: h.tensor_tensor(out, a, b, op),
                          ins=ins, outs=outs, deps=deps)

    def stt(self, eng, out, in0, scalar, in1, op0, op1, ins=(), outs=(),
            deps=()):
        return self.fw.op(
            eng, lambda h: h.scalar_tensor_tensor(out, in0, scalar, in1,
                                                  op0, op1),
            ins=ins, outs=outs, deps=deps)

    def cp(self, eng, out, in_, ins=(), outs=(), deps=()):
        return self.fw.op(eng, lambda h: h.tensor_copy(out, in_),
                          ins=ins, outs=outs, deps=deps)

    def act(self, out, in_, func, bias=0.0, scale=1.0, ins=(), outs=(),
            deps=(), accum=None):
        return self.fw.op(
            "scalar", lambda h: h.activation(out, in_, func, bias=bias,
                                             scale=scale, accum_out=accum),
            ins=ins, outs=outs, deps=deps)

    def mm(self, out, lhsT, rhs, start, stop, ins=(), outs=(), deps=()):
        return self.fw.op(
            "tensor", lambda h: h.matmul(out, lhsT, rhs, start=start,
                                         stop=stop),
            ins=ins, outs=outs, deps=deps)

    def tr(self, out, in_, ins=(), outs=(), deps=()):
        return self.fw.op(
            "tensor", lambda h: h.transpose(out, in_, self.identb[:]),
            ins=list(ins) + [self.buf("identb")], outs=outs, deps=deps)

    def red(self, eng, out, in_, op, absval=False, ins=(), outs=(), deps=()):
        return self.fw.op(
            eng, lambda h: h.tensor_reduce(out, in_, AX.X, op,
                                           apply_absolute_value=absval),
            ins=ins, outs=outs, deps=deps)

    def mset(self, eng, ap, val, outs=(), deps=()):
        return self.fw.op(eng, lambda h: h.memset(ap, val), outs=outs,
                          deps=deps)

    def rcp(self, out, in_, ins=(), outs=(), deps=()):
        return self.fw.op("vector", lambda h: h.reciprocal(out, in_),
                          ins=ins, outs=outs, deps=deps)

    def max8(self, out, in_, ins=(), outs=(), deps=()):
        return self.fw.op("vector", lambda h: h.max(out, in_),
                          ins=ins, outs=outs, deps=deps)

    def mrep(self, out, repl, in_, ins=(), outs=(), deps=()):
        return self.fw.op(
            "vector", lambda h: h.match_replace(out, repl, in_, 0.0),
            ins=ins, outs=outs, deps=deps)

    def coll(self, kind, op, src_ap, dst_ap, ins=(), outs=()):
        return self.fw.op(
            "gpsimd", lambda g: g.collective_compute(kind, op, RG,
                                                     [src_ap], [dst_ap]),
            ins=ins, outs=outs)

    def declare(self):
        nc = self.nc
        DR = nc.dram_tensor
        self.x_in = DR("x", [NTOK, D], F32, kind="ExternalInput")
        self.y_out = DR("y", [NTOK, D], F32, kind="ExternalOutput")
        specs = [("W_in", [D, 2 * D]), ("W_out", [D, D]), ("W_xp", [D, RK]),
                 ("W_dt", [RK, D]), ("W_Bg", [D, DS]), ("W_Cg", [D, DS]),
                 ("W_Bp", [DS, D]), ("W_Cp", [DS, D]), ("w1", [H, D]),
                 ("w2", [D, H])]
        (self.win_d, self.wout_d, self.wxp_d, self.wdt_d, self.wbg_d,
         self.wcg_d, self.wbp_d, self.wcp_d, self.w1_d, self.w2_d) = [
            DR(n, s, F32, kind="ExternalInput") for n, s in specs]
        self.vecs_d = DR("vecs", [128, DC * 12], F32, kind="ExternalInput")
        self.lngb_d = DR("lngb", [128, 2 * D], F32, kind="ExternalInput")
        self.ident_d = DR("ident", [128, 128], BF16, kind="ExternalInput")
        self.rows_d = DR("rows", [1, 64], F32, kind="ExternalInput")
        dk = "ExternalOutput" if self.debug else "Internal"
        self.x2_sp = DR("x2_sp", [NTOK, D], F32, kind=dk)
        self.hln_sp = DR("hln_sp", [NTOK, D], F32, kind=dk)
        self.g_sp = DR("g_sp", [HC, 128, NTOK], F32,
                       kind=("ExternalOutput" if self.debug else "Internal"))
        self.w1q_sp = DR("w1q_sp", [DC, 128, H], BF16, kind="Internal")
        self.w2q_sp = DR("w2q_sp", [HC, 128, D], BF16, kind="Internal")
        self.cc1_src = DR("cc1_src", [1, 64], F32, kind="Internal")
        self.cc1_dst = DR("cc1_dst", [8, 64], F32, kind="Internal")
        self.cc2_src = DR("cc2_src", [1, 32], F32, kind="Internal")
        self.cc2_dst = DR("cc2_dst", [8, 32], F32, kind="Internal")
        self.cc3_src = DR("cc3_src", [1, 2048], F32, kind="Internal")
        self.cc3_dst = DR("cc3_dst", [8, 2048], F32, kind="Internal")
        self.row_sp = DR("row_sp", [1, 1024], F32, kind="Internal")
        if self.debug:
            self.dbg_d = DR("dbg", [1, 320], F32, kind="ExternalOutput")

        sb = nc.alloc_sbuf_tensor
        self.identb = sb("identb", [128, 128], BF16)
        self.vec = sb("vec", [128, DC, 12], F32)
        self.lng_bc = sb("lng_bc", [128, D], F32)
        self.lnb_bc = sb("lnb_bc", [128, D], F32)
        self.onescol = sb("onescol", [128, 1], F32)
        self.zcol = sb("zcol", [128, 1], F32)
        self.onesrow = sb("onesrow", [1, 128], F32)
        self.eps_t = sb("eps_t", [128, 1], F32)
        self.one_t = sb("one_t", [128, 1], F32)
        self.rows = sb("rows_sb", [1, 64], F32)
        self.iota16 = sb("iota16", [1, 16], F32)
        self.stats = sb("stats", [128, 16], F32)
        self.statB = sb("statB", [128, 16], F32)
        self.statR = sb("statR", [128, 16], F32)
        self.lads = sb("lads", [1, 64], F32)
        self.lrow = sb("lrow", [1, 512], F32)
        self.thr = sb("thr", [128, 2 * NLE], F32)
        self.cnt = sb("cnt", [128, NL], F32)
        self.accs = sb("accs", [128, 32], F32)
        self.qwin = sb("qwin", [128, 4], F32)
        self.hstate = sb("hstate", [128, DC], F32)
        self.xblast = sb("xblast", [128, DC], F32)
        self.s_pp = sb("s_pp", [128, 4], F32)
        self.scp = sb("scp", [128, 4], F32)
        self.candx = sb("candx", [128, 16], F32)
        self.gath = sb("gath", [128, 128], F32)
        self.qsub = sb("qsub", [128, 512], F32)
        self.row1k = sb("row1k", [1, 1024], F32)
        self.i32s = sb("i32s", [1, 2], I32)
        self.dbgrow = sb("dbgrow", [1, 320], F32)
        self.srow = sb("srow", [1, 256], F32)
        self.cnthi = sb("cnthi", [128, NT * 8], F32)

        AW = 45056
        self.arena = sb("arena", [128, AW], F32)
        C = self.carve
        self.winb = C(0, [128, DC, 2 * D], BF16)
        self.woutb = C(32768, [128, DC, D], BF16)
        self.wgateb = C(49152, [128, DC, 128], BF16)
        self.wdtb = C(51200, [128, D], BF16)
        self.wbpb = self.wdtb
        self.wcpb = self.wdtb
        self.xt = [C(56832, [128, 2, D]), C(65024, [128, 2, D])]
        self.wstage = [C(56832, [128, 2 * D]), C(65024, [128, 2 * D])]
        self.xres = C(73216, [128, 2, D])
        self.gsta = C(81408, [128, 2, D])
        self.xn2b = C(89600, [128, 2, D], BF16)
        self.xn2T = C(93696, [128, DC, TT], BF16)
        self.sq_a = C(93696, [128, D])
        self.xb = [C(97792, [128, DC, TT]), C(105984, [128, DC, TT])]
        self.zt = [C(114176, [128, DC, TT]), C(122368, [128, DC, TT])]
        self.xc = C(130560, [128, DC, TT])
        self.xcf = C(130560, [128, DC * TT])
        self.xcr = C(130560, [128, 2, D])
        self.dAf = C(151040, [128, DC * TT])
        self.dBxf = C(159232, [128, DC * TT])
        self.xactb = C(138752, [128, DC, TT], BF16)
        self.espt = C(142848, [128, DC, TT])
        self.espf = C(142848, [128, 2048])
        self.esp2 = C(142848, [128, 2, D])
        self.sq_a3 = C(89600, [128, 2, D])
        self.dA = C(151040, [128, DC, TT])
        self.dBx = C(159232, [128, DC, TT])
        self.gated = C(167424, [128, DC, TT], BF16)
        self.mixsb = C(171520, [128, 2, D])
        self.tsmall = C(171520, [128, 1024], BF16)
        self.w1qT = C(0, [128, DC, H], BF16)
        self.w2qT = C(0, [128, HC, D], BF16)
        self.hlnF = [C(65536, [128, 2, D]), C(73728, [128, 2, D])]
        self.k1tok = [C(81920, [128, 2, D], BF16),
                      C(86016, [128, 2, D], BF16)]
        self.k1Tw = [C(90112, [128, DC, 2 * TT], BF16),
                     C(98304, [128, DC, 2 * TT], BF16)]
        self.gsb = [C(106496, [128, 4, 2 * TT]),
                    C(114688, [128, 4, 2 * TT])]
        self.w2s = [C(122880, [128, D]), C(126976, [128, D])]
        self.w2tb = [C(131072, [128, D], BF16), C(133120, [128, D], BF16)]
        self.w2x = [C(135168, [128, DC, 128], BF16),
                    C(137216, [128, DC, 128], BF16)]
        self.w1s = [C(139264, [128, D]), C(143360, [128, D])]
        self.q2rd = [C(139264, [128, 4, 2 * TT]),
                     C(147456, [128, 4, 2 * TT])]
        self.w1tb = [C(147456, [128, D], BF16), C(149504, [128, D], BF16)]
        self.w1x = [C(151552, [128, DC, 128], BF16),
                    C(153600, [128, DC, 128], BF16)]
        self.selscr = C(155648, [128, 2048])
        self.q1scr = C(106496, [128, 2048])
        self.slots1 = C(53248, [128, Q1_SLOTS])
        self.slots2 = C(167936, [128, Q2_SLOTS])
        self.x2F = [C(65536, [128, 2, D]), C(73728, [128, 2, D])]
        self.outsb = [C(81920, [128, 2, D]), C(90112, [128, 2, D])]
        self.gF = [C(98304, [128, 4, TT]), C(102400, [128, 4, TT])]
        self.k2 = [C(106496, [128, 4, TT], BF16),
                   C(108544, [128, 4, TT], BF16)]
        assert AW * 4 >= 180224, AW * 4
        self.pb = [nc.alloc_psum_tensor(f"pb{i}", [128, 512], F32)
                   for i in range(6)]
        self.ptr = [nc.alloc_psum_tensor(f"ptr{i}", [128, 1024], BF16)
                    for i in range(2)]
        self.sbuf_left = nc.sbuf_bytes_remaining

    def lad(self, j):
        return self.lads[0:1, j:j + 1]

    def dbg_mark(self, k):
        if not self.debug or self.dbg_idx >= 319:
            return
        i = self.dbg_idx
        self.dbg_idx += 1
        self.mset("vector", self.dbgrow[0:1, i:i + 1], float(77000 + k),
                  outs=[self.buf("dbgrow")])

    def dbg(self, src11, n=1):
        if not self.debug:
            return
        i = self.dbg_idx
        if i + n > 320:
            return
        self.dbg_idx += n
        self.cp("vector", self.dbgrow[0:1, i:i + n], src11,
                ins=[self.buf("lads"), self.buf("srow"), self.buf("cnt")],
                outs=[self.buf("dbgrow")])

    def phase_barrier(self, arena_keys):
        toks = self.fw.barrier_tokens()
        for k in arena_keys:
            self.buf(k).readers = list(self.buf(k).readers) + toks

    def bc_pp(self, src11, dst_ap, dst_key):
        p = self.pb[5]
        self.mm(p[:, 0:1], self.onesrow[:], src11, True, True,
                ins=[self.buf("onesrow"), self.buf("lads")],
                outs=[self.buf("pb5")])
        self.cp("vector", dst_ap, p[:, 0:1], ins=[self.buf("pb5")],
                outs=[self.buf(dst_key)])

    def bc_row(self, src_row, k, dst_ap, dst_key, src_key):
        p = self.pb[5]
        self.mm(p[:, 0:k], self.onesrow[:], src_row, True, True,
                ins=[self.buf("onesrow"), self.buf(src_key)],
                outs=[self.buf("pb5")])
        self.cp("vector", dst_ap, p[:, 0:k], ins=[self.buf("pb5")],
                outs=[self.buf(dst_key)])

    def col_total(self, cols_ap, k, out_row, src_key, out_key="lads"):
        p = self.pb[5]
        self.mm(p[0:1, 0:k], self.onescol[:], cols_ap, True, True,
                ins=[self.buf("onescol"), self.buf(src_key)],
                outs=[self.buf("pb5")])
        self.cp("vector", out_row, p[0:1, 0:k], ins=[self.buf("pb5")],
                outs=[self.buf(out_key)])

    def part_max_to_scalar(self, col_ap, out11, src_key):
        fw, buf = self.fw, self.buf
        self._bounce_n = getattr(self, "_bounce_n", 0) + 1
        fw.dma("gpsimd",
               self.row_sp.ap()[0:1, 0:128].rearrange("a (p s) -> (a p) s",
                                                      p=128),
               col_ap, ins=[buf(src_key), buf("row_sp")],
               outs=[buf("row_sp")], sem=f"bounce{self._bounce_n}")
        fw.dma("sync", self.srow[0:1, 0:128], self.row_sp.ap()[0:1, 0:128],
               ins=[buf("row_sp")], outs=[buf("srow")],
               sem=f"bounce{self._bounce_n}b")
        self.red("vector", out11, self.srow[0:1, 0:128], ALU.max,
                 ins=[buf("srow")], outs=[buf("lads")])

    def init_phase(self):
        fw, buf = self.fw, self.buf
        dma = fw.dma
        dma("sync", self.identb[:], self.ident_d.ap(), outs=[buf("identb")],
            sem="l_i")
        dma("sync", self.vec[:],
            self.vecs_d.ap().rearrange("p (c n) -> p c n", c=DC),
            outs=[buf("vec")], sem="l_v")
        dma("sync", self.lng_bc[:], self.lngb_d.ap()[:, 0:D],
            outs=[buf("lng_bc")], sem="l_g")
        dma("sync", self.lnb_bc[:], self.lngb_d.ap()[:, D:2 * D],
            outs=[buf("lnb_bc")], sem="l_b")
        dma("sync", self.rows[:], self.rows_d.ap(), outs=[buf("rows_sb")],
            sem="l_r")
        for t, v in [("onescol", 1.0), ("onesrow", 1.0), ("eps_t", EPS),
                     ("one_t", 1.0)]:
            self.mset("vector", getattr(self, t)[:], v, outs=[buf(t)])
        for t in ("cnt", "cnthi", "stats", "lads", "accs", "hstate", "zcol",
                  "xblast", "qwin", "s_pp", "scp", "candx", "gath", "qsub",
                  "srow", "lrow", "thr", "dbgrow"):
            self.mset("vector", getattr(self, t)[:], 0.0, outs=[buf(t)])
        self.fw.op("gpsimd", lambda g: g.iota(
            self.iota16[:], [[1, 16]], base=0, channel_multiplier=0,
            allow_small_or_imprecise_dtypes=True), outs=[buf("iota16")])

        self._ldn = 0

        def ldcast(dst_ap, dst_key, src_ap, rows, cols, p0=0):
            w = self._ldn % 2
            self._ldn += 1
            st = self.wstage[w]
            dma("sync", st[p0:p0 + rows, 0:cols], src_ap,
                ins=[buf(f"wstage{w}")], outs=[buf(f"wstage{w}")],
                sem=f"l_w{w}")
            self.cp("vector", dst_ap, st[p0:p0 + rows, 0:cols],
                    ins=[buf(f"wstage{w}")], outs=[buf(dst_key)])

        self.mset("vector", self.wgateb[:, :, :], 0.0,
                  outs=[buf("wgateb")])
        for c in range(DC):
            r = slice(c * 128, c * 128 + 128)
            ldcast(self.winb[:, c, :], "winb", self.win_d.ap()[r, :], 128,
                   2 * D)
            ldcast(self.woutb[:, c, :], "woutb", self.wout_d.ap()[r, :],
                   128, D)
            ldcast(self.wgateb[:, c, 64:128], "wgateb",
                   self.wxp_d.ap()[r, :], 128, RK)
            ldcast(self.wgateb[:, c, 0:DS], "wgateb",
                   self.wbg_d.ap()[r, :], 128, DS)
            ldcast(self.wgateb[:, c, 32:32 + DS], "wgateb",
                   self.wcg_d.ap()[r, :], 128, DS)
        ldcast(self.wdtb[64:128, :], "wdtb", self.wdt_d.ap(), 64, D, p0=64)
        ldcast(self.wbpb[0:16, :], "wbpb", self.wbp_d.ap(), 16, D)
        ldcast(self.wcpb[32:48, :], "wcpb", self.wcp_d.ap(), 16, D, p0=32)

        self.mset("vector", self.accs[:, 8:10], 0.0, outs=[buf("accs")])
        for c in range(HC):
            w = c % 2
            st = self.wstage[w]
            dma("sync", st[:, 0:D], self.w1_d.ap()[c * 128:(c + 1) * 128, :],
                ins=[buf(f"wstage{w}")], outs=[buf(f"wstage{w}")],
                sem=f"l_w1{w}")
            self.red("vector", self.accs[:, w:w + 1], st[:, 0:D], ALU.add,
                     absval=True, ins=[buf(f"wstage{w}")],
                     outs=[buf("accs")])
            self.tt("vector", self.accs[:, 8:9], self.accs[:, 8:9],
                    self.accs[:, w:w + 1], ALU.add, ins=[buf("accs")],
                    outs=[buf("accs")])
        for c in range(DC):
            for q in range(2):
                w = q % 2
                st = self.wstage[w]
                dma("sync", st[:, 0:2 * D],
                    self.w2_d.ap()[c * 128:(c + 1) * 128,
                                   q * 2 * D:(q + 1) * 2 * D],
                    ins=[buf(f"wstage{w}")], outs=[buf(f"wstage{w}")],
                    sem=f"l_w2{w}")
                self.red("vector", self.accs[:, w:w + 1], st[:, 0:2 * D],
                         ALU.add, absval=True, ins=[buf(f"wstage{w}")],
                         outs=[buf("accs")])
                self.tt("vector", self.accs[:, 9:10], self.accs[:, 9:10],
                        self.accs[:, w:w + 1], ALU.add, ins=[buf("accs")],
                        outs=[buf("accs")])
        self.col_total(self.accs[:, 8:10], 2, self.lads[0:1, 4:6], "accs")
        self.ts("vector", self.lad(0), self.lads[0:1, 4:5],
                float(1.0 / (H * D)), EPS, ALU.mult, ALU.add,
                ins=[buf("lads")], outs=[buf("lads")])
        self.ts("vector", self.lad(2), self.lads[0:1, 5:6],
                float(1.0 / (H * D)), EPS, ALU.mult, ALU.add,
                ins=[buf("lads")], outs=[buf("lads")])
        self.rcp(self.lad(1), self.lad(0), ins=[buf("lads")],
                 outs=[buf("lads")])
        self.rcp(self.lad(3), self.lad(2), ins=[buf("lads")],
                 outs=[buf("lads")])
        self.bc_pp(self.lad(1), self.scp[:, 0:1], "scp")
        self.bc_pp(self.lad(3), self.scp[:, 1:2], "scp")
        self.phase_barrier(["xt0", "xt1"])

    def ln_stats2(self, v3, bv, scr, bscrs, st, bst):
        for tc in range(2):
            self.red("vector", st[:, tc:tc + 1], v3[:, tc, :], ALU.add,
                     ins=[bv], outs=[bst])
            self.act(scr[:, tc, :], v3[:, tc, :], AF.Square, ins=[bv],
                     outs=list(bscrs) + [bst], accum=st[:, 2 + tc:3 + tc])
        self.ts("vector", st[:, 4:6], st[:, 0:2], INV_D, None, ALU.mult,
                ins=[bst], outs=[bst])
        self.ts("vector", st[:, 6:8], st[:, 2:4], INV_D, None, ALU.mult,
                ins=[bst], outs=[bst])
        self.tt("vector", st[:, 8:10], st[:, 4:6], st[:, 4:6], ALU.mult,
                ins=[bst], outs=[bst])
        self.tt("vector", st[:, 6:8], st[:, 6:8], st[:, 8:10], ALU.subtract,
                ins=[bst], outs=[bst])
        self.act(st[:, 8:10], st[:, 6:8], AF.Ln, bias=self.eps_t[:, 0:1],
                 ins=[bst, self.buf("eps_t")], outs=[bst])
        self.act(st[:, 10:12], st[:, 8:10], AF.Exp, scale=-0.5, ins=[bst],
                 outs=[bst])

    def ln_stats(self, v, bv, tc, scr, bscrs, st=None, bst=None):
        if st is None:
            st, bst = self.stats, self.buf("stats")
        self.red("vector", st[:, tc:tc + 1], v, ALU.add, ins=[bv],
                 outs=[bst])
        self.act(scr, v, AF.Square, ins=[bv], outs=list(bscrs) + [bst],
                 accum=st[:, 2 + tc:3 + tc])
        self.ts("vector", st[:, 4 + tc:5 + tc], st[:, tc:tc + 1], INV_D,
                None, ALU.mult, ins=[bst], outs=[bst])
        self.ts("vector", st[:, 6 + tc:7 + tc], st[:, 2 + tc:3 + tc], INV_D,
                None, ALU.mult, ins=[bst], outs=[bst])
        self.tt("vector", st[:, 8 + tc:9 + tc], st[:, 4 + tc:5 + tc],
                st[:, 4 + tc:5 + tc], ALU.mult, ins=[bst], outs=[bst])
        self.tt("vector", st[:, 6 + tc:7 + tc], st[:, 6 + tc:7 + tc],
                st[:, 8 + tc:9 + tc], ALU.subtract, ins=[bst], outs=[bst])
        self.act(st[:, 8 + tc:9 + tc], st[:, 6 + tc:7 + tc], AF.Ln,
                 bias=self.eps_t[:, 0:1], ins=[bst, self.buf("eps_t")],
                 outs=[bst])
        self.act(st[:, 10 + tc:11 + tc], st[:, 8 + tc:9 + tc], AF.Exp,
                 scale=-0.5, ins=[bst], outs=[bst])

    def stage_a(self, i):
        fw, buf = self.fw, self.buf
        ii = i % 2
        xt, bxt = self.xt[ii], buf(f"xt{ii}")
        r0 = i * TT
        fw.dma("sync", xt[:],
               self.x_in.ap()[r0:r0 + TT, :].rearrange(
                   "(tc p) d -> p tc d", p=128),
               ins=[bxt], outs=[bxt], sem=f"l_x{ii}")
        st, bst = self.stats, buf("stats")
        self.ln_stats2(xt, bxt, self.sq_a3, [buf("xn2b"), buf("xn2a")],
                       st, bst)
        r4 = (i % 4) * 4
        self.cp("vector", self.statR[:, r4:r4 + 2], st[:, 4:6],
                ins=[bst], outs=[buf("statR")])
        self.cp("vector", self.statR[:, r4 + 2:r4 + 4], st[:, 10:12],
                ins=[bst], outs=[buf("statR")])
        for tc in range(2):
            self.ts("vector", self.xn2b[:, tc, :], xt[:, tc, :],
                    st[:, 4 + tc:5 + tc],
                    st[:, 10 + tc:11 + tc], ALU.subtract, ALU.mult,
                    ins=[bxt, bst], outs=[buf("xn2b")])
        for tc in range(2):
            for dc in range(DC):
                pt = self.ptr[dc % 2]
                bpt = buf(f"ptr{dc % 2}")
                self.tr(pt[:, 0:128],
                        self.xn2b[:, tc, dc * 128:(dc + 1) * 128],
                        ins=[buf("xn2b")], outs=[bpt])
                self.ts("vector", self.xn2T[:, dc, tc * 128:(tc + 1) * 128],
                        pt[:, 0:128], self.vec[:, dc, V_GAM:V_GAM + 1],
                        self.vec[:, dc, V_BET:V_BET + 1], ALU.mult, ALU.add,
                        ins=[bpt, buf("vec")], outs=[buf("xn2a")])
    def stage_a_mm(self, i):
        buf = self.buf
        ii = i % 2
        xb, z = self.xb[ii], self.zt[ii]
        bxb, bz = buf(f"xb{ii}"), buf(f"zt{ii}")
        for q in range(16):
            pbk, bp = self.pb[q % 2], buf(f"pb{q % 2}")
            for k in range(DC):
                self.mm(pbk[:, 0:TT], self.winb[:, k, q * 128:(q + 1) * 128],
                        self.xn2T[:, k, :], k == 0, k == DC - 1,
                        ins=[buf("winb"), buf("xn2a")], outs=[bp])
            dst = xb[:, q, :] if q < 8 else z[:, q - 8, :]
            dk = bxb if q < 8 else bz
            self.act(dst, pbk[:, 0:TT], AF.Copy, ins=[bp], outs=[dk])

    def stage_b_front_a(self, i):
        fw, buf = self.fw, self.buf
        ii = i % 2
        seqpos = i % 8
        xb, z = self.xb[ii], self.zt[ii]
        bxb, bz = buf(f"xb{ii}"), buf(f"zt{ii}")
        vec, bvec = self.vec, buf("vec")
        xc, bxc = self.xc, buf("xc")
        xbf = xb.rearrange("p a b -> p (a b)")
        NW = DC * TT

        def bc(col):
            return vec[:, :, col:col + 1].broadcast_to([128, DC, TT])

        bdA, bdBx = buf("dA"), buf("dBx")
        self.tt("vector", xc[:, :, :], xb[:, :, :], bc(V_W1), ALU.mult,
                ins=[bxb, bvec], outs=[bxc])
        self.tt("vector", xc[:, :, :], xc[:, :, :], bc(V_CB), ALU.add,
                ins=[bvec, bxc], outs=[bxc])
        self.tt("gpsimd", self.dA[:, :, :], xb[:, :, :], bc(V_W0), ALU.mult,
                ins=[bxb, bvec], outs=[bdA])
        self.tt("vector", self.xcf[:, 1:NW], self.xcf[:, 1:NW],
                self.dAf[:, 0:NW - 1], ALU.add, ins=[bdA, bxc], outs=[bxc])
        self.tt("gpsimd", self.dBx[:, :, :], xb[:, :, :], bc(V_W2), ALU.mult,
                ins=[bxb, bvec], outs=[bdBx])
        self.tt("vector", self.xcf[:, 0:NW - 1], self.xcf[:, 0:NW - 1],
                self.dBxf[:, 1:NW], ALU.add, ins=[bdBx, bxc], outs=[bxc])
        bac = buf("accs")
        self.tt("vector", self.accs[:, 16:23],
                xb[:, 0:7, TT - 1:TT].rearrange("p c o -> p (c o)"),
                vec[:, 1:8, V_W0], ALU.mult, ins=[bxb, bvec], outs=[bac])
        self.tt("vector", xc[:, 1:8, 0:1], xc[:, 1:8, 0:1],
                self.accs[:, 16:23].rearrange("p (c o) -> p c o", o=1),
                ALU.subtract, ins=[bac, bxc], outs=[bxc])
        self.tt("vector", self.accs[:, 16:23],
                xb[:, 1:8, 0:1].rearrange("p c o -> p (c o)"),
                vec[:, 0:7, V_W2], ALU.mult, ins=[bxb, bvec], outs=[bac])
        self.tt("vector", xc[:, 0:7, TT - 1:TT], xc[:, 0:7, TT - 1:TT],
                self.accs[:, 16:23].rearrange("p (c o) -> p c o", o=1),
                ALU.subtract, ins=[bac, bxc], outs=[bxc])
        if seqpos > 0:
            self.tt("vector", self.accs[:, 16:24], self.xblast[:],
                    vec[:, :, V_W0], ALU.mult, ins=[buf("xblast"), bvec],
                    outs=[bac])
            self.tt("vector", xc[:, :, 0:1], xc[:, :, 0:1],
                    self.accs[:, 16:24].rearrange("p (c o) -> p c o", o=1),
                    ALU.add, ins=[bac, bxc], outs=[bxc])
        self.cp("vector", self.xblast[:],
                xb[:, :, TT - 1:TT].rearrange("p c o -> p (c o)"),
                ins=[bxb], outs=[buf("xblast")])

    def stage_b_front_a2(self, i):
        fw, buf = self.fw, self.buf
        ii = i % 2
        seqpos = i % 8
        xb, z = self.xb[ii], self.zt[ii]
        bxb, bz = buf(f"xb{ii}"), buf(f"zt{ii}")
        vec, bvec = self.vec, buf("vec")
        xc, bxc = self.xc, buf("xc")
        bac = buf("accs")
        if seqpos < 7:
            nxt = self.xb[(i + 1) % 2]
            bnxt = buf(f"xb{(i + 1) % 2}")
            self.tt("vector", self.accs[:, 16:24],
                    nxt[:, :, 0:1].rearrange("p c o -> p (c o)"),
                    vec[:, :, V_W2], ALU.mult, ins=[bnxt, bvec], outs=[bac])
            self.tt("vector", xc[:, :, TT - 1:TT], xc[:, :, TT - 1:TT],
                    self.accs[:, 16:24].rearrange("p (c o) -> p c o", o=1),
                    ALU.add, ins=[bac, bxc], outs=[bxc])
        self.act(xc[:, :, :], xc[:, :, :], AF.Silu, ins=[bxc], outs=[bxc])
        self.act(z[:, :, :], z[:, :, :], AF.Silu, ins=[bz], outs=[bz])
        self.cp("vector", self.xactb[:, :, :], xc[:, :, :], ins=[bxc],
                outs=[buf("xactb")])

    def stage_b_front_b(self, i):
        fw, buf = self.fw, self.buf
        ii = i % 2
        seqpos = i % 8
        xb, z = self.xb[ii], self.zt[ii]
        bxb, bz = buf(f"xb{ii}"), buf(f"zt{ii}")
        vec, bvec = self.vec, buf("vec")
        xc, bxc = self.xc, buf("xc")
        bxab = buf("xactb")
        p2, bp2 = self.pb[2], buf("pb2a")
        btm = buf("mixsb")
        t1b = self.tsmall[64:128, 0:TT]
        tBb = self.tsmall[0:16, 0:TT]
        tCb = self.tsmall[32:48, 0:TT]
        for k in range(DC):
            self.mm(p2[0:128, 0:TT], self.wgateb[:, k, :],
                    self.xactb[:, k, :], k == 0, k == DC - 1,
                    ins=[buf("wgateb"), bxab], outs=[bp2])
        self.cp("vector", self.tsmall[0:128, 0:TT], p2[0:128, 0:TT],
                ins=[bp2], outs=[btm])
        esp, besp = self.espt, buf("espt")
        for dc in range(DC):
            p3, bp3 = self.pb[3], buf("pb3a")
            self.mm(p3[:, 0:TT], self.wdtb[64:128, dc * 128:(dc + 1) * 128],
                    t1b, True, True, ins=[buf("wdtb"), btm], outs=[bp3])
            self.act(esp[:, dc, :], p3[:, 0:TT], AF.Exp,
                     bias=vec[:, dc, V_BDT:V_BDT + 1], ins=[bp3, bvec],
                     outs=[besp])
        self.act(esp[:, :, :], esp[:, :, :], AF.Ln,
                 bias=self.one_t[:, 0:1], ins=[besp, buf("one_t")],
                 outs=[besp])
        dA, bdA = self.dA, buf("dA")
        self.tt("vector", dA[:, :, :], esp[:, :, :],
                vec[:, :, V_A:V_A + 1].broadcast_to([128, DC, TT]),
                ALU.mult, ins=[besp, bvec], outs=[bdA])
        self.act(dA[:, :, :], dA[:, :, :], AF.Exp, ins=[bdA], outs=[bdA])
        dBx, bdBx = self.dBx, buf("dBx")
        if seqpos == 0:
            self.mset("vector", self.hstate[:], 0.0, outs=[buf("hstate")])
        for dc in range(DC):
            p4, bp4 = self.pb[3], buf("pb3b")
            self.mm(p4[:, TT:2 * TT],
                    self.wbpb[0:16, dc * 128:(dc + 1) * 128],
                    tBb, True, True, ins=[buf("wbpb"), btm], outs=[bp4])
            self.tt("vector", dBx[:, dc, :], p4[:, TT:2 * TT],
                    esp[:, dc, :], ALU.mult, ins=[bp4, besp], outs=[bdBx])
        self.tt("gpsimd", dBx[:, :, :], dBx[:, :, :], xc[:, :, :],
                ALU.mult, ins=[bdBx, bxc], outs=[bdBx])
        bac = buf("accs")
        dA0 = dA[:, :, 0:1].rearrange("p c o -> p (c o)")
        dBx0 = dBx[:, :, 0:1].rearrange("p c o -> p (c o)")
        self.tt("vector", self.accs[:, 24:32], dA0, self.hstate[:],
                ALU.mult, ins=[bdA, buf("hstate")], outs=[bac])
        self.tt("vector", dBx0, dBx0, self.accs[:, 24:32], ALU.add,
                ins=[bac, bdBx], outs=[bdBx])
        self.ts("vector", dA0, dA0, 0.0, None, ALU.mult, ins=[bdA],
                outs=[bdA])
        self.fw.op("vector", lambda v: v.tensor_tensor_scan(
            self.dBxf[:, :], self.dAf[:, :], self.dBxf[:, :],
            self.zcol[:, 0:1], ALU.mult, ALU.add),
            ins=[bdA, bdBx, buf("zcol")], outs=[bdBx])
        self.cp("vector", self.hstate[:],
                dBx[:, :, TT - 1:TT].rearrange("p c o -> p (c o)"),
                ins=[bdBx], outs=[buf("hstate")])
        for dc in range(DC):
            p5, bp5 = self.pb[2], buf("pb2b")
            self.mm(p5[:, TT:2 * TT],
                    self.wcpb[32:48, dc * 128:(dc + 1) * 128],
                    tCb, True, True, ins=[buf("wcpb"), btm], outs=[bp5])
            self.tt("vector", dA[:, dc, :], dBx[:, dc, :], p5[:, TT:2 * TT],
                    ALU.mult, ins=[bdBx, bp5], outs=[bdA])
        self.tt("gpsimd", esp[:, :, :], xc[:, :, :],
                vec[:, :, V_DSK:V_DSK + 1].broadcast_to([128, DC, TT]),
                ALU.mult, ins=[bxc, bvec], outs=[besp])
        self.tt("vector", dA[:, :, :], dA[:, :, :], esp[:, :, :], ALU.add,
                ins=[besp, bdA], outs=[bdA])
        self.tt("vector", self.gated[:, :, :], dA[:, :, :], z[:, :, :],
                ALU.mult, ins=[bdA, bz], outs=[buf("gated")])

    def stage_b_tail(self, i):
        fw, buf = self.fw, self.buf
        ii = i % 2
        xt, bxt = self.xres, buf("xres")
        fw.dma("sync", xt[:],
               self.x_in.ap()[i * TT:i * TT + TT, :].rearrange(
                   "(tc p) d -> p tc d", p=128),
               ins=[bxt], outs=[bxt], sem=f"l_xr{i % 2}")
        mix, bmix = self.mixsb, buf("mixsb")
        r4 = (i % 4) * 4
        sR, bsR = self.statR, buf("statR")
        for tc in range(2):
            self.ts("vector", mix[:, tc, :], xt[:, tc, :],
                    sR[:, r4 + tc:r4 + tc + 1],
                    sR[:, r4 + 2 + tc:r4 + 3 + tc],
                    ALU.subtract, ALU.mult, ins=[bxt, bsR], outs=[bmix])
        for tc in range(2):
            for nq in range(2):
                pbk, bp = self.pb[4 + nq], buf(f"pb{4 + nq}")
                for k in range(DC):
                    self.mm(pbk[:, 0:512],
                            self.gated[:, k, tc * 128:(tc + 1) * 128],
                            self.woutb[:, k, nq * 512:(nq + 1) * 512],
                            k == 0, k == DC - 1,
                            ins=[buf("gated"), buf("woutb")], outs=[bp])
                self.tt("vector", mix[:, tc, nq * 512:(nq + 1) * 512],
                        mix[:, tc, nq * 512:(nq + 1) * 512],
                        pbk[:, 0:512], ALU.add, ins=[bp], outs=[bmix])
        st, bst = self.statB, buf("statB")
        hl = self.esp2
        bhl = buf("espt")
        self.ln_stats2(mix, bmix, self.esp2, [buf("espt")], st, bst)
        for tc in range(2):
            self.ts("vector", mix[:, tc, :], mix[:, tc, :],
                    st[:, 4 + tc:5 + tc], st[:, 10 + tc:11 + tc],
                    ALU.subtract, ALU.mult, ins=[bmix, bst], outs=[bmix])
            self.tt("vector", mix[:, tc, :], mix[:, tc, :], xt[:, tc, :],
                    ALU.add, ins=[bmix, bxt], outs=[bmix])
        r0 = i * TT
        fw.dma("gpsimd",
               self.x2_sp.ap()[r0:r0 + TT, :].rearrange(
                   "(tc p) d -> p tc d", p=128),
               mix[:, :, :], ins=[bmix, buf("x2_sp")], outs=[buf("x2_sp")],
               sem=f"s_x2{i % 2}")
        self.ln_stats2(mix, bmix, self.esp2, [buf("espt")], st, bst)
        for tc in range(2):
            self.ts("vector", hl[:, tc, :], mix[:, tc, :],
                    st[:, 4 + tc:5 + tc], st[:, 10 + tc:11 + tc],
                    ALU.subtract, ALU.mult, ins=[bmix, bst], outs=[bhl])
            self.tt("vector", hl[:, tc, :], hl[:, tc, :], self.lng_bc[:],
                    ALU.mult, ins=[bhl, buf("lng_bc")], outs=[bhl])
            self.tt("vector", hl[:, tc, :], hl[:, tc, :], self.lnb_bc[:],
                    ALU.add, ins=[bhl, buf("lnb_bc")], outs=[bhl])
        fw.dma("gpsimd",
               self.hln_sp.ap()[r0:r0 + TT, :].rearrange(
                   "(tc p) d -> p tc d", p=128),
               hl[:, :, :], ins=[bhl, buf("hln_sp")], outs=[buf("hln_sp")],
               sem=f"s_hln{i % 2}")
        if i <= 7:
            self.ladder_tile(hl[:, :, :].rearrange("p a b -> p (a b)"),
                             bhl, i, need_abs=True, subs=32, nq=1,
                             shift=(i % 8))
        else:
            self.capture_q1_sb(
                hl[:, :, :].rearrange("p a b -> p (a b)"), bhl, i)
            rb, rk = self.xres, buf("xres")
            fw.dma("sync", rb[:],
                   self.hln_sp.ap()[(i - 8) * TT:(i - 7) * TT, :].rearrange(
                       "(tc p) d -> p tc d", p=128),
                   ins=[buf("hln_sp"), rk], outs=[rk], sem=f"l_hr{i % 2}")
            self.capture_q1_sb(
                rb.rearrange("p a b -> p (a b)"), rk, i - 8)

    def capture_q1_sb(self, flat, bflat, idx):
        buf = self.buf
        width = 2048
        bdA, bdBx = buf("dA"), buf("dBx")
        self.ts("vector", self.dBxf.bitcast(I32)[:, 0:width],
                flat.bitcast(I32), ABSM, None, ALU.bitwise_and,
                ins=[bflat], outs=[bdBx])
        data = self.dBxf[:, 0:width]
        self.ts("vector", self.dAf[:, 0:width], data, self.qwin[:, 1:2],
                None, ALU.is_gt, ins=[bdBx, buf("qwin")],
                outs=[bdA, buf("accs")], accum=self.accs[:, 30:31])
        self.tt("vector", self.cnthi[:, idx:idx + 1],
                self.cnthi[:, idx:idx + 1], self.accs[:, 30:31], ALU.add,
                ins=[buf("accs")], outs=[buf("cnthi")])
        self.ts("vector", self.dAf[:, 0:width], data, self.qwin[:, 1:2],
                None, ALU.is_le, ins=[bdBx, buf("qwin")], outs=[bdA])
        self.tt("gpsimd", self.dAf[:, 0:width], self.dAf[:, 0:width], data,
                ALU.mult, ins=[bdBx, bdA], outs=[bdA])
        for s in range(4):
            self.max8(
                self.slots1[:, idx * 32 + s * 8:idx * 32 + (s + 1) * 8],
                self.dAf[:, s * 512:(s + 1) * 512],
                ins=[bdA], outs=[buf("slots1c")])

    def ladder_tile(self, flat, bflat, i, need_abs, subs, nq, width=2048,
                    shift=0):
        buf = self.buf
        eng = "vector"
        if i == 0:
            self.red("vector", self.accs[:, 25:26], flat, ALU.max,
                     absval=True, ins=[bflat], outs=[buf("accs")])
            self.tt("vector", self.accs[:, 24:25], self.accs[:, 24:25],
                    self.accs[:, 25:26], ALU.max, ins=[buf("accs")],
                    outs=[buf("accs")])
            return
        nsub = width // subs
        sub = flat[:, shift:shift + width - subs + 1:subs]
        if need_abs:
            self.ts(eng, self.qsub[:, 0:nsub].bitcast(I32),
                    sub.bitcast(I32), ABSM, None, ALU.bitwise_and,
                    ins=[bflat], outs=[buf("qsub")])
            data, bdata = self.qsub[:, 0:nsub], buf("qsub")

        else:
            data, bdata = sub, bflat
        for j in range(NL):
            self.ts(eng, self.qsub[:, 256:256 + nsub], data,
                    self.thr[:, j:j + 1], None, ALU.is_gt,
                    ins=[bdata, buf("thr")],
                    outs=[buf("qsubhi"), buf("accs")],
                    accum=self.accs[:, j:j + 1])
        self.tt(eng, self.cnt[:, 0:NL], self.cnt[:, 0:NL],
                self.accs[:, 0:NL], ALU.add, ins=[buf("accs")],
                outs=[buf("cnt")])

    def ladder_setup_gen1(self, nq):
        buf = self.buf
        self.part_max_to_scalar(self.accs[:, 24:25], self.lad(8), "accs")
        geo = self.rows[0:1, 0:NL]
        self.ts("vector", self.srow[0:1, 0:NL], geo, self.lad(8), None,
                ALU.mult, ins=[buf("rows_sb"), buf("lads")],
                outs=[buf("srow")])
        self.bc_row(self.srow[0:1, 0:NL], NL, self.thr[:, 0:NL], "thr",
                    "srow")
        self.mset("vector", self.cnt[:], 0.0, outs=[buf("cnt")])
        self.mset("vector", self.accs[:, 0:NL], 0.0, outs=[buf("accs")])

    def interp_est(self, rstar, out11):
        buf = self.buf
        L = self.lrow
        bL = buf("lrow")
        self.col_total(self.cnt[:, 0:NL], NL, self.lads[0:1, 32:32 + NL],
                       "cnt")
        ct = self.lads[0:1, 32:32 + NL]
        self.dbg(self.srow[0:1, 0:NL], NL)
        self.dbg(ct, NL)
        bl = buf("lads")
        self.ts("vector", L[0:1, 0:NL], ct, float(rstar), None, ALU.is_ge,
                ins=[bl], outs=[bL])
        self.cp("vector", L[0:1, 32:33], L[0:1, 0:1], ins=[bL], outs=[bL])
        self.ts("vector", L[0:1, 64:64 + NL - 1], L[0:1, 0:NL - 1], -1.0,
                1.0, ALU.mult, ALU.add, ins=[bL], outs=[bL])
        self.tt("vector", L[0:1, 33:33 + NL - 1], L[0:1, 1:NL],
                L[0:1, 64:64 + NL - 1], ALU.mult, ins=[bL], outs=[bL])
        sel = L[0:1, 32:32 + NL]
        self.tt("vector", L[0:1, 64:64 + NL], sel, self.srow[0:1, 0:NL],
                ALU.mult, ins=[bL, buf("srow")], outs=[bL])
        self.red("vector", L[0:1, 96:97], L[0:1, 64:64 + NL], ALU.add,
                 ins=[bL], outs=[bL])
        self.tt("vector", L[0:1, 64:64 + NL], sel, ct, ALU.mult,
                ins=[bL, bl], outs=[bL])
        self.red("vector", L[0:1, 97:98], L[0:1, 64:64 + NL], ALU.add,
                 ins=[bL], outs=[bL])
        self.tt("vector", L[0:1, 64:64 + NL - 1], L[0:1, 33:33 + NL - 1],
                self.srow[0:1, 0:NL - 1], ALU.mult, ins=[bL, buf("srow")],
                outs=[bL])
        self.red("vector", L[0:1, 98:99], L[0:1, 64:64 + NL - 1], ALU.add,
                 ins=[bL], outs=[bL])
        self.tt("vector", L[0:1, 64:64 + NL - 1], L[0:1, 33:33 + NL - 1],
                self.lads[0:1, 32:32 + NL - 1], ALU.mult, ins=[bL, bl],
                outs=[bL])
        self.red("vector", L[0:1, 99:100], L[0:1, 64:64 + NL - 1], ALU.add,
                 ins=[bL], outs=[bL])
        self.ts("vector", L[0:1, 100:101], L[0:1, 97:98], float(rstar),
                None, ALU.subtract, ins=[bL], outs=[bL])
        self.tt("vector", L[0:1, 101:102], L[0:1, 97:98], L[0:1, 99:100],
                ALU.subtract, ins=[bL], outs=[bL])
        self.ts("vector", L[0:1, 101:102], L[0:1, 101:102], 1e-6, None,
                ALU.add, ins=[bL], outs=[bL])
        self.rcp(L[0:1, 101:102], L[0:1, 101:102], ins=[bL], outs=[bL])
        self.tt("vector", L[0:1, 100:101], L[0:1, 100:101], L[0:1, 101:102],
                ALU.mult, ins=[bL], outs=[bL])
        self.tt("vector", L[0:1, 98:99], L[0:1, 98:99], L[0:1, 96:97],
                ALU.subtract, ins=[bL], outs=[bL])
        self.tt("vector", L[0:1, 98:99], L[0:1, 98:99], L[0:1, 100:101],
                ALU.mult, ins=[bL], outs=[bL])
        self.tt("vector", L[0:1, 96:97], L[0:1, 96:97], L[0:1, 98:99],
                ALU.add, ins=[bL], outs=[bL])
        self.red("vector", L[0:1, 102:103], L[0:1, 0:NL], ALU.max,
                 ins=[bL], outs=[bL])
        self.tt("vector", L[0:1, 96:97], L[0:1, 96:97], L[0:1, 102:103],
                ALU.mult, ins=[bL], outs=[bL])
        self.ts("vector", L[0:1, 103:104], L[0:1, 102:103], -1.0, 1.0,
                ALU.mult, ALU.add, ins=[bL], outs=[bL])
        self.tt("vector", L[0:1, 103:104], L[0:1, 103:104],
                self.srow[0:1, NL - 1:NL], ALU.mult, ins=[bL, buf("srow")],
                outs=[bL])
        self.tt("vector", out11, L[0:1, 96:97], L[0:1, 103:104], ALU.add,
                ins=[bL], outs=[buf("lads")])

    def ladder_setup_gen2(self, est11, nq):
        buf = self.buf
        lin = self.rows[0:1, NL:2 * NL]
        self.ts("vector", self.srow[0:1, 0:NL], lin, est11, est11,
                ALU.mult, ALU.add, ins=[buf("rows_sb"), buf("lads")],
                outs=[buf("srow")])
        self.bc_row(self.srow[0:1, 0:NL], NL, self.thr[:, 0:NL], "thr",
                    "srow")
        self.mset("vector", self.cnt[:], 0.0, outs=[buf("cnt")])
        self.mset("vector", self.accs[:, 0:NL], 0.0, outs=[buf("accs")])

    def capture_tile(self, flat, bflat, width, spans, cnthi_col, slots,
                     bslots, slot_off, scrt, bscr):
        buf = self.buf
        data, bdata = flat, bflat
        self.ts("vector", scrt[:, 0:width], data, self.qwin[:, 1:2],
                None, ALU.is_gt, ins=[bdata, buf("qwin")],
                outs=[bscr, buf("accs")],
                accum=self.accs[:, 30:31])
        self.tt("vector", self.cnthi[:, cnthi_col:cnthi_col + 1],
                self.cnthi[:, cnthi_col:cnthi_col + 1],
                self.accs[:, 30:31], ALU.add, ins=[buf("accs")],
                outs=[buf("cnthi")])
        self.ts("vector", scrt[:, 0:width], data, self.qwin[:, 1:2],
                None, ALU.is_le, ins=[bdata, buf("qwin")], outs=[bscr])
        self.tt("gpsimd", scrt[:, 0:width], scrt[:, 0:width],
                data, ALU.mult, ins=[bdata, bscr], outs=[bscr])
        for s in range(width // spans):
            self.max8(slots[:, slot_off + s * 8:slot_off + (s + 1) * 8],
                      scrt[:, s * spans:(s + 1) * spans],
                      ins=[bscr], outs=[bslots])

    def rank_consts(self, n_const, jl_out, jh_out, hw_out):
        buf = self.buf
        bl = buf("lads")
        L = self.lrow
        bL = buf("lrow")
        self.mset("vector", L[0:1, 110:111], float(n_const), outs=[bL])
        self.mset("vector", L[0:1, 111:112], 0.995, outs=[bL])
        self.ts("vector", L[0:1, 112:113], L[0:1, 110:111], 1.0, None,
                ALU.subtract, ins=[bL], outs=[bL])
        self.tt("vector", L[0:1, 113:114], L[0:1, 111:112], L[0:1, 112:113],
                ALU.mult, ins=[bL], outs=[bL])
        self.cp("vector", self.i32s[0:1, 0:1], L[0:1, 113:114], ins=[bL],
                outs=[self.buf("i32s")])
        self.cp("vector", L[0:1, 115:116], self.i32s[0:1, 0:1],
                ins=[self.buf("i32s")], outs=[bL])
        self.ts("vector", L[0:1, 114:115], L[0:1, 113:114], 0.0, None,
                ALU.max, ins=[bL], outs=[bL])
        self.tt("vector", L[0:1, 114:115], L[0:1, 113:114], L[0:1, 115:116],
                ALU.subtract, ins=[bL], outs=[bL])
        self.ts("vector", L[0:1, 114:115], L[0:1, 114:115], 0.0, 0.9999,
                ALU.max, ALU.min, ins=[bL], outs=[bL])
        self.tt("vector", L[0:1, 115:116], L[0:1, 113:114], L[0:1, 114:115],
                ALU.subtract, ins=[bL], outs=[bL])
        self.tt("vector", L[0:1, 116:117], L[0:1, 110:111], L[0:1, 115:116],
                ALU.subtract, ins=[bL], outs=[bL])
        self.ts("vector", jl_out, L[0:1, 116:117], 1.0, None, ALU.subtract,
                ins=[bL], outs=[bl])
        self.ts("vector", L[0:1, 117:118], L[0:1, 114:115], 0.0, None,
                ALU.is_gt, ins=[bL], outs=[bL])
        self.tt("vector", jh_out, jl_out, L[0:1, 117:118], ALU.subtract,
                ins=[bL, bl], outs=[bl])
        self.cp("vector", hw_out, L[0:1, 114:115], ins=[bL], outs=[bl])

    def gap_pick(self, glob_row, thr_row, tgt11, lo_out, hi_out, chi_out,
                 npt):
        buf = self.buf
        L, bL = self.lrow, buf("lrow")
        bl = buf("lads")
        self.ts("vector", L[0:1, 220:221], tgt11, 1.0, None, ALU.add,
                ins=[bl], outs=[bL])
        self.fw.op("vector", lambda v: v.tensor_scalar(
            L[0:1, 128:128 + npt], glob_row, L[0:1, 220:221], None,
            ALU.is_ge), ins=[bL, buf("srow")], outs=[bL])
        self._gp_n2 = getattr(self, "_gp_n2", -1) + 1
        if False and self.debug and self._gp_n2 == 9:
            self.dbg_mark(1)
            self.dbg(L[0:1, 128:128 + npt], npt)
            self.dbg(glob_row, npt)
            self.dbg(L[0:1, 220:221])
        self.ts("vector", L[0:1, 150:150 + npt - 1],
                L[0:1, 129:128 + npt], -1.0, 1.0, ALU.mult, ALU.add,
                ins=[bL], outs=[bL])
        self.tt("vector", L[0:1, 172:172 + npt - 1],
                L[0:1, 128:128 + npt - 1], L[0:1, 150:150 + npt - 1],
                ALU.mult, ins=[bL], outs=[bL])
        self.cp("vector", L[0:1, 172 + npt - 1:172 + npt],
                L[0:1, 128 + npt - 1:128 + npt], ins=[bL], outs=[bL])
        sel = L[0:1, 172:172 + npt]
        self.tt("vector", L[0:1, 195:195 + npt], sel, thr_row, ALU.mult,
                ins=[bL, buf("srow")], outs=[bL])
        self.red("vector", lo_out, L[0:1, 195:195 + npt], ALU.add,
                 ins=[bL], outs=[bl])
        self.tt("vector", L[0:1, 195:195 + npt - 1],
                L[0:1, 172:172 + npt - 1], thr_row[0:1, 1:npt], ALU.mult,
                ins=[bL, buf("srow")], outs=[bL])
        self.red("vector", hi_out, L[0:1, 195:195 + npt - 1], ALU.add,
                 ins=[bL], outs=[bl])
        self.tt("vector", L[0:1, 195:195 + npt - 1],
                L[0:1, 172:172 + npt - 1], glob_row[0:1, 1:npt], ALU.mult,
                ins=[bL, buf("srow")], outs=[bL])
        self.red("vector", chi_out, L[0:1, 195:195 + npt - 1], ALU.add,
                 ins=[bL], outs=[bl])
        self._gp_n = getattr(self, "_gp_n", -1) + 1
        if False and self.debug and self._gp_n in (8, 9) and self.dbg_idx < 230:
            self.dbg_mark(2 + self._gp_n)
            self.dbg(L[0:1, 220:221])
            self.dbg(L[0:1, 128:128 + npt], npt)
            self.dbg(glob_row, npt)
            self.dbg(thr_row, npt)
            self.dbg(sel, npt)

    def phase_barrier_all(self):
        toks = self.fw.barrier_tokens()
        self._barrier_toks = toks
        for b in self._bufs.values():
            b.readers = list(b.readers) + toks

    def window_from_est(self, est11, win):
        buf = self.buf
        self.ts("vector", self.lad(10), est11, float(1.0 - win), None,
                ALU.mult, ins=[buf("lads")], outs=[buf("lads")])
        self.ts("vector", self.lad(11), est11, float(1.0 + win), None,
                ALU.mult, ins=[buf("lads")], outs=[buf("lads")])
        self.bc_pp(self.lad(10), self.qwin[:, 0:1], "qwin")
        self.bc_pp(self.lad(11), self.qwin[:, 1:2], "qwin")

    def w1_prep_chunk(self, c):
        fw, buf = self.fw, self.buf
        par = c % 2
        st, bst = self.w1s[par], buf(f"w1s{par}")
        tb, btb = self.w1tb[par], buf(f"w1t{par}")
        xb_, bxb_ = self.w1x[par], buf(f"w1x{par}")
        fw.dma("sync", st[:, :], self.w1_d.ap()[c * 128:(c + 1) * 128, :],
               ins=[bst], outs=[bst], sem=f"l_w1p{par}")
        self.ts("vector", st[:, :], st[:, :], self.scp[:, 0:1], MAGIC,
                ALU.mult, ALU.add, ins=[bst, buf("scp")], outs=[bst])
        self.ts("vector", st[:, :], st[:, :], MAGIC, -1.0, ALU.subtract,
                ALU.max, ins=[bst], outs=[bst])
        self.ts("vector", tb[:, :], st[:, :], 1.0, None, ALU.min,
                ins=[bst], outs=[btb])
        for dc in range(DC):
            pt = self.ptr[dc % 2]
            bpt = buf(f"ptr{dc % 2}")
            self.tr(pt[:, 0:128], tb[:, dc * 128:(dc + 1) * 128],
                    ins=[btb], outs=[bpt])
            self.cp("vector", xb_[:, dc, :], pt[:, 0:128], ins=[bpt],
                    outs=[bxb_])
        fw.dma("gpsimd",
               self.w1q_sp.ap()[:, :, c * 128:(c + 1) * 128].rearrange(
                   "dc p h -> p dc h"),
               xb_[:], ins=[bxb_, buf("w1q_sp")], outs=[buf("w1q_sp")],
               sem=f"s_w1q{par}")

    def w1_finish(self):
        for dc in range(DC):
            self.fw.dma("sync", self.w1qT[:, dc, :],
                        self.w1q_sp.ap()[dc, :, :],
                        ins=[self.buf("w1q_sp"), self.buf("w1qT")],
                        outs=[self.buf("w1qT"), self.buf("slots1c")],
                        sem="l_w1q")

    def w2_prep_chunk(self, c):
        fw, buf = self.fw, self.buf
        for qt in range(4):
            par = (c * 4 + qt) % 2
            st, bst = self.w2s[par], buf(f"w2s{par}")
            tb, btb = self.w2tb[par], buf(f"w2tb{par}")
            xb_, bxb_ = self.w2x[par], buf(f"w2x{par}")
            fw.dma("sync", st[:, :],
                   self.w2_d.ap()[c * 128:(c + 1) * 128,
                                  qt * 1024:(qt + 1) * 1024],
                   ins=[bst], outs=[bst], sem=f"l_w2p{par}")
            self.ts("vector", st[:, :], st[:, :], self.scp[:, 1:2], MAGIC,
                    ALU.mult, ALU.add, ins=[bst, buf("scp")], outs=[bst])
            self.ts("vector", st[:, :], st[:, :], MAGIC, -1.0,
                    ALU.subtract, ALU.max, ins=[bst], outs=[bst])
            self.ts("vector", tb[:, :], st[:, :], 1.0, None, ALU.min,
                    ins=[bst], outs=[btb])
            for blk in range(8):
                pt = self.ptr[blk % 2]
                bpt = buf(f"ptr{blk % 2}")
                self.tr(pt[:, 0:128], tb[:, blk * 128:(blk + 1) * 128],
                        ins=[btb], outs=[bpt])
                self.cp("vector", xb_[:, blk, :], pt[:, 0:128], ins=[bpt],
                        outs=[bxb_])
            fw.dma("gpsimd",
                   self.w2q_sp.ap()[qt * 8:(qt + 1) * 8, :,
                                    c * 128:(c + 1) * 128].rearrange(
                       "hc p d -> p hc d"),
                   xb_[:], ins=[bxb_, buf("w2q_sp")], outs=[buf("w2q_sp")],
                   sem=f"s_w2q{par}")

    def w2_finish(self):
        for hc in range(HC):
            self.fw.dma("scalar", self.w2qT[:, hc, :],
                        self.w2q_sp.ap()[hc, :, :],
                        ins=[self.buf("w2q_sp")],
                        outs=[self.buf("w1qT"), self.buf("w2qT")],
                        sem="l_w2q")

    def q1_window_phase(self):
        for c in range(HC):
            self.w1_prep_chunk(c)

    def capture_q1_tile(self, flat, bflat, i):
        buf = self.buf
        width = 2048
        self.ts("vector", self.selscr[:, 0:width].bitcast(I32)[:, 0:width],
                flat.bitcast(I32), ABSM, None, ALU.bitwise_and,
                ins=[bflat], outs=[buf("selscr")])
        data, bdata = self.selscr[:, 0:width], buf("selscr")
        scr = self.q1scr
        bscr = buf("q1scr")
        self.ts("vector", scr, data, self.qwin[:, 1:2], None, ALU.is_gt,
                ins=[bdata, buf("qwin")], outs=[bscr, buf("accs")],
                accum=self.accs[:, 30:31])
        self.tt("vector", self.cnthi[:, i:i + 1], self.cnthi[:, i:i + 1],
                self.accs[:, 30:31], ALU.add, ins=[buf("accs")],
                outs=[buf("cnthi")])
        self.ts("vector", scr, data, self.qwin[:, 1:2], None, ALU.is_le,
                ins=[bdata, buf("qwin")], outs=[bscr])
        self.tt("vector", scr, scr, data, ALU.mult, ins=[bdata, bscr],
                outs=[bscr])
        for s in range(8):
            self.max8(self.slots1[:, i * 64 + s * 8:i * 64 + (s + 1) * 8],
                      scr[:, s * 256:(s + 1) * 256],
                      ins=[bscr], outs=[buf("slots1c")])

    def select_quantile(self, qid, slots, bslots, cnthi_lo, cnthi_n, nconst,
                        gamma_lad, s_col, gs_col):
        fw, buf = self.fw, self.buf
        L, bL = self.lrow, buf("lrow")
        bl = buf("lads")
        SR, bsr = self.srow, buf("srow")
        self.red("vector", self.accs[:, 26:27],
                 self.cnthi[:, cnthi_lo:cnthi_lo + cnthi_n], ALU.add,
                 ins=[buf("cnthi")], outs=[buf("accs")])
        self.col_total(self.accs[:, 26:27], 1, self.lad(16), "accs")
        self.rank_consts(nconst, self.lad(13), self.lad(14), self.lad(15))
        self.cp("vector", L[0:1, 0:1], self.lad(16), ins=[bl], outs=[bL])
        self.cp("vector", L[0:1, 1:2], self.lad(10), ins=[bl], outs=[bL])
        self.cp("vector", L[0:1, 2:3], self.lad(11), ins=[bl], outs=[bL])
        fw.dma("gpsimd", self.cc1_src.ap(), L[0:1, 0:64],
               ins=[bL, buf("cc1")], outs=[buf("cc1")], sem=f"c1s{qid}")
        self.coll("AllGather", ALU.bypass, self.cc1_src.ap(),
                  self.cc1_dst.ap(), ins=[buf("cc1")], outs=[buf("cc1d")])
        fw.dma("sync", L[0:1, 0:512],
               self.cc1_dst.ap().rearrange("r k -> (r k)"),
               ins=[buf("cc1d"), bL], outs=[bL], sem=f"c1l{qid}")
        self.red("vector", self.lad(18), L[0:1, 1:512:64], ALU.max,
                 ins=[bL], outs=[bl])
        self.red("vector", self.lad(19), L[0:1, 2:512:64], ALU.min,
                 ins=[bL], outs=[bl])
        self.red("vector", self.lad(12), L[0:1, 0:512:64], ALU.add,
                 ins=[bL], outs=[bl])
        self.tt("vector", self.lad(17), self.lad(19), self.lad(18),
                ALU.subtract, ins=[bl], outs=[bl])
        nslots = slots.shape[-1]
        for rnd in range(2):
            if rnd == 0:
                span, base = self.lad(17), self.lad(18)
            else:
                self.tt("vector", self.lad(36), self.lad(34), self.lad(33),
                        ALU.subtract, ins=[bl], outs=[bl])
                span, base = self.lad(36), self.lad(33)
            self.ts("vector", SR[0:1, 32:32 + NLE],
                    self.rows[0:1, 32:32 + NLE], span, base,
                    ALU.mult, ALU.add, ins=[buf("rows_sb"), bl], outs=[bsr])
            self.bc_row(SR[0:1, 32:32 + NLE], NLE, self.thr[:, NLE:2 * NLE],
                        "thr", "srow")
            for j in range(NLE):
                self.ts("vector", self.selscr[:, 0:nslots], slots,
                        self.thr[:, NLE + j:NLE + j + 1], None, ALU.is_gt,
                        ins=[bslots, buf("thr")], outs=[buf("selscr"),
                                                        buf("accs")],
                        accum=self.accs[:, j:j + 1])
            self.col_total(self.accs[:, 0:NLE], NLE, L[0:1, 300:300 + NLE],
                           "accs", out_key="lrow")
            self.ts("vector", L[0:1, 0:NLE], L[0:1, 300:300 + NLE],
                    self.lad(16), None, ALU.add, ins=[bL, bl], outs=[bL])
            fw.dma("gpsimd", self.cc2_src.ap(), L[0:1, 0:32],
                   ins=[bL, buf("cc2")], outs=[buf("cc2")],
                   sem=f"c2s{qid}{rnd}")
            self.coll("AllGather", ALU.bypass, self.cc2_src.ap(),
                      self.cc2_dst.ap(), ins=[buf("cc2")],
                      outs=[buf("cc2d")])
            fw.dma("sync", L[0:1, 0:256],
                   self.cc2_dst.ap().rearrange("r k -> (r k)"),
                   ins=[buf("cc2d"), bL], outs=[bL], sem=f"c2l{qid}{rnd}")
            self.cp("vector", SR[0:1, 96:96 + NLE], L[0:1, 0:NLE], ins=[bL],
                    outs=[bsr])
            for r in range(1, 8):
                self.tt("vector", SR[0:1, 96:96 + NLE],
                        SR[0:1, 96:96 + NLE],
                        L[0:1, r * 32:r * 32 + NLE], ALU.add, ins=[bL, bsr],
                        outs=[bsr])
            self.gap_pick(SR[0:1, 96:96 + NLE], SR[0:1, 32:32 + NLE],
                          self.lad(13), self.lad(33), self.lad(34),
                          self.lad(35), NLE)
        if self.debug and qid == 2:
            self.dbg_mark(15)
            self.dbg(self.lads[0:1, 20:27], 7)
            self.dbg(self.lads[0:1, 33:36], 3)
            self.dbg(self.srow[0:1, 96:96 + NLE], NLE)
            self.dbg(self.srow[0:1, 32:32 + NLE], NLE)
        self.bc_pp(self.lad(33), self.qwin[:, 2:3], "qwin")
        self.bc_pp(self.lad(34), self.qwin[:, 3:4], "qwin")
        self.stt("vector", self.selscr[:, 0:nslots], slots, self.qwin[:, 2:3],
                 slots, ALU.is_gt, ALU.mult, ins=[bslots, buf("qwin")],
                 outs=[buf("selscr")])
        self.stt("vector", self.selscr[:, 0:nslots], self.selscr[:, 0:nslots],
                 self.qwin[:, 3:4], self.selscr[:, 0:nslots], ALU.is_le,
                 ALU.mult, ins=[buf("qwin")], outs=[buf("selscr")])
        self.max8(self.candx[:, 0:8], self.selscr[:, 0:nslots],
                  ins=[buf("selscr")], outs=[buf("candx")])
        self.mrep(self.selscr[:, 0:nslots], self.candx[:, 0:8],
                  self.selscr[:, 0:nslots], ins=[buf("candx")],
                  outs=[buf("selscr")])
        self.max8(self.candx[:, 8:16], self.selscr[:, 0:nslots],
                  ins=[buf("selscr")], outs=[buf("candx")])
        fw.dma("gpsimd",
               self.cc3_src.ap().rearrange("a (p s) -> (a p) s", p=128),
               self.candx[:],
               ins=[buf("candx"), buf("cc3")], outs=[buf("cc3")],
               sem=f"c3s{qid}")
        self.coll("AllGather", ALU.bypass, self.cc3_src.ap(),
                  self.cc3_dst.ap(), ins=[buf("cc3")], outs=[buf("cc3d")])
        fw.dma("sync", self.gath[:].rearrange("p (r s) -> p r s", r=8),
               self.cc3_dst.ap().rearrange("r (p s) -> p r s", p=128),
               ins=[buf("cc3d"), buf("gath")], outs=[buf("gath")],
               sem=f"c3l{qid}")
        for ti, (tgt, vout) in enumerate([(self.lad(13), self.lad(28))]):
            lo, hi, ca = self.lad(33), self.lad(34), self.lad(35)
            for rnd in range(2):
                self.tt("vector", self.lad(36), hi, lo, ALU.subtract,
                        ins=[bl], outs=[bl])
                self.ts("vector", SR[0:1, 32:32 + NLE],
                        self.rows[0:1, 32:32 + NLE], self.lad(36), lo,
                        ALU.mult, ALU.add, ins=[buf("rows_sb"), bl],
                        outs=[bsr])
                self.bc_row(SR[0:1, 32:32 + NLE], NLE,
                            self.thr[:, NLE:2 * NLE], "thr", "srow")
                for j in range(NLE):
                    self.ts("vector", self.qsub[:, 128:256], self.gath[:],
                            self.thr[:, NLE + j:NLE + j + 1], None,
                            ALU.is_gt, ins=[buf("gath"), buf("thr")],
                            outs=[buf("qsub"), buf("accs")],
                            accum=self.accs[:, j:j + 1])
                self.col_total(self.accs[:, 0:NLE], NLE,
                               L[0:1, 300:300 + NLE], "accs",
                               out_key="lrow")
                self.ts("vector", SR[0:1, 96:96 + NLE],
                        L[0:1, 300:300 + NLE], self.lad(35), None, ALU.add,
                        ins=[bL, bl], outs=[bsr])
                lo2 = self.lad(44 + 3 * rnd + 8 * ti)
                hi2 = self.lad(45 + 3 * rnd + 8 * ti)
                ca2 = self.lad(46 + 3 * rnd + 8 * ti)
                self.gap_pick(SR[0:1, 96:96 + NLE], SR[0:1, 32:32 + NLE],
                              tgt, lo2, hi2, ca2, NLE)
                lo, hi, ca = lo2, hi2, ca2
            self.bc_pp(lo, self.qwin[:, 2:3], "qwin")
            self.bc_pp(hi, self.qwin[:, 3:4], "qwin")
            self.stt("vector", self.qsub[:, 0:128], self.gath[:],
                     self.qwin[:, 2:3], self.gath[:], ALU.is_gt, ALU.mult,
                     ins=[buf("gath"), buf("qwin")], outs=[buf("qsub")])
            self.stt("vector", self.qsub[:, 0:128], self.qsub[:, 0:128],
                     self.qwin[:, 3:4], self.qsub[:, 0:128], ALU.is_le,
                     ALU.mult, ins=[buf("qwin")], outs=[buf("qsub")])
            self.max8(self.candx[:, 0:8], self.qsub[:, 0:128],
                      ins=[buf("qsub")], outs=[buf("candx")])
            fw.dma("gpsimd",
                   self.row_sp.ap().rearrange("a (p s) -> (a p) s", p=128),
                   self.candx[:, 0:8],
                   ins=[buf("candx"), buf("row_sp")], outs=[buf("row_sp")],
                   sem=f"rs{qid}{ti}")
            fw.dma("sync", self.row1k[0:1, :], self.row_sp.ap(),
                   ins=[buf("row_sp"), buf("row1k")], outs=[buf("row1k")],
                   sem=f"rl{qid}{ti}")
            self.max8(SR[0:1, 224:232], self.row1k[0:1, :],
                      ins=[buf("row1k")], outs=[bsr])
            self.mrep(self.row1k[0:1, :], SR[0:1, 224:232],
                      self.row1k[0:1, :], ins=[bsr], outs=[buf("row1k")])
            self.max8(SR[0:1, 232:240], self.row1k[0:1, :],
                      ins=[buf("row1k")], outs=[bsr])
            self.tt("vector", self.lad(36), tgt, ca, ALU.subtract,
                    ins=[bl], outs=[bl])
            self.ts("vector", L[0:1, 52:53], self.lad(36), 15.0, None,
                    ALU.min, ins=[bl], outs=[bL])
            self.ts("vector", L[0:1, 16:32], self.iota16[:], L[0:1, 52:53],
                    None, ALU.is_equal, ins=[buf("iota16"), bL], outs=[bL])
            self.tt("vector", L[0:1, 16:32], L[0:1, 16:32],
                    SR[0:1, 224:240], ALU.mult, ins=[bsr, bL], outs=[bL])
            self.red("vector", vout, L[0:1, 16:32], ALU.add, ins=[bL],
                     outs=[bl])
            ok = self.lad(40 + ti)
            self.ts("vector", L[0:1, 40:41], tgt, 1.0, None, ALU.add,
                    ins=[bl], outs=[bL])
            self.fw.op("vector", lambda v: v.tensor_scalar(
                L[0:1, 41:42], SR[0:1, 96:97], L[0:1, 40:41], None,
                ALU.is_ge), ins=[bsr, bL], outs=[bL])
            self.ts("vector", L[0:1, 42:43], self.lad(36), 0.0, None,
                    ALU.is_ge, ins=[bl], outs=[bL])
            self.ts("vector", L[0:1, 43:44], self.lad(36), 15.5, None,
                    ALU.is_le, ins=[bl], outs=[bL])
            self.fw.op("vector", lambda v: v.tensor_tensor(
                L[0:1, 53:54], SR[0:1, 224:225], SR[0:1, 239:240],
                ALU.is_equal), ins=[bsr], outs=[bL])
            self.tt("vector", L[0:1, 43:44], L[0:1, 43:44], L[0:1, 53:54],
                    ALU.max, ins=[bL], outs=[bL])
            self.tt("vector", L[0:1, 41:42], L[0:1, 41:42], L[0:1, 42:43],
                    ALU.mult, ins=[bL], outs=[bL])
            self.tt("vector", ok, L[0:1, 41:42], L[0:1, 43:44], ALU.mult,
                    ins=[bL], outs=[bl])
        self.cp("vector", self.lad(29), self.lad(28), ins=[bl], outs=[bl])
        self.cp("vector", self.lad(41), self.lad(40), ins=[bl], outs=[bl])
        self.ts("vector", L[0:1, 44:45], self.lad(15), -1.0, 1.0, ALU.mult,
                ALU.add, ins=[bl], outs=[bL])
        self.tt("vector", L[0:1, 45:46], self.lad(28), L[0:1, 44:45],
                ALU.mult, ins=[bl, bL], outs=[bL])
        self.tt("vector", L[0:1, 46:47], self.lad(29), self.lad(15),
                ALU.mult, ins=[bl], outs=[bL])
        self.tt("vector", L[0:1, 45:46], L[0:1, 45:46], L[0:1, 46:47],
                ALU.add, ins=[bL], outs=[bL])
        self.tt("vector", self.lad(42), self.lad(40), self.lad(41),
                ALU.mult, ins=[bl], outs=[bl])
        self.tt("vector", L[0:1, 47:48], self.lad(33), self.lad(34),
                ALU.add, ins=[bl], outs=[bL])
        self.ts("vector", L[0:1, 47:48], L[0:1, 47:48], 0.5, None, ALU.mult,
                ins=[bL], outs=[bL])
        self.tt("vector", L[0:1, 45:46], L[0:1, 45:46], self.lad(42),
                ALU.mult, ins=[bl, bL], outs=[bL])
        self.ts("vector", L[0:1, 48:49], self.lad(42), -1.0, 1.0, ALU.mult,
                ALU.add, ins=[bl], outs=[bL])
        self.tt("vector", L[0:1, 47:48], L[0:1, 47:48], L[0:1, 48:49],
                ALU.mult, ins=[bL], outs=[bL])
        self.tt("vector", L[0:1, 45:46], L[0:1, 45:46], L[0:1, 47:48],
                ALU.add, ins=[bL], outs=[bL])
        self.ts("vector", self.lad(43), L[0:1, 45:46], EPS, None, ALU.max,
                ins=[bL], outs=[bl])
        for s11 in (self.lad(9), self.lad(10), self.lad(11), self.lad(18),
                    self.lad(19), self.lad(12), self.lad(13), self.lad(14),
                    self.lad(28), self.lad(29), self.lad(40), self.lad(41),
                    self.lad(42), self.lad(43)):
            self.dbg(s11)
        self.rcp(L[0:1, 51:52], self.lad(43), ins=[bl], outs=[bL])
        self.ts("vector", self.lad(60), L[0:1, 51:52], 127.0, None,
                ALU.mult, ins=[bL], outs=[bl])
        self.tt("vector", L[0:1, 50:51], self.lad(gamma_lad), self.lad(43),
                ALU.mult, ins=[bl], outs=[bL])
        self.ts("vector", self.lad(61), L[0:1, 50:51],
                float(1.0 / 127.0), None, ALU.mult, ins=[bL], outs=[bl])
        self.bc_pp(self.lad(60), self.s_pp[:, s_col:s_col + 1], "s_pp")
        self.bc_pp(self.lad(61), self.s_pp[:, gs_col:gs_col + 1], "s_pp")

    def f1_phase(self):
        fw, buf = self.fw, self.buf
        self.mset("vector", self.accs[:, 24:26], 0.0, outs=[buf("accs")])
        self.q2_calls = 0
        for i in range(NT):
            par = i % 2
            pi = (i // 2) % 2
            hb, hk = self.hlnF[par], buf(f"hlnF{par}")
            kt, bkt = self.k1tok[par], buf(f"k1tok{par}")
            kT, bkT = self.k1Tw[pi], buf(f"k1Tw{pi}")
            r0 = i * TT
            fw.dma("sync", hb[:],
                   self.hln_sp.ap()[r0:r0 + TT, :].rearrange(
                       "(tc p) d -> p tc d", p=128),
                   ins=[buf("hln_sp"), hk], outs=[hk], sem=f"f1h{par}")
            for tc in range(2):
                v = hb[:, tc, :]
                self.ts("vector", v, v, self.s_pp[:, 0:1], MAGIC, ALU.mult,
                        ALU.add, ins=[hk, buf("s_pp")], outs=[hk])
                self.ts("vector", v, v, MAGIC, -128.0, ALU.subtract,
                        ALU.max, ins=[hk], outs=[hk])
                self.ts("vector", kt[:, tc, :], v, 127.0, None,
                        ALU.min, ins=[hk], outs=[bkt])
            for tc in range(2):
                for dc in range(DC):
                    pt = self.ptr[dc % 2]
                    bpt = buf(f"ptr{dc % 2}")
                    self.tr(pt[:, 0:128],
                            kt[:, tc, dc * 128:(dc + 1) * 128],
                            ins=[bkt], outs=[bpt])
                    self.cp("vector",
                            kT[:, dc,
                               par * TT + tc * 128:par * TT + (tc + 1) * 128],
                            pt[:, 0:128], ins=[bpt], outs=[bkT])
            if par == 1:
                p = i // 2
                r0p = (i - 1) * TT
                gb, gk = self.gsb[pi], buf(f"gsb{pi}")
                for gr in range(8):
                    if p >= 4:
                        rb, rk = self.q2rd[gr % 2], buf(f"q2rd{gr % 2}")
                        fw.dma("scalar", rb[:],
                               self.g_sp.ap()[gr * 4:(gr + 1) * 4, :,
                                              (p - 4) * 512:
                                              (p - 4) * 512 + 512].rearrange(
                                   "hc p t -> p hc t"),
                               ins=[buf("g_sp"), rk], outs=[rk],
                               sem=f"qgr{gr % 2}")
                    for h4 in range(4):
                        hc = gr * 4 + h4
                        pbk, bp = self.pb[h4], buf(f"pb{h4}")
                        for dc in range(DC):
                            self.mm(pbk[:, 0:512],
                                    self.w1qT[:, dc,
                                              hc * 128:(hc + 1) * 128],
                                    kT[:, dc, :], dc == 0, dc == DC - 1,
                                    ins=[buf("w1qT"), bkT], outs=[bp])
                        self.act(gb[:, h4, :], pbk[:, 0:512], AF.Gelu,
                                 scale=self.s_pp[:, 1:2],
                                 ins=[bp, buf("s_pp")], outs=[gk])
                    fw.dma("gpsimd",
                           self.g_sp.ap()[gr * 4:(gr + 1) * 4, :,
                                          r0p:r0p + 512].rearrange(
                               "hc p t -> p hc t"),
                           gb[:], ins=[gk, buf("g_sp")],
                           outs=[buf("g_sp")], sem=f"sg{pi}")
                    if p == 0:
                        self.ladder_tile(gb.rearrange("p a b -> p (a b)"),
                                         gk, 0, need_abs=False, subs=32,
                                         nq=2, width=2048, shift=0)
                    elif p <= 2 or (p == 3 and gr % 2 == p % 2):
                        self.ladder_tile(gb.rearrange("p a b -> p (a b)"),
                                         gk, 1, need_abs=False, subs=32,
                                         nq=2, width=2048,
                                         shift=(p * 4 + gr) % 32)
                        self.q2_calls += 1
                    if p >= 4:
                        j = p - 4
                        rb, rk = self.q2rd[gr % 2], buf(f"q2rd{gr % 2}")
                        self.capture_tile(
                            gb.rearrange("p a b -> p (a b)"), gk,
                            2048, 1024, NT + p, self.slots2,
                            buf("slots2c"), (p * 8 + gr) * 16,
                            self.selscr, buf("selscr"))
                        self.capture_tile(
                            rb.rearrange("p a b -> p (a b)"), rk,
                            2048, 1024, NT + j, self.slots2,
                            buf("slots2c"), (j * 8 + gr) * 16,
                            self.selscr, buf("selscr"))
                if p == 0:
                    self.ladder_setup_gen1(2)
                if p == 2:
                    self.interp_est(0.005 * self.q2_calls * 64 * 128,
                                    self.lad(9))
                    self.ladder_setup_gen2(self.lad(9), 2)
                    self.q2_calls = 0
                if p == 3:
                    self.interp_est(0.005 * self.q2_calls * 64 * 128,
                                    self.lad(9))
                    self.window_from_est(self.lad(9), Q2_WIN)
                self.w2_prep_chunk(p)
        self.w2_finish()

    def f2_phase(self):
        fw, buf = self.fw, self.buf
        out_toks = []
        st, bst = self.stats, buf("stats")
        for i in range(NT):
            par = i % 2
            r0 = i * TT
            x2b, bx2 = self.x2F[par], buf(f"x2F{par}")
            ob, bob = self.outsb[par], buf(f"outs{par}")
            fw.dma("sync", x2b[:],
                   self.x2_sp.ap()[r0:r0 + TT, :].rearrange(
                       "(tc p) d -> p tc d", p=128),
                   ins=[buf("x2_sp"), bx2], outs=[bx2], sem=f"f2x{par}")
            for gr in range(8):
                gb = self.gF[gr % 2]
                gk = buf(f"gF{gr % 2}")
                kb = self.k2[gr % 2]
                kk = buf(f"k2_{gr % 2}")
                fw.dma("sync" if gr % 2 == 0 else "scalar", gb[:],
                       self.g_sp.ap()[gr * 4:(gr + 1) * 4, :,
                                      r0:r0 + TT].rearrange(
                           "hc p t -> p hc t"),
                       ins=[buf("g_sp"), gk], outs=[gk], sem=f"f2g{gr % 2}")
                flat = gb.rearrange("p a b -> p (a b)")
                self.ts("vector", flat, flat, self.s_pp[:, 2:3], MAGIC,
                        ALU.mult, ALU.add, ins=[gk, buf("s_pp")], outs=[gk])
                self.ts("vector", flat, flat, MAGIC, -128.0, ALU.subtract,
                        ALU.max, ins=[gk], outs=[gk])
                self.ts("vector", kb.rearrange("p a b -> p (a b)"), flat,
                        127.0, None, ALU.min, ins=[gk], outs=[kk])
                for h4 in range(4):
                    hc = gr * 4 + h4
                    for tc in range(2):
                        for nq in range(2):
                            pbk = self.pb[tc * 2 + nq]
                            bp = buf(f"pb{tc * 2 + nq}")
                            self.mm(pbk[:, 0:512],
                                    kb[:, h4, tc * 128:(tc + 1) * 128],
                                    self.w2qT[:, hc,
                                              nq * 512:(nq + 1) * 512],
                                    hc == 0, hc == HC - 1,
                                    ins=[kk, buf("w2qT")], outs=[bp])
            for tc in range(2):
                for nq in range(2):
                    pbk = self.pb[tc * 2 + nq]
                    bp = buf(f"pb{tc * 2 + nq}")
                    self.ts("vector",
                            ob[:, tc, nq * 512:(nq + 1) * 512],
                            pbk[:, 0:512], self.s_pp[:, 3:4], None,
                            ALU.mult, ins=[bp, buf("s_pp")],
                            outs=[bob])
            self.ln_stats2(ob, bob, self.esp2, [buf("espt")], st, bst)
            for tc in range(2):
                v = ob[:, tc, :]
                self.ts("vector", v, v, st[:, 4 + tc:5 + tc],
                        st[:, 10 + tc:11 + tc], ALU.subtract, ALU.mult,
                        ins=[bst], outs=[bob])
                self.tt("vector", v, v, x2b[:, tc, :], ALU.add,
                        ins=[bx2], outs=[bob])
            t = fw.dma("gpsimd",
                       self.y_out.ap()[r0:r0 + TT, :].rearrange(
                           "(tc p) d -> p tc d", p=128),
                       ob[:], ins=[bob, buf("y")],
                       outs=[buf("y")], sem=f"out{par}")
            out_toks.append(t)
        if self.debug:
            t = fw.dma("gpsimd", self.dbg_d.ap()[0:1, 0:320],
                       self.dbgrow[0:1, :],
                       ins=[buf("dbgrow"), buf("dbg")], outs=[buf("dbg")],
                       sem="outd")
            out_toks.append(t)
        fw.wait_all("sync", out_toks)

    def emit(self):
        self.fw.emit()


def build_program(debug=False):
    nc = bass.Bass("TRN2", target_bir_lowering=False)
    k = K(nc, debug)
    k.declare()
    k.init_phase()

    def post_b(bi):
        if bi == 0:
            k.ladder_setup_gen1(1)
        if bi == 3:
            k.interp_est(0.005 * 3 * 64 * 128, k.lad(9))
            k.ladder_setup_gen2(k.lad(9), 1)
        if bi == 7:
            k.interp_est(0.005 * 4 * 64 * 128, k.lad(9))
            k.window_from_est(k.lad(9), Q1_WIN)

    for i in range(NT):
        k.stage_a(i)
        if i % 8 != 0:
            k.stage_b_front_a(i - 1)
        if i >= 2 and (i - 2) % 8 < 6:
            k.stage_b_tail(i - 2)
            post_b(i - 2)
        k.stage_a_mm(i)
        if i % 8 != 0:
            k.stage_b_front_a2(i - 1)
            k.stage_b_front_b(i - 1)
        if i % 8 == 7:
            k.stage_b_tail(i - 1)
            post_b(i - 1)
            k.stage_b_front_a(i)
            k.stage_b_front_a2(i)
            k.stage_b_front_b(i)
            k.stage_b_tail(i)
            post_b(i)
    k.phase_barrier_all()
    k.q1_window_phase()
    k.select_quantile(1, k.slots1[:, :], k.buf("slots1c"), 0, NT, N_Q1,
                      0, 0, 1)
    k.w1_finish()
    k.phase_barrier_all()
    k.f1_phase()
    k.phase_barrier_all()
    k.select_quantile(2, k.slots2[:, :], k.buf("slots2c"), NT, NT, N_Q2,
                      2, 2, 3)
    k.phase_barrier_all()
    k.f2_phase()
    k.emit()
    return nc


def make_core_inputs(inputs, core):
    x = np.asarray(inputs["L0_attn_x"])
    xs = np.ascontiguousarray(x[2 * core:2 * core + 2].reshape(NTOK, D))
    vl = [np.asarray(inputs["L0_attn_conv_w"])[0, 0],
          np.asarray(inputs["L0_attn_conv_w"])[1, 0],
          np.asarray(inputs["L0_attn_conv_w"])[2, 0],
          np.asarray(inputs["L0_attn_conv_b"]),
          np.asarray(inputs["L0_attn_A"]),
          np.asarray(inputs["L0_attn_D_skip"]),
          np.asarray(inputs["L0_attn_b_dt"]),
          np.asarray(inputs["L0_attn_gamma"]),
          np.asarray(inputs["L0_attn_beta"]),
          np.zeros(D, np.float32), np.zeros(D, np.float32),
          np.zeros(D, np.float32)]
    vecs = np.stack([np.asarray(v, np.float32).reshape(DC, 128).T
                     for v in vl], axis=2).reshape(128, DC * 12)
    vecs = np.ascontiguousarray(vecs)
    lngb = np.ascontiguousarray(np.concatenate([
        np.broadcast_to(np.asarray(inputs["L0_ffn_ln_g"], np.float32),
                        (128, D)),
        np.broadcast_to(np.asarray(inputs["L0_ffn_ln_b"], np.float32),
                        (128, D))], axis=1))
    ident = np.eye(128, dtype=ml_dtypes.bfloat16)
    rows = np.zeros((1, 64), np.float32)
    rows[0, 0:NL] = (GEOLO ** (np.arange(NL) / (NL - 1))).astype(np.float32)
    rows[0, NL:2 * NL] = ((NL - 1) / 2.0 - np.arange(NL)) / ((NL - 1) / 2.0) \
        * LIN_W
    rows[0, 32:32 + NLE] = np.arange(NLE) / float(NLE - 1)
    return {
        "x": xs,
        "W_in": np.asarray(inputs["L0_attn_W_in"], np.float32),
        "W_out": np.asarray(inputs["L0_attn_W_out"], np.float32),
        "W_xp": np.asarray(inputs["L0_attn_W_xp"], np.float32),
        "W_dt": np.asarray(inputs["L0_attn_W_dt"], np.float32),
        "W_Bg": np.asarray(inputs["L0_attn_W_Bg"], np.float32),
        "W_Cg": np.asarray(inputs["L0_attn_W_Cg"], np.float32),
        "W_Bp": np.asarray(inputs["L0_attn_W_Bp"], np.float32),
        "W_Cp": np.asarray(inputs["L0_attn_W_Cp"], np.float32),
        "w1": np.asarray(inputs["L0_ffn_w1"], np.float32),
        "w2": np.asarray(inputs["L0_ffn_w2"], np.float32),
        "vecs": vecs, "lngb": lngb, "ident": ident, "rows": rows,
    }


def kernel(**inputs):
    nc = build_program(DEBUG)
    in_maps = [make_core_inputs(inputs, c) for c in range(8)]
    res = run_bass_kernel_spmd(nc, in_maps, core_ids=list(range(8)))
    outs = [np.asarray(r["y"]) for r in res.results]
    return np.concatenate(outs, 0).reshape(16, 2048, 1024)


# version 22
# speedup vs baseline: 1.0603x; 1.0008x over previous
import contextlib

import numpy as np
import ml_dtypes

import concourse.bass as bass
import concourse.mybir as mybir
from concourse.bass_utils import run_bass_kernel_spmd

F32 = mybir.dt.float32
BF16 = mybir.dt.bfloat16
I32 = mybir.dt.int32
AF = mybir.ActivationFunctionType
ALU = mybir.AluOpType
AX = mybir.AxisListType

DEBUG = False
TRACE_SITES = False

ENGINES = ("tensor", "vector", "scalar", "gpsimd", "sync")


class Buf:
    __slots__ = ("name", "writer", "readers")

    def __init__(self, name=""):
        self.name = name
        self.writer = None
        self.readers = []


class Fw:

    def __init__(self, nc):
        self.nc = nc
        self._stack = contextlib.ExitStack()
        self.sem = {}
        self.count = {}
        self.known = {e: {} for e in ENGINES}
        self.eng_sem = {}
        self.prog = {e: [] for e in ENGINES}
        self.sites = {}
        for e in ENGINES:
            sname = "S_" + e
            self.sem[sname] = self._stack.enter_context(nc.semaphore(sname))
            self.count[sname] = 0
            self.eng_sem[e] = sname

    def new_sem(self, name):
        if name not in self.sem:
            self.sem[name] = self._stack.enter_context(self.nc.semaphore(name))
            self.count[name] = 0
        return name

    def _sched_wait(self, eng, token):
        if token is None:
            return
        sname, val = token
        if self.known[eng].get(sname, 0) >= val:
            return
        if not sname.startswith("S_") and val != self.count[sname]:
            raise RuntimeError(
                f"wait on dma sem {sname} at {val} != cumulative "
                f"{self.count[sname]}; use a dedicated sem")
        self.known[eng][sname] = val
        sem = self.sem[sname]
        self.prog[eng].append(lambda h, s=sem, v=val: h.wait_ge(s, v))

    def _deps_for(self, ins, outs):
        deps = []
        for b in ins:
            if b.writer is not None:
                deps.append(b.writer)
        for b in outs:
            if b.writer is not None:
                deps.append(b.writer)
            deps.extend(b.readers)
        return deps

    def _finish(self, ins, outs, token):
        for b in ins:
            b.readers.append(token)
        for b in outs:
            b.writer = token
            b.readers = []

    def op(self, eng, fn, ins=(), outs=(), deps=()):
        for d in self._deps_for(ins, outs):
            self._sched_wait(eng, d)
        for d in deps:
            self._sched_wait(eng, d)
        sname = self.eng_sem[eng]
        self.count[sname] += 1
        sem = self.sem[sname]

        def run(h, fn=fn, sem=sem):
            fn(h).then_inc(sem, 1)

        self.prog[eng].append(run)
        if TRACE_SITES:
            import traceback
            self.sites.setdefault(eng, {})[len(self.prog[eng]) - 1] = \
                traceback.extract_stack()[-4:-2]
        token = (sname, self.count[sname])
        self._finish(ins, outs, token)
        return token

    def dma(self, eng, out_ap, in_ap, ins=(), outs=(), sem="dma", deps=(),
            **kw):
        self.new_sem(sem)
        for d in self._deps_for(ins, outs):
            self._sched_wait(eng, d)
        for d in deps:
            self._sched_wait(eng, d)
        self.count[sem] += 16
        semh = self.sem[sem]

        def run(h, out_ap=out_ap, in_ap=in_ap, semh=semh, kw=kw):
            h.dma_start(out=out_ap, in_=in_ap, **kw).then_inc(semh, 16)

        self.prog[eng].append(run)
        token = (sem, self.count[sem])
        self._finish(ins, outs, token)
        return token

    def wait_all(self, eng, tokens):
        best = {}
        for sname, val in tokens:
            best[sname] = max(best.get(sname, 0), val)
        for t in best.items():
            self._sched_wait(eng, t)

    def barrier_tokens(self):
        toks = [(self.eng_sem[e], self.count[self.eng_sem[e]])
                for e in ENGINES]
        toks += [(s, self.count[s]) for s in self.sem
                 if not s.startswith("S_")]
        return toks

    def emit(self):
        with self.nc.Block() as block:
            for e in ENGINES:
                if not self.prog[e]:
                    continue
                getattr(block, e)(lambda h, e=e: [f(h) for f in self.prog[e]])
        self._stack.close()


D = 1024
H = 4096
DC = 8
HC = 32
TT = 256
NT = 16
NTOK = NT * TT
RK = 64
DS = 16
EPS = 1e-5
INV_D = float(1.0 / 1024.0)
MAGIC = float(3 << 22)
N_Q1 = float(16 * 2048 * 1024)
N_Q2 = float(16 * 2048 * 4096)
RG = [[0, 1, 2, 3, 4, 5, 6, 7]]
NL = 16
NLE = 14
GEOLO = 0.04
LIN_W = 0.10
Q1_WIN = 0.06
Q2_WIN = 0.045
Q1_SUB_PER_TILE = 2 * 128 * 128
Q2_SUB_PER_TILE = 4 * 128 * 128
Q1_SLOTS = 4 * 8 * NT
Q2_SLOTS = 64 * 16
V_W0, V_W1, V_W2, V_CB, V_A, V_DSK, V_BDT, V_GAM, V_BET = range(9)
ABSM = 0x7FFFFFFF


class K:
    def __init__(self, nc, debug=False):
        self.nc = nc
        self.fw = Fw(nc)
        self.debug = debug
        self._bufs = {}
        self.dbg_idx = 0

    def buf(self, key):
        if key not in self._bufs:
            b = Buf(key)
            b.readers = list(getattr(self, "_barrier_toks", ()))
            self._bufs[key] = b
        return self._bufs[key]

    def carve(self, off_bytes, shape, dtype=F32):
        n = 1
        for s in shape[1:]:
            n *= s
        esz = 2 if dtype == BF16 else 4
        assert off_bytes % 4 == 0
        f0 = off_bytes // 4
        nf = (n * esz + 3) // 4
        ap = self.arena[:, f0:f0 + nf]
        if dtype == BF16:
            ap = ap.bitcast(BF16)[:, :n]
        if len(shape) == 3:
            ap = ap.rearrange("p (a b) -> p a b", a=shape[1])
        return ap

    def ts(self, eng, out, in_, s1, s2, op0, op1=None, ins=(), outs=(),
           deps=(), accum=None):
        if op1 is None and s2 is None and accum is None:
            return self.fw.op(
                eng, lambda h: h.tensor_scalar(out, in_, s1, None, op0),
                ins=ins, outs=outs, deps=deps)
        if op1 is None:
            op1 = ALU.add if accum is not None and s2 is None else op1
        assert op1 is not None, "need op1"
        return self.fw.op(
            eng, lambda h: h.tensor_scalar(out, in_, s1, s2, op0, op1,
                                           accum_out=accum),
            ins=ins, outs=outs, deps=deps)

    def tt(self, eng, out, a, b, op, ins=(), outs=(), deps=()):
        return self.fw.op(eng, lambda h: h.tensor_tensor(out, a, b, op),
                          ins=ins, outs=outs, deps=deps)

    def stt(self, eng, out, in0, scalar, in1, op0, op1, ins=(), outs=(),
            deps=()):
        return self.fw.op(
            eng, lambda h: h.scalar_tensor_tensor(out, in0, scalar, in1,
                                                  op0, op1),
            ins=ins, outs=outs, deps=deps)

    def cp(self, eng, out, in_, ins=(), outs=(), deps=()):
        return self.fw.op(eng, lambda h: h.tensor_copy(out, in_),
                          ins=ins, outs=outs, deps=deps)

    def act(self, out, in_, func, bias=0.0, scale=1.0, ins=(), outs=(),
            deps=(), accum=None):
        return self.fw.op(
            "scalar", lambda h: h.activation(out, in_, func, bias=bias,
                                             scale=scale, accum_out=accum),
            ins=ins, outs=outs, deps=deps)

    def mm(self, out, lhsT, rhs, start, stop, ins=(), outs=(), deps=()):
        return self.fw.op(
            "tensor", lambda h: h.matmul(out, lhsT, rhs, start=start,
                                         stop=stop),
            ins=ins, outs=outs, deps=deps)

    def tr(self, out, in_, ins=(), outs=(), deps=()):
        return self.fw.op(
            "tensor", lambda h: h.transpose(out, in_, self.identb[:]),
            ins=list(ins) + [self.buf("identb")], outs=outs, deps=deps)

    def red(self, eng, out, in_, op, absval=False, ins=(), outs=(), deps=()):
        return self.fw.op(
            eng, lambda h: h.tensor_reduce(out, in_, AX.X, op,
                                           apply_absolute_value=absval),
            ins=ins, outs=outs, deps=deps)

    def mset(self, eng, ap, val, outs=(), deps=()):
        return self.fw.op(eng, lambda h: h.memset(ap, val), outs=outs,
                          deps=deps)

    def rcp(self, out, in_, ins=(), outs=(), deps=()):
        return self.fw.op("vector", lambda h: h.reciprocal(out, in_),
                          ins=ins, outs=outs, deps=deps)

    def max8(self, out, in_, ins=(), outs=(), deps=()):
        return self.fw.op("vector", lambda h: h.max(out, in_),
                          ins=ins, outs=outs, deps=deps)

    def mrep(self, out, repl, in_, ins=(), outs=(), deps=()):
        return self.fw.op(
            "vector", lambda h: h.match_replace(out, repl, in_, 0.0),
            ins=ins, outs=outs, deps=deps)

    def coll(self, kind, op, src_ap, dst_ap, ins=(), outs=()):
        return self.fw.op(
            "gpsimd", lambda g: g.collective_compute(kind, op, RG,
                                                     [src_ap], [dst_ap]),
            ins=ins, outs=outs)

    def declare(self):
        nc = self.nc
        DR = nc.dram_tensor
        self.x_in = DR("x", [NTOK, D], F32, kind="ExternalInput")
        self.y_out = DR("y", [NTOK, D], F32, kind="ExternalOutput")
        specs = [("W_in", [D, 2 * D]), ("W_out", [D, D]), ("W_xp", [D, RK]),
                 ("W_dt", [RK, D]), ("W_Bg", [D, DS]), ("W_Cg", [D, DS]),
                 ("W_Bp", [DS, D]), ("W_Cp", [DS, D]), ("w1", [H, D]),
                 ("w2", [D, H])]
        (self.win_d, self.wout_d, self.wxp_d, self.wdt_d, self.wbg_d,
         self.wcg_d, self.wbp_d, self.wcp_d, self.w1_d, self.w2_d) = [
            DR(n, s, F32, kind="ExternalInput") for n, s in specs]
        self.vecs_d = DR("vecs", [128, DC * 12], F32, kind="ExternalInput")
        self.lngb_d = DR("lngb", [128, 2 * D], F32, kind="ExternalInput")
        self.ident_d = DR("ident", [128, 128], BF16, kind="ExternalInput")
        self.rows_d = DR("rows", [1, 64], F32, kind="ExternalInput")
        dk = "ExternalOutput" if self.debug else "Internal"
        self.x2_sp = DR("x2_sp", [NTOK, D], F32, kind=dk)
        self.hln_sp = DR("hln_sp", [NTOK, D], F32, kind=dk)
        self.g_sp = DR("g_sp", [HC, 128, NTOK], F32,
                       kind=("ExternalOutput" if self.debug else "Internal"))
        self.w1q_sp = DR("w1q_sp", [DC, 128, H], BF16, kind="Internal")
        self.w2q_sp = DR("w2q_sp", [HC, 128, D], BF16, kind="Internal")
        self.cc1_src = DR("cc1_src", [1, 64], F32, kind="Internal")
        self.cc1_dst = DR("cc1_dst", [8, 64], F32, kind="Internal")
        self.cc2_src = DR("cc2_src", [1, 32], F32, kind="Internal")
        self.cc2_dst = DR("cc2_dst", [8, 32], F32, kind="Internal")
        self.cc3_src = DR("cc3_src", [1, 2048], F32, kind="Internal")
        self.cc3_dst = DR("cc3_dst", [8, 2048], F32, kind="Internal")
        self.row_sp = DR("row_sp", [1, 1024], F32, kind="Internal")
        if self.debug:
            self.dbg_d = DR("dbg", [1, 320], F32, kind="ExternalOutput")

        sb = nc.alloc_sbuf_tensor
        self.identb = sb("identb", [128, 128], BF16)
        self.vec = sb("vec", [128, DC, 12], F32)
        self.lng_bc = sb("lng_bc", [128, D], F32)
        self.lnb_bc = sb("lnb_bc", [128, D], F32)
        self.onescol = sb("onescol", [128, 1], F32)
        self.zcol = sb("zcol", [128, 1], F32)
        self.onesrow = sb("onesrow", [1, 128], F32)
        self.eps_t = sb("eps_t", [128, 1], F32)
        self.one_t = sb("one_t", [128, 1], F32)
        self.rows = sb("rows_sb", [1, 64], F32)
        self.iota16 = sb("iota16", [1, 16], F32)
        self.stats = sb("stats", [128, 16], F32)
        self.statB = sb("statB", [128, 16], F32)
        self.statR = sb("statR", [128, 16], F32)
        self.lads = sb("lads", [1, 64], F32)
        self.lrow = sb("lrow", [1, 512], F32)
        self.thr = sb("thr", [128, 2 * NLE], F32)
        self.cnt = sb("cnt", [128, NL], F32)
        self.accs = sb("accs", [128, 32], F32)
        self.qwin = sb("qwin", [128, 4], F32)
        self.hstate = sb("hstate", [128, DC], F32)
        self.xblast = sb("xblast", [128, DC], F32)
        self.s_pp = sb("s_pp", [128, 4], F32)
        self.scp = sb("scp", [128, 4], F32)
        self.candx = sb("candx", [128, 16], F32)
        self.gath = sb("gath", [128, 128], F32)
        self.qsub = sb("qsub", [128, 512], F32)
        self.row1k = sb("row1k", [1, 1024], F32)
        self.i32s = sb("i32s", [1, 2], I32)
        self.dbgrow = sb("dbgrow", [1, 320], F32)
        self.srow = sb("srow", [1, 256], F32)
        self.cnthi = sb("cnthi", [128, NT * 8], F32)

        AW = 45056
        self.arena = sb("arena", [128, AW], F32)
        C = self.carve
        self.winb = C(0, [128, DC, 2 * D], BF16)
        self.woutb = C(32768, [128, DC, D], BF16)
        self.wgateb = C(49152, [128, DC, 128], BF16)
        self.wdtb = C(51200, [128, D], BF16)
        self.wbpb = self.wdtb
        self.wcpb = self.wdtb
        self.xt = [C(56832, [128, 2, D]), C(65024, [128, 2, D])]
        self.wstage = [C(56832, [128, 2 * D]), C(65024, [128, 2 * D])]
        self.xres = C(73216, [128, 2, D])
        self.gsta = C(81408, [128, 2, D])
        self.xn2b = C(89600, [128, 2, D], BF16)
        self.xn2T = C(93696, [128, DC, TT], BF16)
        self.sq_a = C(93696, [128, D])
        self.xb = [C(97792, [128, DC, TT]), C(105984, [128, DC, TT])]
        self.zt = [C(114176, [128, DC, TT]), C(122368, [128, DC, TT])]
        self.xc = C(130560, [128, DC, TT])
        self.xcf = C(130560, [128, DC * TT])
        self.xcr = C(130560, [128, 2, D])
        self.dAf = C(151040, [128, DC * TT])
        self.dBxf = C(159232, [128, DC * TT])
        self.xactb = C(138752, [128, DC, TT], BF16)
        self.espt = C(142848, [128, DC, TT])
        self.espf = C(142848, [128, 2048])
        self.esp2 = C(142848, [128, 2, D])
        self.sq_a3 = C(89600, [128, 2, D])
        self.dA = C(151040, [128, DC, TT])
        self.dBx = C(159232, [128, DC, TT])
        self.gated = C(167424, [128, DC, TT], BF16)
        self.mixsb = C(171520, [128, 2, D])
        self.tsmall = C(171520, [128, 1024], BF16)
        self.w1qT = C(0, [128, DC, H], BF16)
        self.w2qT = C(0, [128, HC, D], BF16)
        self.hlnF = [C(65536, [128, 2, D]), C(73728, [128, 2, D])]
        self.k1tok = [C(81920, [128, 2, D], BF16),
                      C(86016, [128, 2, D], BF16)]
        self.k1Tw = [C(90112, [128, DC, 2 * TT], BF16),
                     C(98304, [128, DC, 2 * TT], BF16)]
        self.gsb = [C(106496, [128, 4, 2 * TT]),
                    C(114688, [128, 4, 2 * TT])]
        self.w2s = [C(122880, [128, D]), C(126976, [128, D])]
        self.w2tb = [C(131072, [128, D], BF16), C(133120, [128, D], BF16)]
        self.w2x = [C(135168, [128, DC, 128], BF16),
                    C(137216, [128, DC, 128], BF16)]
        self.w1s = [C(139264, [128, D]), C(143360, [128, D])]
        self.q2rd = [C(139264, [128, 4, 2 * TT]),
                     C(147456, [128, 4, 2 * TT])]
        self.w1tb = [C(147456, [128, D], BF16), C(149504, [128, D], BF16)]
        self.w1x = [C(151552, [128, DC, 128], BF16),
                    C(153600, [128, DC, 128], BF16)]
        self.selscr = C(155648, [128, 2048])
        self.q1scr = C(106496, [128, 2048])
        self.slots1 = C(53248, [128, Q1_SLOTS])
        self.slots2 = C(167936, [128, Q2_SLOTS])
        self.x2F = [C(65536, [128, 2, D]), C(73728, [128, 2, D])]
        self.outsb = [C(81920, [128, 2, D]), C(90112, [128, 2, D])]
        self.gF = [C(98304, [128, 4, TT]), C(102400, [128, 4, TT])]
        self.k2 = [C(106496, [128, 4, TT], BF16),
                   C(108544, [128, 4, TT], BF16)]
        assert AW * 4 >= 180224, AW * 4
        self.pb = [nc.alloc_psum_tensor(f"pb{i}", [128, 512], F32)
                   for i in range(6)]
        self.ptr = [nc.alloc_psum_tensor(f"ptr{i}", [128, 1024], BF16)
                    for i in range(2)]
        self.sbuf_left = nc.sbuf_bytes_remaining

    def lad(self, j):
        return self.lads[0:1, j:j + 1]

    def dbg_mark(self, k):
        if not self.debug or self.dbg_idx >= 319:
            return
        i = self.dbg_idx
        self.dbg_idx += 1
        self.mset("vector", self.dbgrow[0:1, i:i + 1], float(77000 + k),
                  outs=[self.buf("dbgrow")])

    def dbg(self, src11, n=1):
        if not self.debug:
            return
        i = self.dbg_idx
        if i + n > 320:
            return
        self.dbg_idx += n
        self.cp("vector", self.dbgrow[0:1, i:i + n], src11,
                ins=[self.buf("lads"), self.buf("srow"), self.buf("cnt")],
                outs=[self.buf("dbgrow")])

    def phase_barrier(self, arena_keys):
        toks = self.fw.barrier_tokens()
        for k in arena_keys:
            self.buf(k).readers = list(self.buf(k).readers) + toks

    def bc_pp(self, src11, dst_ap, dst_key):
        p = self.pb[5]
        self.mm(p[:, 0:1], self.onesrow[:], src11, True, True,
                ins=[self.buf("onesrow"), self.buf("lads")],
                outs=[self.buf("pb5")])
        self.cp("vector", dst_ap, p[:, 0:1], ins=[self.buf("pb5")],
                outs=[self.buf(dst_key)])

    def bc_row(self, src_row, k, dst_ap, dst_key, src_key):
        p = self.pb[5]
        self.mm(p[:, 0:k], self.onesrow[:], src_row, True, True,
                ins=[self.buf("onesrow"), self.buf(src_key)],
                outs=[self.buf("pb5")])
        self.cp("vector", dst_ap, p[:, 0:k], ins=[self.buf("pb5")],
                outs=[self.buf(dst_key)])

    def col_total(self, cols_ap, k, out_row, src_key, out_key="lads"):
        p = self.pb[5]
        self.mm(p[0:1, 0:k], self.onescol[:], cols_ap, True, True,
                ins=[self.buf("onescol"), self.buf(src_key)],
                outs=[self.buf("pb5")])
        self.cp("vector", out_row, p[0:1, 0:k], ins=[self.buf("pb5")],
                outs=[self.buf(out_key)])

    def part_max_to_scalar(self, col_ap, out11, src_key):
        fw, buf = self.fw, self.buf
        self._bounce_n = getattr(self, "_bounce_n", 0) + 1
        fw.dma("gpsimd",
               self.row_sp.ap()[0:1, 0:128].rearrange("a (p s) -> (a p) s",
                                                      p=128),
               col_ap, ins=[buf(src_key), buf("row_sp")],
               outs=[buf("row_sp")], sem=f"bounce{self._bounce_n}")
        fw.dma("sync", self.srow[0:1, 0:128], self.row_sp.ap()[0:1, 0:128],
               ins=[buf("row_sp")], outs=[buf("srow")],
               sem=f"bounce{self._bounce_n}b")
        self.red("vector", out11, self.srow[0:1, 0:128], ALU.max,
                 ins=[buf("srow")], outs=[buf("lads")])

    def init_phase(self):
        fw, buf = self.fw, self.buf
        dma = fw.dma
        dma("sync", self.identb[:], self.ident_d.ap(), outs=[buf("identb")],
            sem="l_i")
        dma("sync", self.vec[:],
            self.vecs_d.ap().rearrange("p (c n) -> p c n", c=DC),
            outs=[buf("vec")], sem="l_v")
        dma("sync", self.lng_bc[:], self.lngb_d.ap()[:, 0:D],
            outs=[buf("lng_bc")], sem="l_g")
        dma("sync", self.lnb_bc[:], self.lngb_d.ap()[:, D:2 * D],
            outs=[buf("lnb_bc")], sem="l_b")
        dma("sync", self.rows[:], self.rows_d.ap(), outs=[buf("rows_sb")],
            sem="l_r")
        for t, v in [("onescol", 1.0), ("onesrow", 1.0), ("eps_t", EPS),
                     ("one_t", 1.0)]:
            self.mset("vector", getattr(self, t)[:], v, outs=[buf(t)])
        for t in ("cnt", "cnthi", "stats", "lads", "accs", "hstate", "zcol",
                  "xblast", "qwin", "s_pp", "scp", "candx", "gath", "qsub",
                  "srow", "lrow", "thr", "dbgrow"):
            self.mset("vector", getattr(self, t)[:], 0.0, outs=[buf(t)])
        self.fw.op("gpsimd", lambda g: g.iota(
            self.iota16[:], [[1, 16]], base=0, channel_multiplier=0,
            allow_small_or_imprecise_dtypes=True), outs=[buf("iota16")])

        self._ldn = 0

        def ldcast(dst_ap, dst_key, src_ap, rows, cols, p0=0):
            w = self._ldn % 2
            self._ldn += 1
            st = self.wstage[w]
            dma("sync", st[p0:p0 + rows, 0:cols], src_ap,
                ins=[buf(f"wstage{w}")], outs=[buf(f"wstage{w}")],
                sem=f"l_w{w}")
            self.cp("vector", dst_ap, st[p0:p0 + rows, 0:cols],
                    ins=[buf(f"wstage{w}")], outs=[buf(dst_key)])

        self.mset("vector", self.wgateb[:, :, :], 0.0,
                  outs=[buf("wgateb")])
        for c in range(DC):
            r = slice(c * 128, c * 128 + 128)
            ldcast(self.winb[:, c, :], "winb", self.win_d.ap()[r, :], 128,
                   2 * D)
            ldcast(self.woutb[:, c, :], "woutb", self.wout_d.ap()[r, :],
                   128, D)
            ldcast(self.wgateb[:, c, 64:128], "wgateb",
                   self.wxp_d.ap()[r, :], 128, RK)
            ldcast(self.wgateb[:, c, 0:DS], "wgateb",
                   self.wbg_d.ap()[r, :], 128, DS)
            ldcast(self.wgateb[:, c, 32:32 + DS], "wgateb",
                   self.wcg_d.ap()[r, :], 128, DS)
        ldcast(self.wdtb[64:128, :], "wdtb", self.wdt_d.ap(), 64, D, p0=64)
        ldcast(self.wbpb[0:16, :], "wbpb", self.wbp_d.ap(), 16, D)
        ldcast(self.wcpb[32:48, :], "wcpb", self.wcp_d.ap(), 16, D, p0=32)

        self.mset("vector", self.accs[:, 8:10], 0.0, outs=[buf("accs")])
        for c in range(HC):
            w = c % 2
            st = self.wstage[w]
            dma("sync", st[:, 0:D], self.w1_d.ap()[c * 128:(c + 1) * 128, :],
                ins=[buf(f"wstage{w}")], outs=[buf(f"wstage{w}")],
                sem=f"l_w1{w}")
            self.red("vector", self.accs[:, w:w + 1], st[:, 0:D], ALU.add,
                     absval=True, ins=[buf(f"wstage{w}")],
                     outs=[buf("accs")])
            self.tt("vector", self.accs[:, 8:9], self.accs[:, 8:9],
                    self.accs[:, w:w + 1], ALU.add, ins=[buf("accs")],
                    outs=[buf("accs")])
        for c in range(DC):
            for q in range(2):
                w = q % 2
                st = self.wstage[w]
                dma("sync", st[:, 0:2 * D],
                    self.w2_d.ap()[c * 128:(c + 1) * 128,
                                   q * 2 * D:(q + 1) * 2 * D],
                    ins=[buf(f"wstage{w}")], outs=[buf(f"wstage{w}")],
                    sem=f"l_w2{w}")
                self.red("vector", self.accs[:, w:w + 1], st[:, 0:2 * D],
                         ALU.add, absval=True, ins=[buf(f"wstage{w}")],
                         outs=[buf("accs")])
                self.tt("vector", self.accs[:, 9:10], self.accs[:, 9:10],
                        self.accs[:, w:w + 1], ALU.add, ins=[buf("accs")],
                        outs=[buf("accs")])
        self.col_total(self.accs[:, 8:10], 2, self.lads[0:1, 4:6], "accs")
        self.ts("vector", self.lad(0), self.lads[0:1, 4:5],
                float(1.0 / (H * D)), EPS, ALU.mult, ALU.add,
                ins=[buf("lads")], outs=[buf("lads")])
        self.ts("vector", self.lad(2), self.lads[0:1, 5:6],
                float(1.0 / (H * D)), EPS, ALU.mult, ALU.add,
                ins=[buf("lads")], outs=[buf("lads")])
        self.rcp(self.lad(1), self.lad(0), ins=[buf("lads")],
                 outs=[buf("lads")])
        self.rcp(self.lad(3), self.lad(2), ins=[buf("lads")],
                 outs=[buf("lads")])
        self.bc_pp(self.lad(1), self.scp[:, 0:1], "scp")
        self.bc_pp(self.lad(3), self.scp[:, 1:2], "scp")
        self.phase_barrier(["xt0", "xt1"])

    def ln_stats2(self, v3, bv, scr, bscrs, st, bst):
        for tc in range(2):
            self.red("vector", st[:, tc:tc + 1], v3[:, tc, :], ALU.add,
                     ins=[bv], outs=[bst])
            self.act(scr[:, tc, :], v3[:, tc, :], AF.Square, ins=[bv],
                     outs=list(bscrs) + [bst], accum=st[:, 2 + tc:3 + tc])
        self.ts("vector", st[:, 4:6], st[:, 0:2], INV_D, None, ALU.mult,
                ins=[bst], outs=[bst])
        self.ts("vector", st[:, 6:8], st[:, 2:4], INV_D, None, ALU.mult,
                ins=[bst], outs=[bst])
        self.tt("vector", st[:, 8:10], st[:, 4:6], st[:, 4:6], ALU.mult,
                ins=[bst], outs=[bst])
        self.tt("vector", st[:, 6:8], st[:, 6:8], st[:, 8:10], ALU.subtract,
                ins=[bst], outs=[bst])
        self.act(st[:, 8:10], st[:, 6:8], AF.Ln, bias=self.eps_t[:, 0:1],
                 ins=[bst, self.buf("eps_t")], outs=[bst])
        self.act(st[:, 10:12], st[:, 8:10], AF.Exp, scale=-0.5, ins=[bst],
                 outs=[bst])

    def ln_stats(self, v, bv, tc, scr, bscrs, st=None, bst=None):
        if st is None:
            st, bst = self.stats, self.buf("stats")
        self.red("vector", st[:, tc:tc + 1], v, ALU.add, ins=[bv],
                 outs=[bst])
        self.act(scr, v, AF.Square, ins=[bv], outs=list(bscrs) + [bst],
                 accum=st[:, 2 + tc:3 + tc])
        self.ts("vector", st[:, 4 + tc:5 + tc], st[:, tc:tc + 1], INV_D,
                None, ALU.mult, ins=[bst], outs=[bst])
        self.ts("vector", st[:, 6 + tc:7 + tc], st[:, 2 + tc:3 + tc], INV_D,
                None, ALU.mult, ins=[bst], outs=[bst])
        self.tt("vector", st[:, 8 + tc:9 + tc], st[:, 4 + tc:5 + tc],
                st[:, 4 + tc:5 + tc], ALU.mult, ins=[bst], outs=[bst])
        self.tt("vector", st[:, 6 + tc:7 + tc], st[:, 6 + tc:7 + tc],
                st[:, 8 + tc:9 + tc], ALU.subtract, ins=[bst], outs=[bst])
        self.act(st[:, 8 + tc:9 + tc], st[:, 6 + tc:7 + tc], AF.Ln,
                 bias=self.eps_t[:, 0:1], ins=[bst, self.buf("eps_t")],
                 outs=[bst])
        self.act(st[:, 10 + tc:11 + tc], st[:, 8 + tc:9 + tc], AF.Exp,
                 scale=-0.5, ins=[bst], outs=[bst])

    def stage_a(self, i):
        fw, buf = self.fw, self.buf
        ii = i % 2
        xt, bxt = self.xt[ii], buf(f"xt{ii}")
        r0 = i * TT
        fw.dma("sync", xt[:],
               self.x_in.ap()[r0:r0 + TT, :].rearrange(
                   "(tc p) d -> p tc d", p=128),
               ins=[bxt], outs=[bxt], sem=f"l_x{ii}")
        st, bst = self.stats, buf("stats")
        self.ln_stats2(xt, bxt, self.sq_a3, [buf("xn2b"), buf("xn2a")],
                       st, bst)
        r4 = (i % 4) * 4
        self.cp("vector", self.statR[:, r4:r4 + 2], st[:, 4:6],
                ins=[bst], outs=[buf("statR")])
        self.cp("vector", self.statR[:, r4 + 2:r4 + 4], st[:, 10:12],
                ins=[bst], outs=[buf("statR")])
        for tc in range(2):
            self.ts("vector", self.xn2b[:, tc, :], xt[:, tc, :],
                    st[:, 4 + tc:5 + tc],
                    st[:, 10 + tc:11 + tc], ALU.subtract, ALU.mult,
                    ins=[bxt, bst], outs=[buf("xn2b")])
        for tc in range(2):
            for dc in range(DC):
                pt = self.ptr[dc % 2]
                bpt = buf(f"ptr{dc % 2}")
                self.tr(pt[:, 0:128],
                        self.xn2b[:, tc, dc * 128:(dc + 1) * 128],
                        ins=[buf("xn2b")], outs=[bpt])
                self.ts("vector", self.xn2T[:, dc, tc * 128:(tc + 1) * 128],
                        pt[:, 0:128], self.vec[:, dc, V_GAM:V_GAM + 1],
                        self.vec[:, dc, V_BET:V_BET + 1], ALU.mult, ALU.add,
                        ins=[bpt, buf("vec")], outs=[buf("xn2a")])
    def stage_a_mm(self, i):
        buf = self.buf
        ii = i % 2
        xb, z = self.xb[ii], self.zt[ii]
        bxb, bz = buf(f"xb{ii}"), buf(f"zt{ii}")
        for q in range(16):
            pbk, bp = self.pb[q % 2], buf(f"pb{q % 2}")
            for k in range(DC):
                self.mm(pbk[:, 0:TT], self.winb[:, k, q * 128:(q + 1) * 128],
                        self.xn2T[:, k, :], k == 0, k == DC - 1,
                        ins=[buf("winb"), buf("xn2a")], outs=[bp])
            dst = xb[:, q, :] if q < 8 else z[:, q - 8, :]
            dk = bxb if q < 8 else bz
            self.act(dst, pbk[:, 0:TT], AF.Copy, ins=[bp], outs=[dk])

    def stage_b_front_a(self, i):
        fw, buf = self.fw, self.buf
        ii = i % 2
        seqpos = i % 8
        xb, z = self.xb[ii], self.zt[ii]
        bxb, bz = buf(f"xb{ii}"), buf(f"zt{ii}")
        vec, bvec = self.vec, buf("vec")
        xc, bxc = self.xc, buf("xc")
        xbf = xb.rearrange("p a b -> p (a b)")
        NW = DC * TT

        def bc(col):
            return vec[:, :, col:col + 1].broadcast_to([128, DC, TT])

        bdA, bdBx = buf("dA"), buf("dBx")
        self.tt("vector", xc[:, :, :], xb[:, :, :], bc(V_W1), ALU.mult,
                ins=[bxb, bvec], outs=[bxc])
        self.tt("vector", xc[:, :, :], xc[:, :, :], bc(V_CB), ALU.add,
                ins=[bvec, bxc], outs=[bxc])
        self.tt("gpsimd", self.dA[:, :, :], xb[:, :, :], bc(V_W0), ALU.mult,
                ins=[bxb, bvec], outs=[bdA])
        self.tt("vector", self.xcf[:, 1:NW], self.xcf[:, 1:NW],
                self.dAf[:, 0:NW - 1], ALU.add, ins=[bdA, bxc], outs=[bxc])
        self.tt("gpsimd", self.dBx[:, :, :], xb[:, :, :], bc(V_W2), ALU.mult,
                ins=[bxb, bvec], outs=[bdBx])
        self.tt("vector", self.xcf[:, 0:NW - 1], self.xcf[:, 0:NW - 1],
                self.dBxf[:, 1:NW], ALU.add, ins=[bdBx, bxc], outs=[bxc])
        bac = buf("accs")
        self.tt("vector", self.accs[:, 16:23],
                xb[:, 0:7, TT - 1:TT].rearrange("p c o -> p (c o)"),
                vec[:, 1:8, V_W0], ALU.mult, ins=[bxb, bvec], outs=[bac])
        self.tt("vector", xc[:, 1:8, 0:1], xc[:, 1:8, 0:1],
                self.accs[:, 16:23].rearrange("p (c o) -> p c o", o=1),
                ALU.subtract, ins=[bac, bxc], outs=[bxc])
        self.tt("vector", self.accs[:, 16:23],
                xb[:, 1:8, 0:1].rearrange("p c o -> p (c o)"),
                vec[:, 0:7, V_W2], ALU.mult, ins=[bxb, bvec], outs=[bac])
        self.tt("vector", xc[:, 0:7, TT - 1:TT], xc[:, 0:7, TT - 1:TT],
                self.accs[:, 16:23].rearrange("p (c o) -> p c o", o=1),
                ALU.subtract, ins=[bac, bxc], outs=[bxc])
        if seqpos > 0:
            self.tt("vector", self.accs[:, 16:24], self.xblast[:],
                    vec[:, :, V_W0], ALU.mult, ins=[buf("xblast"), bvec],
                    outs=[bac])
            self.tt("vector", xc[:, :, 0:1], xc[:, :, 0:1],
                    self.accs[:, 16:24].rearrange("p (c o) -> p c o", o=1),
                    ALU.add, ins=[bac, bxc], outs=[bxc])
        self.cp("vector", self.xblast[:],
                xb[:, :, TT - 1:TT].rearrange("p c o -> p (c o)"),
                ins=[bxb], outs=[buf("xblast")])

    def stage_b_front_a2(self, i):
        fw, buf = self.fw, self.buf
        ii = i % 2
        seqpos = i % 8
        xb, z = self.xb[ii], self.zt[ii]
        bxb, bz = buf(f"xb{ii}"), buf(f"zt{ii}")
        vec, bvec = self.vec, buf("vec")
        xc, bxc = self.xc, buf("xc")
        bac = buf("accs")
        if seqpos < 7:
            nxt = self.xb[(i + 1) % 2]
            bnxt = buf(f"xb{(i + 1) % 2}")
            self.tt("vector", self.accs[:, 16:24],
                    nxt[:, :, 0:1].rearrange("p c o -> p (c o)"),
                    vec[:, :, V_W2], ALU.mult, ins=[bnxt, bvec], outs=[bac])
            self.tt("vector", xc[:, :, TT - 1:TT], xc[:, :, TT - 1:TT],
                    self.accs[:, 16:24].rearrange("p (c o) -> p c o", o=1),
                    ALU.add, ins=[bac, bxc], outs=[bxc])
        self.act(xc[:, :, :], xc[:, :, :], AF.Silu, ins=[bxc], outs=[bxc])
        self.act(z[:, :, :], z[:, :, :], AF.Silu, ins=[bz], outs=[bz])
        self.cp("vector", self.xactb[:, :, :], xc[:, :, :], ins=[bxc],
                outs=[buf("xactb")])

    def stage_b_front_b(self, i):
        fw, buf = self.fw, self.buf
        ii = i % 2
        seqpos = i % 8
        xb, z = self.xb[ii], self.zt[ii]
        bxb, bz = buf(f"xb{ii}"), buf(f"zt{ii}")
        vec, bvec = self.vec, buf("vec")
        xc, bxc = self.xc, buf("xc")
        bxab = buf("xactb")
        p2, bp2 = self.pb[2], buf("pb2a")
        btm = buf("mixsb")
        t1b = self.tsmall[64:128, 0:TT]
        tBb = self.tsmall[0:16, 0:TT]
        tCb = self.tsmall[32:48, 0:TT]
        for k in range(DC):
            self.mm(p2[0:128, 0:TT], self.wgateb[:, k, :],
                    self.xactb[:, k, :], k == 0, k == DC - 1,
                    ins=[buf("wgateb"), bxab], outs=[bp2])
        self.cp("vector", self.tsmall[0:128, 0:TT], p2[0:128, 0:TT],
                ins=[bp2], outs=[btm])
        esp, besp = self.espt, buf("espt")
        for dc in range(DC):
            p3, bp3 = self.pb[3], buf("pb3a")
            self.mm(p3[:, 0:TT], self.wdtb[64:128, dc * 128:(dc + 1) * 128],
                    t1b, True, True, ins=[buf("wdtb"), btm], outs=[bp3])
            self.act(esp[:, dc, :], p3[:, 0:TT], AF.Exp,
                     bias=vec[:, dc, V_BDT:V_BDT + 1], ins=[bp3, bvec],
                     outs=[besp])
        self.act(esp[:, :, :], esp[:, :, :], AF.Ln,
                 bias=self.one_t[:, 0:1], ins=[besp, buf("one_t")],
                 outs=[besp])
        dA, bdA = self.dA, buf("dA")
        self.tt("vector", dA[:, :, :], esp[:, :, :],
                vec[:, :, V_A:V_A + 1].broadcast_to([128, DC, TT]),
                ALU.mult, ins=[besp, bvec], outs=[bdA])
        self.act(dA[:, :, :], dA[:, :, :], AF.Exp, ins=[bdA], outs=[bdA])
        dBx, bdBx = self.dBx, buf("dBx")
        if seqpos == 0:
            self.mset("vector", self.hstate[:], 0.0, outs=[buf("hstate")])
        for dc in range(DC):
            p4, bp4 = self.pb[3], buf("pb3b")
            self.mm(p4[:, TT:2 * TT],
                    self.wbpb[0:16, dc * 128:(dc + 1) * 128],
                    tBb, True, True, ins=[buf("wbpb"), btm], outs=[bp4])
            self.tt("vector", dBx[:, dc, :], p4[:, TT:2 * TT],
                    esp[:, dc, :], ALU.mult, ins=[bp4, besp], outs=[bdBx])
        self.tt("gpsimd", dBx[:, :, :], dBx[:, :, :], xc[:, :, :],
                ALU.mult, ins=[bdBx, bxc], outs=[bdBx])
        bac = buf("accs")
        dA0 = dA[:, :, 0:1].rearrange("p c o -> p (c o)")
        dBx0 = dBx[:, :, 0:1].rearrange("p c o -> p (c o)")
        self.tt("vector", self.accs[:, 24:32], dA0, self.hstate[:],
                ALU.mult, ins=[bdA, buf("hstate")], outs=[bac])
        self.tt("vector", dBx0, dBx0, self.accs[:, 24:32], ALU.add,
                ins=[bac, bdBx], outs=[bdBx])
        self.ts("vector", dA0, dA0, 0.0, None, ALU.mult, ins=[bdA],
                outs=[bdA])
        self.fw.op("vector", lambda v: v.tensor_tensor_scan(
            self.dBxf[:, :], self.dAf[:, :], self.dBxf[:, :],
            self.zcol[:, 0:1], ALU.mult, ALU.add),
            ins=[bdA, bdBx, buf("zcol")], outs=[bdBx])
        self.cp("vector", self.hstate[:],
                dBx[:, :, TT - 1:TT].rearrange("p c o -> p (c o)"),
                ins=[bdBx], outs=[buf("hstate")])
        for dc in range(DC):
            p5, bp5 = self.pb[2], buf("pb2b")
            self.mm(p5[:, TT:2 * TT],
                    self.wcpb[32:48, dc * 128:(dc + 1) * 128],
                    tCb, True, True, ins=[buf("wcpb"), btm], outs=[bp5])
            self.tt("vector", dA[:, dc, :], dBx[:, dc, :], p5[:, TT:2 * TT],
                    ALU.mult, ins=[bdBx, bp5], outs=[bdA])
        self.tt("gpsimd", esp[:, :, :], xc[:, :, :],
                vec[:, :, V_DSK:V_DSK + 1].broadcast_to([128, DC, TT]),
                ALU.mult, ins=[bxc, bvec], outs=[besp])
        self.tt("vector", dA[:, :, :], dA[:, :, :], esp[:, :, :], ALU.add,
                ins=[besp, bdA], outs=[bdA])
        self.tt("vector", self.gated[:, :, :], dA[:, :, :], z[:, :, :],
                ALU.mult, ins=[bdA, bz], outs=[buf("gated")])

    def stage_b_tail(self, i):
        fw, buf = self.fw, self.buf
        ii = i % 2
        xt, bxt = self.xres, buf("xres")
        fw.dma("sync", xt[:],
               self.x_in.ap()[i * TT:i * TT + TT, :].rearrange(
                   "(tc p) d -> p tc d", p=128),
               ins=[bxt], outs=[bxt], sem=f"l_xr{i % 2}")
        mix, bmix = self.mixsb, buf("mixsb")
        r4 = (i % 4) * 4
        sR, bsR = self.statR, buf("statR")
        for tc in range(2):
            self.ts("vector", mix[:, tc, :], xt[:, tc, :],
                    sR[:, r4 + tc:r4 + tc + 1],
                    sR[:, r4 + 2 + tc:r4 + 3 + tc],
                    ALU.subtract, ALU.mult, ins=[bxt, bsR], outs=[bmix])
        for tc in range(2):
            for nq in range(2):
                pbk, bp = self.pb[4 + nq], buf(f"pb{4 + nq}")
                for k in range(DC):
                    self.mm(pbk[:, 0:512],
                            self.gated[:, k, tc * 128:(tc + 1) * 128],
                            self.woutb[:, k, nq * 512:(nq + 1) * 512],
                            k == 0, k == DC - 1,
                            ins=[buf("gated"), buf("woutb")], outs=[bp])
                self.tt("vector", mix[:, tc, nq * 512:(nq + 1) * 512],
                        mix[:, tc, nq * 512:(nq + 1) * 512],
                        pbk[:, 0:512], ALU.add, ins=[bp], outs=[bmix])
        st, bst = self.statB, buf("statB")
        hl = self.esp2
        bhl = buf("espt")
        self.ln_stats2(mix, bmix, self.esp2, [buf("espt")], st, bst)
        for tc in range(2):
            self.ts("vector", mix[:, tc, :], mix[:, tc, :],
                    st[:, 4 + tc:5 + tc], st[:, 10 + tc:11 + tc],
                    ALU.subtract, ALU.mult, ins=[bmix, bst], outs=[bmix])
            self.tt("vector", mix[:, tc, :], mix[:, tc, :], xt[:, tc, :],
                    ALU.add, ins=[bmix, bxt], outs=[bmix])
        r0 = i * TT
        fw.dma("gpsimd",
               self.x2_sp.ap()[r0:r0 + TT, :].rearrange(
                   "(tc p) d -> p tc d", p=128),
               mix[:, :, :], ins=[bmix, buf("x2_sp")], outs=[buf("x2_sp")],
               sem=f"s_x2{i % 2}")
        self.ln_stats2(mix, bmix, self.esp2, [buf("espt")], st, bst)
        for tc in range(2):
            self.ts("vector", hl[:, tc, :], mix[:, tc, :],
                    st[:, 4 + tc:5 + tc], st[:, 10 + tc:11 + tc],
                    ALU.subtract, ALU.mult, ins=[bmix, bst], outs=[bhl])
            self.tt("vector", hl[:, tc, :], hl[:, tc, :], self.lng_bc[:],
                    ALU.mult, ins=[bhl, buf("lng_bc")], outs=[bhl])
            self.tt("vector", hl[:, tc, :], hl[:, tc, :], self.lnb_bc[:],
                    ALU.add, ins=[bhl, buf("lnb_bc")], outs=[bhl])
        fw.dma("gpsimd",
               self.hln_sp.ap()[r0:r0 + TT, :].rearrange(
                   "(tc p) d -> p tc d", p=128),
               hl[:, :, :], ins=[bhl, buf("hln_sp")], outs=[buf("hln_sp")],
               sem=f"s_hln{i % 2}")
        if i <= 7:
            self.ladder_tile(hl[:, :, :].rearrange("p a b -> p (a b)"),
                             bhl, i, need_abs=True, subs=32, nq=1,
                             shift=(i % 8))
        else:
            self.capture_q1_sb(
                hl[:, :, :].rearrange("p a b -> p (a b)"), bhl, i)
            rb, rk = self.xres, buf("xres")
            fw.dma("sync", rb[:],
                   self.hln_sp.ap()[(i - 8) * TT:(i - 7) * TT, :].rearrange(
                       "(tc p) d -> p tc d", p=128),
                   ins=[buf("hln_sp"), rk], outs=[rk], sem=f"l_hr{i % 2}")
            self.capture_q1_sb(
                rb.rearrange("p a b -> p (a b)"), rk, i - 8)

    def capture_q1_sb(self, flat, bflat, idx):
        buf = self.buf
        width = 2048
        bdA, bdBx = buf("dA"), buf("dBx")
        self.ts("vector", self.dBxf.bitcast(I32)[:, 0:width],
                flat.bitcast(I32), ABSM, None, ALU.bitwise_and,
                ins=[bflat], outs=[bdBx])
        data = self.dBxf[:, 0:width]
        self.ts("vector", self.dAf[:, 0:width], data, self.qwin[:, 1:2],
                None, ALU.is_gt, ins=[bdBx, buf("qwin")],
                outs=[bdA, buf("accs")], accum=self.accs[:, 30:31])
        self.tt("vector", self.cnthi[:, idx:idx + 1],
                self.cnthi[:, idx:idx + 1], self.accs[:, 30:31], ALU.add,
                ins=[buf("accs")], outs=[buf("cnthi")])
        self.ts("vector", self.dAf[:, 0:width], data, self.qwin[:, 1:2],
                None, ALU.is_le, ins=[bdBx, buf("qwin")], outs=[bdA])
        self.tt("gpsimd", self.dAf[:, 0:width], self.dAf[:, 0:width], data,
                ALU.mult, ins=[bdBx, bdA], outs=[bdA])
        for s in range(4):
            self.max8(
                self.slots1[:, idx * 32 + s * 8:idx * 32 + (s + 1) * 8],
                self.dAf[:, s * 512:(s + 1) * 512],
                ins=[bdA], outs=[buf("slots1c")])

    def ladder_tile(self, flat, bflat, i, need_abs, subs, nq, width=2048,
                    shift=0):
        buf = self.buf
        eng = "vector"
        if i == 0:
            self.red("vector", self.accs[:, 25:26], flat, ALU.max,
                     absval=True, ins=[bflat], outs=[buf("accs")])
            self.tt("vector", self.accs[:, 24:25], self.accs[:, 24:25],
                    self.accs[:, 25:26], ALU.max, ins=[buf("accs")],
                    outs=[buf("accs")])
            return
        nsub = width // subs
        sub = flat[:, shift:shift + width - subs + 1:subs]
        if need_abs:
            self.ts(eng, self.qsub[:, 0:nsub].bitcast(I32),
                    sub.bitcast(I32), ABSM, None, ALU.bitwise_and,
                    ins=[bflat], outs=[buf("qsub")])
            data, bdata = self.qsub[:, 0:nsub], buf("qsub")

        else:
            data, bdata = sub, bflat
        for j in range(NL):
            self.ts(eng, self.qsub[:, 256:256 + nsub], data,
                    self.thr[:, j:j + 1], None, ALU.is_gt,
                    ins=[bdata, buf("thr")],
                    outs=[buf("qsubhi"), buf("accs")],
                    accum=self.accs[:, j:j + 1])
        self.tt(eng, self.cnt[:, 0:NL], self.cnt[:, 0:NL],
                self.accs[:, 0:NL], ALU.add, ins=[buf("accs")],
                outs=[buf("cnt")])

    def ladder_setup_gen1(self, nq):
        buf = self.buf
        self.part_max_to_scalar(self.accs[:, 24:25], self.lad(8), "accs")
        geo = self.rows[0:1, 0:NL]
        self.ts("vector", self.srow[0:1, 0:NL], geo, self.lad(8), None,
                ALU.mult, ins=[buf("rows_sb"), buf("lads")],
                outs=[buf("srow")])
        self.bc_row(self.srow[0:1, 0:NL], NL, self.thr[:, 0:NL], "thr",
                    "srow")
        self.mset("vector", self.cnt[:], 0.0, outs=[buf("cnt")])
        self.mset("vector", self.accs[:, 0:NL], 0.0, outs=[buf("accs")])

    def interp_est(self, rstar, out11):
        buf = self.buf
        L = self.lrow
        bL = buf("lrow")
        self.col_total(self.cnt[:, 0:NL], NL, self.lads[0:1, 32:32 + NL],
                       "cnt")
        ct = self.lads[0:1, 32:32 + NL]
        self.dbg(self.srow[0:1, 0:NL], NL)
        self.dbg(ct, NL)
        bl = buf("lads")
        self.ts("vector", L[0:1, 0:NL], ct, float(rstar), None, ALU.is_ge,
                ins=[bl], outs=[bL])
        self.cp("vector", L[0:1, 32:33], L[0:1, 0:1], ins=[bL], outs=[bL])
        self.ts("vector", L[0:1, 64:64 + NL - 1], L[0:1, 0:NL - 1], -1.0,
                1.0, ALU.mult, ALU.add, ins=[bL], outs=[bL])
        self.tt("vector", L[0:1, 33:33 + NL - 1], L[0:1, 1:NL],
                L[0:1, 64:64 + NL - 1], ALU.mult, ins=[bL], outs=[bL])
        sel = L[0:1, 32:32 + NL]
        self.tt("vector", L[0:1, 64:64 + NL], sel, self.srow[0:1, 0:NL],
                ALU.mult, ins=[bL, buf("srow")], outs=[bL])
        self.red("vector", L[0:1, 96:97], L[0:1, 64:64 + NL], ALU.add,
                 ins=[bL], outs=[bL])
        self.tt("vector", L[0:1, 64:64 + NL], sel, ct, ALU.mult,
                ins=[bL, bl], outs=[bL])
        self.red("vector", L[0:1, 97:98], L[0:1, 64:64 + NL], ALU.add,
                 ins=[bL], outs=[bL])
        self.tt("vector", L[0:1, 64:64 + NL - 1], L[0:1, 33:33 + NL - 1],
                self.srow[0:1, 0:NL - 1], ALU.mult, ins=[bL, buf("srow")],
                outs=[bL])
        self.red("vector", L[0:1, 98:99], L[0:1, 64:64 + NL - 1], ALU.add,
                 ins=[bL], outs=[bL])
        self.tt("vector", L[0:1, 64:64 + NL - 1], L[0:1, 33:33 + NL - 1],
                self.lads[0:1, 32:32 + NL - 1], ALU.mult, ins=[bL, bl],
                outs=[bL])
        self.red("vector", L[0:1, 99:100], L[0:1, 64:64 + NL - 1], ALU.add,
                 ins=[bL], outs=[bL])
        self.ts("vector", L[0:1, 100:101], L[0:1, 97:98], float(rstar),
                None, ALU.subtract, ins=[bL], outs=[bL])
        self.tt("vector", L[0:1, 101:102], L[0:1, 97:98], L[0:1, 99:100],
                ALU.subtract, ins=[bL], outs=[bL])
        self.ts("vector", L[0:1, 101:102], L[0:1, 101:102], 1e-6, None,
                ALU.add, ins=[bL], outs=[bL])
        self.rcp(L[0:1, 101:102], L[0:1, 101:102], ins=[bL], outs=[bL])
        self.tt("vector", L[0:1, 100:101], L[0:1, 100:101], L[0:1, 101:102],
                ALU.mult, ins=[bL], outs=[bL])
        self.tt("vector", L[0:1, 98:99], L[0:1, 98:99], L[0:1, 96:97],
                ALU.subtract, ins=[bL], outs=[bL])
        self.tt("vector", L[0:1, 98:99], L[0:1, 98:99], L[0:1, 100:101],
                ALU.mult, ins=[bL], outs=[bL])
        self.tt("vector", L[0:1, 96:97], L[0:1, 96:97], L[0:1, 98:99],
                ALU.add, ins=[bL], outs=[bL])
        self.red("vector", L[0:1, 102:103], L[0:1, 0:NL], ALU.max,
                 ins=[bL], outs=[bL])
        self.tt("vector", L[0:1, 96:97], L[0:1, 96:97], L[0:1, 102:103],
                ALU.mult, ins=[bL], outs=[bL])
        self.ts("vector", L[0:1, 103:104], L[0:1, 102:103], -1.0, 1.0,
                ALU.mult, ALU.add, ins=[bL], outs=[bL])
        self.tt("vector", L[0:1, 103:104], L[0:1, 103:104],
                self.srow[0:1, NL - 1:NL], ALU.mult, ins=[bL, buf("srow")],
                outs=[bL])
        self.tt("vector", out11, L[0:1, 96:97], L[0:1, 103:104], ALU.add,
                ins=[bL], outs=[buf("lads")])

    def ladder_setup_gen2(self, est11, nq):
        buf = self.buf
        lin = self.rows[0:1, NL:2 * NL]
        self.ts("vector", self.srow[0:1, 0:NL], lin, est11, est11,
                ALU.mult, ALU.add, ins=[buf("rows_sb"), buf("lads")],
                outs=[buf("srow")])
        self.bc_row(self.srow[0:1, 0:NL], NL, self.thr[:, 0:NL], "thr",
                    "srow")
        self.mset("vector", self.cnt[:], 0.0, outs=[buf("cnt")])
        self.mset("vector", self.accs[:, 0:NL], 0.0, outs=[buf("accs")])

    def capture_tile(self, flat, bflat, width, spans, cnthi_col, slots,
                     bslots, slot_off, scrt, bscr):
        buf = self.buf
        data, bdata = flat, bflat
        self.ts("vector", scrt[:, 0:width], data, self.qwin[:, 1:2],
                None, ALU.is_gt, ins=[bdata, buf("qwin")],
                outs=[bscr, buf("accs")],
                accum=self.accs[:, 30:31])
        self.tt("vector", self.cnthi[:, cnthi_col:cnthi_col + 1],
                self.cnthi[:, cnthi_col:cnthi_col + 1],
                self.accs[:, 30:31], ALU.add, ins=[buf("accs")],
                outs=[buf("cnthi")])
        self.ts("vector", scrt[:, 0:width], data, self.qwin[:, 1:2],
                None, ALU.is_le, ins=[bdata, buf("qwin")], outs=[bscr])
        self.tt("gpsimd", scrt[:, 0:width], scrt[:, 0:width],
                data, ALU.mult, ins=[bdata, bscr], outs=[bscr])
        for s in range(width // spans):
            self.max8(slots[:, slot_off + s * 8:slot_off + (s + 1) * 8],
                      scrt[:, s * spans:(s + 1) * spans],
                      ins=[bscr], outs=[bslots])

    def rank_consts(self, n_const, jl_out, jh_out, hw_out):
        buf = self.buf
        bl = buf("lads")
        L = self.lrow
        bL = buf("lrow")
        self.mset("vector", L[0:1, 110:111], float(n_const), outs=[bL])
        self.mset("vector", L[0:1, 111:112], 0.995, outs=[bL])
        self.ts("vector", L[0:1, 112:113], L[0:1, 110:111], 1.0, None,
                ALU.subtract, ins=[bL], outs=[bL])
        self.tt("vector", L[0:1, 113:114], L[0:1, 111:112], L[0:1, 112:113],
                ALU.mult, ins=[bL], outs=[bL])
        self.cp("vector", self.i32s[0:1, 0:1], L[0:1, 113:114], ins=[bL],
                outs=[self.buf("i32s")])
        self.cp("vector", L[0:1, 115:116], self.i32s[0:1, 0:1],
                ins=[self.buf("i32s")], outs=[bL])
        self.ts("vector", L[0:1, 114:115], L[0:1, 113:114], 0.0, None,
                ALU.max, ins=[bL], outs=[bL])
        self.tt("vector", L[0:1, 114:115], L[0:1, 113:114], L[0:1, 115:116],
                ALU.subtract, ins=[bL], outs=[bL])
        self.ts("vector", L[0:1, 114:115], L[0:1, 114:115], 0.0, 0.9999,
                ALU.max, ALU.min, ins=[bL], outs=[bL])
        self.tt("vector", L[0:1, 115:116], L[0:1, 113:114], L[0:1, 114:115],
                ALU.subtract, ins=[bL], outs=[bL])
        self.tt("vector", L[0:1, 116:117], L[0:1, 110:111], L[0:1, 115:116],
                ALU.subtract, ins=[bL], outs=[bL])
        self.ts("vector", jl_out, L[0:1, 116:117], 1.0, None, ALU.subtract,
                ins=[bL], outs=[bl])
        self.ts("vector", L[0:1, 117:118], L[0:1, 114:115], 0.0, None,
                ALU.is_gt, ins=[bL], outs=[bL])
        self.tt("vector", jh_out, jl_out, L[0:1, 117:118], ALU.subtract,
                ins=[bL, bl], outs=[bl])
        self.cp("vector", hw_out, L[0:1, 114:115], ins=[bL], outs=[bl])

    def gap_pick(self, glob_row, thr_row, tgt11, lo_out, hi_out, chi_out,
                 npt):
        buf = self.buf
        L, bL = self.lrow, buf("lrow")
        bl = buf("lads")
        self.ts("vector", L[0:1, 220:221], tgt11, 1.0, None, ALU.add,
                ins=[bl], outs=[bL])
        self.fw.op("vector", lambda v: v.tensor_scalar(
            L[0:1, 128:128 + npt], glob_row, L[0:1, 220:221], None,
            ALU.is_ge), ins=[bL, buf("srow")], outs=[bL])
        self._gp_n2 = getattr(self, "_gp_n2", -1) + 1
        if False and self.debug and self._gp_n2 == 9:
            self.dbg_mark(1)
            self.dbg(L[0:1, 128:128 + npt], npt)
            self.dbg(glob_row, npt)
            self.dbg(L[0:1, 220:221])
        self.ts("vector", L[0:1, 150:150 + npt - 1],
                L[0:1, 129:128 + npt], -1.0, 1.0, ALU.mult, ALU.add,
                ins=[bL], outs=[bL])
        self.tt("vector", L[0:1, 172:172 + npt - 1],
                L[0:1, 128:128 + npt - 1], L[0:1, 150:150 + npt - 1],
                ALU.mult, ins=[bL], outs=[bL])
        self.cp("vector", L[0:1, 172 + npt - 1:172 + npt],
                L[0:1, 128 + npt - 1:128 + npt], ins=[bL], outs=[bL])
        sel = L[0:1, 172:172 + npt]
        self.tt("vector", L[0:1, 195:195 + npt], sel, thr_row, ALU.mult,
                ins=[bL, buf("srow")], outs=[bL])
        self.red("vector", lo_out, L[0:1, 195:195 + npt], ALU.add,
                 ins=[bL], outs=[bl])
        self.tt("vector", L[0:1, 195:195 + npt - 1],
                L[0:1, 172:172 + npt - 1], thr_row[0:1, 1:npt], ALU.mult,
                ins=[bL, buf("srow")], outs=[bL])
        self.red("vector", hi_out, L[0:1, 195:195 + npt - 1], ALU.add,
                 ins=[bL], outs=[bl])
        self.tt("vector", L[0:1, 195:195 + npt - 1],
                L[0:1, 172:172 + npt - 1], glob_row[0:1, 1:npt], ALU.mult,
                ins=[bL, buf("srow")], outs=[bL])
        self.red("vector", chi_out, L[0:1, 195:195 + npt - 1], ALU.add,
                 ins=[bL], outs=[bl])
        self._gp_n = getattr(self, "_gp_n", -1) + 1
        if False and self.debug and self._gp_n in (8, 9) and self.dbg_idx < 230:
            self.dbg_mark(2 + self._gp_n)
            self.dbg(L[0:1, 220:221])
            self.dbg(L[0:1, 128:128 + npt], npt)
            self.dbg(glob_row, npt)
            self.dbg(thr_row, npt)
            self.dbg(sel, npt)

    def phase_barrier_all(self):
        toks = self.fw.barrier_tokens()
        self._barrier_toks = toks
        for b in self._bufs.values():
            b.readers = list(b.readers) + toks

    def window_from_est(self, est11, win):
        buf = self.buf
        self.ts("vector", self.lad(10), est11, float(1.0 - win), None,
                ALU.mult, ins=[buf("lads")], outs=[buf("lads")])
        self.ts("vector", self.lad(11), est11, float(1.0 + win), None,
                ALU.mult, ins=[buf("lads")], outs=[buf("lads")])
        self.bc_pp(self.lad(10), self.qwin[:, 0:1], "qwin")
        self.bc_pp(self.lad(11), self.qwin[:, 1:2], "qwin")

    def w1_prep_chunk(self, c):
        fw, buf = self.fw, self.buf
        par = c % 2
        st, bst = self.w1s[par], buf(f"w1s{par}")
        tb, btb = self.w1tb[par], buf(f"w1t{par}")
        xb_, bxb_ = self.w1x[par], buf(f"w1x{par}")
        fw.dma("sync", st[:, :], self.w1_d.ap()[c * 128:(c + 1) * 128, :],
               ins=[bst], outs=[bst], sem=f"l_w1p{par}")
        self.ts("vector", st[:, :], st[:, :], self.scp[:, 0:1], MAGIC,
                ALU.mult, ALU.add, ins=[bst, buf("scp")], outs=[bst])
        self.ts("vector", st[:, :], st[:, :], MAGIC, -1.0, ALU.subtract,
                ALU.max, ins=[bst], outs=[bst])
        self.ts("vector", tb[:, :], st[:, :], 1.0, None, ALU.min,
                ins=[bst], outs=[btb])
        for dc in range(DC):
            pt = self.ptr[dc % 2]
            bpt = buf(f"ptr{dc % 2}")
            self.tr(pt[:, 0:128], tb[:, dc * 128:(dc + 1) * 128],
                    ins=[btb], outs=[bpt])
            self.cp("vector", xb_[:, dc, :], pt[:, 0:128], ins=[bpt],
                    outs=[bxb_])
        fw.dma("gpsimd",
               self.w1q_sp.ap()[:, :, c * 128:(c + 1) * 128].rearrange(
                   "dc p h -> p dc h"),
               xb_[:], ins=[bxb_, buf("w1q_sp")], outs=[buf("w1q_sp")],
               sem=f"s_w1q{par}")

    def w1_finish(self):
        for dc in range(DC):
            self.fw.dma("sync", self.w1qT[:, dc, :],
                        self.w1q_sp.ap()[dc, :, :],
                        ins=[self.buf("w1q_sp"), self.buf("w1qT")],
                        outs=[self.buf("w1qT"), self.buf("slots1c")],
                        sem="l_w1q")

    def w2_prep_chunk(self, c):
        fw, buf = self.fw, self.buf
        for qt in range(4):
            par = (c * 4 + qt) % 2
            st, bst = self.w2s[par], buf(f"w2s{par}")
            tb, btb = self.w2tb[par], buf(f"w2tb{par}")
            xb_, bxb_ = self.w2x[par], buf(f"w2x{par}")
            fw.dma("sync", st[:, :],
                   self.w2_d.ap()[c * 128:(c + 1) * 128,
                                  qt * 1024:(qt + 1) * 1024],
                   ins=[bst], outs=[bst], sem=f"l_w2p{par}")
            self.ts("vector", st[:, :], st[:, :], self.scp[:, 1:2], MAGIC,
                    ALU.mult, ALU.add, ins=[bst, buf("scp")], outs=[bst])
            self.ts("vector", st[:, :], st[:, :], MAGIC, -1.0,
                    ALU.subtract, ALU.max, ins=[bst], outs=[bst])
            self.ts("vector", tb[:, :], st[:, :], 1.0, None, ALU.min,
                    ins=[bst], outs=[btb])
            for blk in range(8):
                pt = self.ptr[blk % 2]
                bpt = buf(f"ptr{blk % 2}")
                self.tr(pt[:, 0:128], tb[:, blk * 128:(blk + 1) * 128],
                        ins=[btb], outs=[bpt])
                self.cp("vector", xb_[:, blk, :], pt[:, 0:128], ins=[bpt],
                        outs=[bxb_])
            fw.dma("gpsimd",
                   self.w2q_sp.ap()[qt * 8:(qt + 1) * 8, :,
                                    c * 128:(c + 1) * 128].rearrange(
                       "hc p d -> p hc d"),
                   xb_[:], ins=[bxb_, buf("w2q_sp")], outs=[buf("w2q_sp")],
                   sem=f"s_w2q{par}")

    def w2_finish(self):
        for hc in range(HC):
            self.fw.dma("scalar", self.w2qT[:, hc, :],
                        self.w2q_sp.ap()[hc, :, :],
                        ins=[self.buf("w2q_sp")],
                        outs=[self.buf("w1qT"), self.buf("w2qT")],
                        sem="l_w2q")

    def q1_window_phase(self):
        for c in range(HC):
            self.w1_prep_chunk(c)

    def capture_q1_tile(self, flat, bflat, i):
        buf = self.buf
        width = 2048
        self.ts("vector", self.selscr[:, 0:width].bitcast(I32)[:, 0:width],
                flat.bitcast(I32), ABSM, None, ALU.bitwise_and,
                ins=[bflat], outs=[buf("selscr")])
        data, bdata = self.selscr[:, 0:width], buf("selscr")
        scr = self.q1scr
        bscr = buf("q1scr")
        self.ts("vector", scr, data, self.qwin[:, 1:2], None, ALU.is_gt,
                ins=[bdata, buf("qwin")], outs=[bscr, buf("accs")],
                accum=self.accs[:, 30:31])
        self.tt("vector", self.cnthi[:, i:i + 1], self.cnthi[:, i:i + 1],
                self.accs[:, 30:31], ALU.add, ins=[buf("accs")],
                outs=[buf("cnthi")])
        self.ts("vector", scr, data, self.qwin[:, 1:2], None, ALU.is_le,
                ins=[bdata, buf("qwin")], outs=[bscr])
        self.tt("vector", scr, scr, data, ALU.mult, ins=[bdata, bscr],
                outs=[bscr])
        for s in range(8):
            self.max8(self.slots1[:, i * 64 + s * 8:i * 64 + (s + 1) * 8],
                      scr[:, s * 256:(s + 1) * 256],
                      ins=[bscr], outs=[buf("slots1c")])

    def select_quantile(self, qid, slots, bslots, cnthi_lo, cnthi_n, nconst,
                        gamma_lad, s_col, gs_col):
        fw, buf = self.fw, self.buf
        L, bL = self.lrow, buf("lrow")
        bl = buf("lads")
        SR, bsr = self.srow, buf("srow")
        self.red("vector", self.accs[:, 26:27],
                 self.cnthi[:, cnthi_lo:cnthi_lo + cnthi_n], ALU.add,
                 ins=[buf("cnthi")], outs=[buf("accs")])
        self.col_total(self.accs[:, 26:27], 1, self.lad(16), "accs")
        self.rank_consts(nconst, self.lad(13), self.lad(14), self.lad(15))
        self.cp("vector", L[0:1, 0:1], self.lad(16), ins=[bl], outs=[bL])
        self.cp("vector", L[0:1, 1:2], self.lad(10), ins=[bl], outs=[bL])
        self.cp("vector", L[0:1, 2:3], self.lad(11), ins=[bl], outs=[bL])
        fw.dma("gpsimd", self.cc1_src.ap(), L[0:1, 0:64],
               ins=[bL, buf("cc1")], outs=[buf("cc1")], sem=f"c1s{qid}")
        self.coll("AllGather", ALU.bypass, self.cc1_src.ap(),
                  self.cc1_dst.ap(), ins=[buf("cc1")], outs=[buf("cc1d")])
        fw.dma("sync", L[0:1, 0:512],
               self.cc1_dst.ap().rearrange("r k -> (r k)"),
               ins=[buf("cc1d"), bL], outs=[bL], sem=f"c1l{qid}")
        self.red("vector", self.lad(18), L[0:1, 1:512:64], ALU.max,
                 ins=[bL], outs=[bl])
        self.red("vector", self.lad(19), L[0:1, 2:512:64], ALU.min,
                 ins=[bL], outs=[bl])
        self.red("vector", self.lad(12), L[0:1, 0:512:64], ALU.add,
                 ins=[bL], outs=[bl])
        self.tt("vector", self.lad(17), self.lad(19), self.lad(18),
                ALU.subtract, ins=[bl], outs=[bl])
        nslots = slots.shape[-1]
        for rnd in range(2):
            if rnd == 0:
                span, base = self.lad(17), self.lad(18)
            else:
                self.tt("vector", self.lad(36), self.lad(34), self.lad(33),
                        ALU.subtract, ins=[bl], outs=[bl])
                span, base = self.lad(36), self.lad(33)
            self.ts("vector", SR[0:1, 32:32 + NLE],
                    self.rows[0:1, 32:32 + NLE], span, base,
                    ALU.mult, ALU.add, ins=[buf("rows_sb"), bl], outs=[bsr])
            self.bc_row(SR[0:1, 32:32 + NLE], NLE, self.thr[:, NLE:2 * NLE],
                        "thr", "srow")
            for j in range(NLE):
                self.ts("vector", self.selscr[:, 0:nslots], slots,
                        self.thr[:, NLE + j:NLE + j + 1], None, ALU.is_gt,
                        ins=[bslots, buf("thr")], outs=[buf("selscr"),
                                                        buf("accs")],
                        accum=self.accs[:, j:j + 1])
            self.col_total(self.accs[:, 0:NLE], NLE, L[0:1, 300:300 + NLE],
                           "accs", out_key="lrow")
            self.ts("vector", L[0:1, 0:NLE], L[0:1, 300:300 + NLE],
                    self.lad(16), None, ALU.add, ins=[bL, bl], outs=[bL])
            fw.dma("gpsimd", self.cc2_src.ap(), L[0:1, 0:32],
                   ins=[bL, buf("cc2")], outs=[buf("cc2")],
                   sem=f"c2s{qid}{rnd}")
            self.coll("AllGather", ALU.bypass, self.cc2_src.ap(),
                      self.cc2_dst.ap(), ins=[buf("cc2")],
                      outs=[buf("cc2d")])
            fw.dma("sync", L[0:1, 0:256],
                   self.cc2_dst.ap().rearrange("r k -> (r k)"),
                   ins=[buf("cc2d"), bL], outs=[bL], sem=f"c2l{qid}{rnd}")
            self.cp("vector", SR[0:1, 96:96 + NLE], L[0:1, 0:NLE], ins=[bL],
                    outs=[bsr])
            for r in range(1, 8):
                self.tt("vector", SR[0:1, 96:96 + NLE],
                        SR[0:1, 96:96 + NLE],
                        L[0:1, r * 32:r * 32 + NLE], ALU.add, ins=[bL, bsr],
                        outs=[bsr])
            self.gap_pick(SR[0:1, 96:96 + NLE], SR[0:1, 32:32 + NLE],
                          self.lad(13), self.lad(33), self.lad(34),
                          self.lad(35), NLE)
        if self.debug and qid == 2:
            self.dbg_mark(15)
            self.dbg(self.lads[0:1, 20:27], 7)
            self.dbg(self.lads[0:1, 33:36], 3)
            self.dbg(self.srow[0:1, 96:96 + NLE], NLE)
            self.dbg(self.srow[0:1, 32:32 + NLE], NLE)
        self.bc_pp(self.lad(33), self.qwin[:, 2:3], "qwin")
        self.bc_pp(self.lad(34), self.qwin[:, 3:4], "qwin")
        self.stt("vector", self.selscr[:, 0:nslots], slots, self.qwin[:, 2:3],
                 slots, ALU.is_gt, ALU.mult, ins=[bslots, buf("qwin")],
                 outs=[buf("selscr")])
        self.stt("vector", self.selscr[:, 0:nslots], self.selscr[:, 0:nslots],
                 self.qwin[:, 3:4], self.selscr[:, 0:nslots], ALU.is_le,
                 ALU.mult, ins=[buf("qwin")], outs=[buf("selscr")])
        self.max8(self.candx[:, 0:8], self.selscr[:, 0:nslots],
                  ins=[buf("selscr")], outs=[buf("candx")])
        self.mrep(self.selscr[:, 0:nslots], self.candx[:, 0:8],
                  self.selscr[:, 0:nslots], ins=[buf("candx")],
                  outs=[buf("selscr")])
        self.max8(self.candx[:, 8:16], self.selscr[:, 0:nslots],
                  ins=[buf("selscr")], outs=[buf("candx")])
        fw.dma("gpsimd",
               self.cc3_src.ap().rearrange("a (p s) -> (a p) s", p=128),
               self.candx[:],
               ins=[buf("candx"), buf("cc3")], outs=[buf("cc3")],
               sem=f"c3s{qid}")
        self.coll("AllGather", ALU.bypass, self.cc3_src.ap(),
                  self.cc3_dst.ap(), ins=[buf("cc3")], outs=[buf("cc3d")])
        fw.dma("sync", self.gath[:].rearrange("p (r s) -> p r s", r=8),
               self.cc3_dst.ap().rearrange("r (p s) -> p r s", p=128),
               ins=[buf("cc3d"), buf("gath")], outs=[buf("gath")],
               sem=f"c3l{qid}")
        for ti, (tgt, vout) in enumerate([(self.lad(13), self.lad(28))]):
            lo, hi, ca = self.lad(33), self.lad(34), self.lad(35)
            for rnd in range(2):
                self.tt("vector", self.lad(36), hi, lo, ALU.subtract,
                        ins=[bl], outs=[bl])
                self.ts("vector", SR[0:1, 32:32 + NLE],
                        self.rows[0:1, 32:32 + NLE], self.lad(36), lo,
                        ALU.mult, ALU.add, ins=[buf("rows_sb"), bl],
                        outs=[bsr])
                self.bc_row(SR[0:1, 32:32 + NLE], NLE,
                            self.thr[:, NLE:2 * NLE], "thr", "srow")
                for j in range(NLE):
                    self.ts("vector", self.qsub[:, 128:256], self.gath[:],
                            self.thr[:, NLE + j:NLE + j + 1], None,
                            ALU.is_gt, ins=[buf("gath"), buf("thr")],
                            outs=[buf("qsub"), buf("accs")],
                            accum=self.accs[:, j:j + 1])
                self.col_total(self.accs[:, 0:NLE], NLE,
                               L[0:1, 300:300 + NLE], "accs",
                               out_key="lrow")
                self.ts("vector", SR[0:1, 96:96 + NLE],
                        L[0:1, 300:300 + NLE], self.lad(35), None, ALU.add,
                        ins=[bL, bl], outs=[bsr])
                lo2 = self.lad(44 + 3 * rnd + 8 * ti)
                hi2 = self.lad(45 + 3 * rnd + 8 * ti)
                ca2 = self.lad(46 + 3 * rnd + 8 * ti)
                self.gap_pick(SR[0:1, 96:96 + NLE], SR[0:1, 32:32 + NLE],
                              tgt, lo2, hi2, ca2, NLE)
                lo, hi, ca = lo2, hi2, ca2
            self.bc_pp(lo, self.qwin[:, 2:3], "qwin")
            self.bc_pp(hi, self.qwin[:, 3:4], "qwin")
            self.stt("vector", self.qsub[:, 0:128], self.gath[:],
                     self.qwin[:, 2:3], self.gath[:], ALU.is_gt, ALU.mult,
                     ins=[buf("gath"), buf("qwin")], outs=[buf("qsub")])
            self.stt("vector", self.qsub[:, 0:128], self.qsub[:, 0:128],
                     self.qwin[:, 3:4], self.qsub[:, 0:128], ALU.is_le,
                     ALU.mult, ins=[buf("qwin")], outs=[buf("qsub")])
            self.max8(self.candx[:, 0:8], self.qsub[:, 0:128],
                      ins=[buf("qsub")], outs=[buf("candx")])
            fw.dma("gpsimd",
                   self.row_sp.ap().rearrange("a (p s) -> (a p) s", p=128),
                   self.candx[:, 0:8],
                   ins=[buf("candx"), buf("row_sp")], outs=[buf("row_sp")],
                   sem=f"rs{qid}{ti}")
            fw.dma("sync", self.row1k[0:1, :], self.row_sp.ap(),
                   ins=[buf("row_sp"), buf("row1k")], outs=[buf("row1k")],
                   sem=f"rl{qid}{ti}")
            self.max8(SR[0:1, 224:232], self.row1k[0:1, :],
                      ins=[buf("row1k")], outs=[bsr])
            self.mrep(self.row1k[0:1, :], SR[0:1, 224:232],
                      self.row1k[0:1, :], ins=[bsr], outs=[buf("row1k")])
            self.max8(SR[0:1, 232:240], self.row1k[0:1, :],
                      ins=[buf("row1k")], outs=[bsr])
            self.tt("vector", self.lad(36), tgt, ca, ALU.subtract,
                    ins=[bl], outs=[bl])
            self.ts("vector", L[0:1, 52:53], self.lad(36), 15.0, None,
                    ALU.min, ins=[bl], outs=[bL])
            self.ts("vector", L[0:1, 16:32], self.iota16[:], L[0:1, 52:53],
                    None, ALU.is_equal, ins=[buf("iota16"), bL], outs=[bL])
            self.tt("vector", L[0:1, 16:32], L[0:1, 16:32],
                    SR[0:1, 224:240], ALU.mult, ins=[bsr, bL], outs=[bL])
            self.red("vector", vout, L[0:1, 16:32], ALU.add, ins=[bL],
                     outs=[bl])
            ok = self.lad(40 + ti)
            self.ts("vector", L[0:1, 40:41], tgt, 1.0, None, ALU.add,
                    ins=[bl], outs=[bL])
            self.fw.op("vector", lambda v: v.tensor_scalar(
                L[0:1, 41:42], SR[0:1, 96:97], L[0:1, 40:41], None,
                ALU.is_ge), ins=[bsr, bL], outs=[bL])
            self.ts("vector", L[0:1, 42:43], self.lad(36), 0.0, None,
                    ALU.is_ge, ins=[bl], outs=[bL])
            self.ts("vector", L[0:1, 43:44], self.lad(36), 15.5, None,
                    ALU.is_le, ins=[bl], outs=[bL])
            self.fw.op("vector", lambda v: v.tensor_tensor(
                L[0:1, 53:54], SR[0:1, 224:225], SR[0:1, 239:240],
                ALU.is_equal), ins=[bsr], outs=[bL])
            self.tt("vector", L[0:1, 43:44], L[0:1, 43:44], L[0:1, 53:54],
                    ALU.max, ins=[bL], outs=[bL])
            self.tt("vector", L[0:1, 41:42], L[0:1, 41:42], L[0:1, 42:43],
                    ALU.mult, ins=[bL], outs=[bL])
            self.tt("vector", ok, L[0:1, 41:42], L[0:1, 43:44], ALU.mult,
                    ins=[bL], outs=[bl])
        self.cp("vector", self.lad(29), self.lad(28), ins=[bl], outs=[bl])
        self.cp("vector", self.lad(41), self.lad(40), ins=[bl], outs=[bl])
        self.ts("vector", L[0:1, 44:45], self.lad(15), -1.0, 1.0, ALU.mult,
                ALU.add, ins=[bl], outs=[bL])
        self.tt("vector", L[0:1, 45:46], self.lad(28), L[0:1, 44:45],
                ALU.mult, ins=[bl, bL], outs=[bL])
        self.tt("vector", L[0:1, 46:47], self.lad(29), self.lad(15),
                ALU.mult, ins=[bl], outs=[bL])
        self.tt("vector", L[0:1, 45:46], L[0:1, 45:46], L[0:1, 46:47],
                ALU.add, ins=[bL], outs=[bL])
        self.tt("vector", self.lad(42), self.lad(40), self.lad(41),
                ALU.mult, ins=[bl], outs=[bl])
        self.tt("vector", L[0:1, 47:48], self.lad(33), self.lad(34),
                ALU.add, ins=[bl], outs=[bL])
        self.ts("vector", L[0:1, 47:48], L[0:1, 47:48], 0.5, None, ALU.mult,
                ins=[bL], outs=[bL])
        self.tt("vector", L[0:1, 45:46], L[0:1, 45:46], self.lad(42),
                ALU.mult, ins=[bl, bL], outs=[bL])
        self.ts("vector", L[0:1, 48:49], self.lad(42), -1.0, 1.0, ALU.mult,
                ALU.add, ins=[bl], outs=[bL])
        self.tt("vector", L[0:1, 47:48], L[0:1, 47:48], L[0:1, 48:49],
                ALU.mult, ins=[bL], outs=[bL])
        self.tt("vector", L[0:1, 45:46], L[0:1, 45:46], L[0:1, 47:48],
                ALU.add, ins=[bL], outs=[bL])
        self.ts("vector", self.lad(43), L[0:1, 45:46], EPS, None, ALU.max,
                ins=[bL], outs=[bl])
        for s11 in (self.lad(9), self.lad(10), self.lad(11), self.lad(18),
                    self.lad(19), self.lad(12), self.lad(13), self.lad(14),
                    self.lad(28), self.lad(29), self.lad(40), self.lad(41),
                    self.lad(42), self.lad(43)):
            self.dbg(s11)
        self.rcp(L[0:1, 51:52], self.lad(43), ins=[bl], outs=[bL])
        self.ts("vector", self.lad(60), L[0:1, 51:52], 127.0, None,
                ALU.mult, ins=[bL], outs=[bl])
        self.tt("vector", L[0:1, 50:51], self.lad(gamma_lad), self.lad(43),
                ALU.mult, ins=[bl], outs=[bL])
        self.ts("vector", self.lad(61), L[0:1, 50:51],
                float(1.0 / 127.0), None, ALU.mult, ins=[bL], outs=[bl])
        self.bc_pp(self.lad(60), self.s_pp[:, s_col:s_col + 1], "s_pp")
        self.bc_pp(self.lad(61), self.s_pp[:, gs_col:gs_col + 1], "s_pp")

    def f1_phase(self):
        fw, buf = self.fw, self.buf
        self.mset("vector", self.accs[:, 24:26], 0.0, outs=[buf("accs")])
        self.q2_calls = 0
        for i in range(NT):
            par = i % 2
            pi = (i // 2) % 2
            hb, hk = self.hlnF[par], buf(f"hlnF{par}")
            kt, bkt = self.k1tok[par], buf(f"k1tok{par}")
            kT, bkT = self.k1Tw[pi], buf(f"k1Tw{pi}")
            r0 = i * TT
            fw.dma("sync", hb[:],
                   self.hln_sp.ap()[r0:r0 + TT, :].rearrange(
                       "(tc p) d -> p tc d", p=128),
                   ins=[buf("hln_sp"), hk], outs=[hk], sem=f"f1h{par}")
            for tc in range(2):
                v = hb[:, tc, :]
                self.ts("vector", v, v, self.s_pp[:, 0:1], MAGIC, ALU.mult,
                        ALU.add, ins=[hk, buf("s_pp")], outs=[hk])
                self.ts("vector", v, v, MAGIC, -128.0, ALU.subtract,
                        ALU.max, ins=[hk], outs=[hk])
                self.ts("vector", kt[:, tc, :], v, 127.0, None,
                        ALU.min, ins=[hk], outs=[bkt])
            for tc in range(2):
                for dc in range(DC):
                    pt = self.ptr[dc % 2]
                    bpt = buf(f"ptr{dc % 2}")
                    self.tr(pt[:, 0:128],
                            kt[:, tc, dc * 128:(dc + 1) * 128],
                            ins=[bkt], outs=[bpt])
                    self.cp("vector",
                            kT[:, dc,
                               par * TT + tc * 128:par * TT + (tc + 1) * 128],
                            pt[:, 0:128], ins=[bpt], outs=[bkT])
            if par == 1:
                p = i // 2
                r0p = (i - 1) * TT
                gb, gk = self.gsb[pi], buf(f"gsb{pi}")
                for gr in range(8):
                    if p >= 4:
                        rb, rk = self.q2rd[gr % 2], buf(f"q2rd{gr % 2}")
                        fw.dma("scalar", rb[:],
                               self.g_sp.ap()[gr * 4:(gr + 1) * 4, :,
                                              (p - 4) * 512:
                                              (p - 4) * 512 + 512].rearrange(
                                   "hc p t -> p hc t"),
                               ins=[buf("g_sp"), rk], outs=[rk],
                               sem=f"qgr{gr % 2}")
                    for h4 in range(4):
                        hc = gr * 4 + h4
                        pbk, bp = self.pb[h4], buf(f"pb{h4}")
                        for dc in range(DC):
                            self.mm(pbk[:, 0:512],
                                    self.w1qT[:, dc,
                                              hc * 128:(hc + 1) * 128],
                                    kT[:, dc, :], dc == 0, dc == DC - 1,
                                    ins=[buf("w1qT"), bkT], outs=[bp])
                        self.act(gb[:, h4, :], pbk[:, 0:512], AF.Gelu,
                                 scale=self.s_pp[:, 1:2],
                                 ins=[bp, buf("s_pp")], outs=[gk])
                    fw.dma("gpsimd",
                           self.g_sp.ap()[gr * 4:(gr + 1) * 4, :,
                                          r0p:r0p + 512].rearrange(
                               "hc p t -> p hc t"),
                           gb[:], ins=[gk, buf("g_sp")],
                           outs=[buf("g_sp")], sem=f"sg{pi}")
                    if p == 0:
                        self.ladder_tile(gb.rearrange("p a b -> p (a b)"),
                                         gk, 0, need_abs=False, subs=32,
                                         nq=2, width=2048, shift=0)
                    elif p <= 2 or (p == 3 and gr % 2 == p % 2):
                        self.ladder_tile(gb.rearrange("p a b -> p (a b)"),
                                         gk, 1, need_abs=False, subs=32,
                                         nq=2, width=2048,
                                         shift=(p * 4 + gr) % 32)
                        self.q2_calls += 1
                    if p >= 4:
                        j = p - 4
                        rb, rk = self.q2rd[gr % 2], buf(f"q2rd{gr % 2}")
                        self.capture_tile(
                            gb.rearrange("p a b -> p (a b)"), gk,
                            2048, 1024, NT + p, self.slots2,
                            buf("slots2c"), (p * 8 + gr) * 16,
                            self.selscr, buf("selscr"))
                        self.capture_tile(
                            rb.rearrange("p a b -> p (a b)"), rk,
                            2048, 1024, NT + j, self.slots2,
                            buf("slots2c"), (j * 8 + gr) * 16,
                            self.selscr, buf("selscr"))
                if p == 0:
                    self.ladder_setup_gen1(2)
                if p == 2:
                    self.interp_est(0.005 * self.q2_calls * 64 * 128,
                                    self.lad(9))
                    self.ladder_setup_gen2(self.lad(9), 2)
                    self.q2_calls = 0
                if p == 3:
                    self.interp_est(0.005 * self.q2_calls * 64 * 128,
                                    self.lad(9))
                    self.window_from_est(self.lad(9), Q2_WIN)
                self.w2_prep_chunk(p)
        self.w2_finish()

    def f2_phase(self):
        fw, buf = self.fw, self.buf
        out_toks = []
        st, bst = self.stats, buf("stats")
        for i in range(NT):
            par = i % 2
            r0 = i * TT
            x2b, bx2 = self.x2F[par], buf(f"x2F{par}")
            ob, bob = self.outsb[par], buf(f"outs{par}")
            fw.dma("sync", x2b[:],
                   self.x2_sp.ap()[r0:r0 + TT, :].rearrange(
                       "(tc p) d -> p tc d", p=128),
                   ins=[buf("x2_sp"), bx2], outs=[bx2], sem=f"f2x{par}")
            for gr in range(8):
                gb = self.gF[gr % 2]
                gk = buf(f"gF{gr % 2}")
                kb = self.k2[gr % 2]
                kk = buf(f"k2_{gr % 2}")
                fw.dma("sync" if gr % 2 == 0 else "scalar", gb[:],
                       self.g_sp.ap()[gr * 4:(gr + 1) * 4, :,
                                      r0:r0 + TT].rearrange(
                           "hc p t -> p hc t"),
                       ins=[buf("g_sp"), gk], outs=[gk], sem=f"f2g{gr % 2}")
                flat = gb.rearrange("p a b -> p (a b)")
                self.ts("vector", flat, flat, self.s_pp[:, 2:3], MAGIC,
                        ALU.mult, ALU.add, ins=[gk, buf("s_pp")], outs=[gk])
                self.ts("vector", flat, flat, MAGIC, -128.0, ALU.subtract,
                        ALU.max, ins=[gk], outs=[gk])
                self.ts("vector", kb.rearrange("p a b -> p (a b)"), flat,
                        127.0, None, ALU.min, ins=[gk], outs=[kk])
                for h4 in range(4):
                    hc = gr * 4 + h4
                    for tc in range(2):
                        for nq in range(2):
                            pbk = self.pb[tc * 2 + nq]
                            bp = buf(f"pb{tc * 2 + nq}")
                            self.mm(pbk[:, 0:512],
                                    kb[:, h4, tc * 128:(tc + 1) * 128],
                                    self.w2qT[:, hc,
                                              nq * 512:(nq + 1) * 512],
                                    hc == 0, hc == HC - 1,
                                    ins=[kk, buf("w2qT")], outs=[bp])
            for tc in range(2):
                for nq in range(2):
                    pbk = self.pb[tc * 2 + nq]
                    bp = buf(f"pb{tc * 2 + nq}")
                    self.ts("vector",
                            ob[:, tc, nq * 512:(nq + 1) * 512],
                            pbk[:, 0:512], self.s_pp[:, 3:4], None,
                            ALU.mult, ins=[bp, buf("s_pp")],
                            outs=[bob])
            self.ln_stats2(ob, bob, self.esp2, [buf("espt")], st, bst)
            for tc in range(2):
                v = ob[:, tc, :]
                self.ts("vector", v, v, st[:, 4 + tc:5 + tc],
                        st[:, 10 + tc:11 + tc], ALU.subtract, ALU.mult,
                        ins=[bst], outs=[bob])
                self.tt("vector", v, v, x2b[:, tc, :], ALU.add,
                        ins=[bx2], outs=[bob])
            t = fw.dma("gpsimd",
                       self.y_out.ap()[r0:r0 + TT, :].rearrange(
                           "(tc p) d -> p tc d", p=128),
                       ob[:], ins=[bob, buf("y")],
                       outs=[buf("y")], sem=f"out{par}")
            out_toks.append(t)
        if self.debug:
            t = fw.dma("gpsimd", self.dbg_d.ap()[0:1, 0:320],
                       self.dbgrow[0:1, :],
                       ins=[buf("dbgrow"), buf("dbg")], outs=[buf("dbg")],
                       sem="outd")
            out_toks.append(t)
        fw.wait_all("sync", out_toks)

    def emit(self):
        self.fw.emit()


def build_program(debug=False):
    nc = bass.Bass("TRN2", target_bir_lowering=False)
    k = K(nc, debug)
    k.declare()
    k.init_phase()

    def post_b(bi):
        if bi == 0:
            k.ladder_setup_gen1(1)
        if bi == 3:
            k.interp_est(0.005 * 3 * 64 * 128, k.lad(9))
            k.ladder_setup_gen2(k.lad(9), 1)
        if bi == 7:
            k.interp_est(0.005 * 4 * 64 * 128, k.lad(9))
            k.window_from_est(k.lad(9), Q1_WIN)

    for i in range(NT):
        k.stage_a(i)
        if i % 8 != 0:
            k.stage_b_front_a(i - 1)
        if i >= 2 and (i - 2) % 8 < 6:
            k.stage_b_tail(i - 2)
            post_b(i - 2)
        k.stage_a_mm(i)
        if i % 8 != 0:
            k.stage_b_front_a2(i - 1)
            k.stage_b_front_b(i - 1)
        if i % 8 == 7:
            k.stage_b_tail(i - 1)
            post_b(i - 1)
            k.stage_b_front_a(i)
            k.stage_b_front_a2(i)
            k.stage_b_front_b(i)
            k.stage_b_tail(i)
            post_b(i)
    k.phase_barrier_all()
    k.q1_window_phase()
    k.select_quantile(1, k.slots1[:, :], k.buf("slots1c"), 0, NT, N_Q1,
                      0, 0, 1)
    k.w1_finish()
    k.phase_barrier_all()
    k.f1_phase()
    k.phase_barrier_all()
    k.select_quantile(2, k.slots2[:, :], k.buf("slots2c"), NT, NT, N_Q2,
                      2, 2, 3)
    k.phase_barrier_all()
    k.f2_phase()
    k.emit()
    return nc


def make_core_inputs(inputs, core):
    x = np.asarray(inputs["L0_attn_x"])
    xs = np.ascontiguousarray(x[2 * core:2 * core + 2].reshape(NTOK, D))
    vl = [np.asarray(inputs["L0_attn_conv_w"])[0, 0],
          np.asarray(inputs["L0_attn_conv_w"])[1, 0],
          np.asarray(inputs["L0_attn_conv_w"])[2, 0],
          np.asarray(inputs["L0_attn_conv_b"]),
          np.asarray(inputs["L0_attn_A"]),
          np.asarray(inputs["L0_attn_D_skip"]),
          np.asarray(inputs["L0_attn_b_dt"]),
          np.asarray(inputs["L0_attn_gamma"]),
          np.asarray(inputs["L0_attn_beta"]),
          np.zeros(D, np.float32), np.zeros(D, np.float32),
          np.zeros(D, np.float32)]
    vecs = np.stack([np.asarray(v, np.float32).reshape(DC, 128).T
                     for v in vl], axis=2).reshape(128, DC * 12)
    vecs = np.ascontiguousarray(vecs)
    lngb = np.ascontiguousarray(np.concatenate([
        np.broadcast_to(np.asarray(inputs["L0_ffn_ln_g"], np.float32),
                        (128, D)),
        np.broadcast_to(np.asarray(inputs["L0_ffn_ln_b"], np.float32),
                        (128, D))], axis=1))
    ident = np.eye(128, dtype=ml_dtypes.bfloat16)
    rows = np.zeros((1, 64), np.float32)
    rows[0, 0:NL] = (GEOLO ** (np.arange(NL) / (NL - 1))).astype(np.float32)
    rows[0, NL:2 * NL] = ((NL - 1) / 2.0 - np.arange(NL)) / ((NL - 1) / 2.0) \
        * LIN_W
    rows[0, 32:32 + NLE] = np.arange(NLE) / float(NLE - 1)
    return {
        "x": xs,
        "W_in": np.asarray(inputs["L0_attn_W_in"], np.float32),
        "W_out": np.asarray(inputs["L0_attn_W_out"], np.float32),
        "W_xp": np.asarray(inputs["L0_attn_W_xp"], np.float32),
        "W_dt": np.asarray(inputs["L0_attn_W_dt"], np.float32),
        "W_Bg": np.asarray(inputs["L0_attn_W_Bg"], np.float32),
        "W_Cg": np.asarray(inputs["L0_attn_W_Cg"], np.float32),
        "W_Bp": np.asarray(inputs["L0_attn_W_Bp"], np.float32),
        "W_Cp": np.asarray(inputs["L0_attn_W_Cp"], np.float32),
        "w1": np.asarray(inputs["L0_ffn_w1"], np.float32),
        "w2": np.asarray(inputs["L0_ffn_w2"], np.float32),
        "vecs": vecs, "lngb": lngb, "ident": ident, "rows": rows,
    }


def kernel(**inputs):
    nc = build_program(DEBUG)
    in_maps = [make_core_inputs(inputs, c) for c in range(8)]
    res = run_bass_kernel_spmd(nc, in_maps, core_ids=list(range(8)))
    outs = [np.asarray(r["y"]) for r in res.results]
    return np.concatenate(outs, 0).reshape(16, 2048, 1024)
